# version 10
# baseline (speedup 1.0000x reference)
import math
from contextlib import ExitStack

import numpy as np
import ml_dtypes

import concourse.bass as bass
import concourse.mybir as mybir
from concourse.bass_utils import run_bass_kernel_spmd

F32 = mybir.dt.float32
BF16 = mybir.dt.bfloat16
AF = mybir.ActivationFunctionType
ALU = mybir.AluOpType
AX = mybir.AxisListType
BFNP = ml_dtypes.bfloat16

S = 4096
D = 1024
NT = S // 128
NG = S // 512
ALPHA = 1.0 / math.sqrt(16.0 + 512.0 + 512.0)
HID = 2730
HIDP = 2816
NHC = HIDP // 128
F1P = 2 * HIDP
QSC = 8.0
NEG = -1e30

ENGS = ("tensor", "vector", "scalar", "gpsimd", "sync")
AUTO_DRAIN = False
DRAIN_SECTIONS = ()
WO2_SYNC_CAST = False

from contextlib import contextmanager
PROG = {"tensor": "pe", "vector": "ve", "scalar": "se", "gpsimd": "ge"}
ARENA_COLS = 90624


def _host_consts():
    c = {}
    c["ident_bf"] = np.eye(128, dtype=BFNP)
    c["ident_f32"] = np.eye(128, dtype=np.float32)
    pos = np.arange(S, dtype=np.float64)[:, None]
    ids = np.arange(D // 2, dtype=np.float64)
    theta = 10000.0 ** (-2.0 * ids) / (D // 2)
    r = pos * theta[None, :]
    c["mla_cos"] = np.cos(r).astype(np.float32)
    c["mla_sin"] = np.sin(r).astype(np.float32)
    half = np.arange(0, 64, 2, dtype=np.float64)
    inv_freq = 1.0 / 10000.0 ** (half / 64.0)
    t = np.arange(S, dtype=np.float64)
    fr = t[:, None] * inv_freq[None, :]
    fr = np.concatenate([fr, fr], -1)
    base = (half + 0.4 * 64.0) / (1.4 * 64.0)
    power = (t - S // 2) / 256.0
    sc = base[None, :] ** power[:, None]
    sc = np.concatenate([sc, sc], -1)
    cf, sf = np.cos(fr), np.sin(fr)
    tile8 = lambda a: np.tile(a.astype(np.float32), (1, 8))
    c["xp_cq"] = tile8(cf * sc)
    c["xp_sq"] = tile8(sf * sc)
    c["xp_ck"] = tile8(cf / sc)
    c["xp_sk"] = tile8(sf / sc)
    p = np.arange(128)[:, None]
    cc = np.arange(640)[None, :]
    keep = (p <= cc) & (cc <= p + 512)
    c["mask640"] = np.where(keep, 0.0, NEG).astype(BFNP)
    return c


_CONSTS = None


def _consts():
    global _CONSTS
    if _CONSTS is None:
        _CONSTS = _host_consts()
    return _CONSTS


def _prep_shared(inp):
    d = {}
    d["wdq"] = inp["L0_attn_Wd_q_w"]
    d["wuq"] = inp["L0_attn_Wu_q_w"]
    d["wqr"] = inp["L0_attn_Wq_r_w"]
    d["wkr"] = inp["L0_attn_Wk_r_w"]
    d["wdkv"] = inp["L0_attn_Wd_kv_w"]
    d["wuk"] = inp["L0_attn_Wu_k_w"]
    d["wuv"] = inp["L0_attn_Wu_v_w"]
    d["wo"] = inp["L0_attn_Wo_w"]
    chunkcol = lambda b: np.ascontiguousarray(np.asarray(b).reshape(-1, 128).T)
    bmap = lambda b: np.ascontiguousarray(np.broadcast_to(np.asarray(b), (128, b.shape[-1])))
    d["bdq_c"] = chunkcol(inp["L0_attn_Wd_q_b"])
    d["bdkv_c"] = chunkcol(inp["L0_attn_Wd_kv_b"])
    d["buq_m"] = bmap(inp["L0_attn_Wu_q_b"])
    d["bqr_m"] = bmap(inp["L0_attn_Wq_r_b"])
    d["bkr_m"] = bmap(inp["L0_attn_Wk_r_b"])
    d["buk_m"] = bmap(inp["L0_attn_Wu_k_b"])
    d["buv_m"] = bmap(inp["L0_attn_Wu_v_b"])
    d["wob_m"] = bmap(inp["L0_attn_Wo_b"])
    d["wqkv2"] = inp["L0_ffn_wqkv"]
    d["wo2"] = inp["L0_ffn_wo"]
    w1 = np.asarray(inp["L0_ffn_w1"])
    w1p = np.zeros((2, D, F1P), np.float32)
    w1p[:, :, :HID] = w1[:, :, :HID]
    w1p[:, :, HIDP:HIDP + HID] = w1[:, :, HID:]
    d["w1p"] = w1p
    w2 = np.asarray(inp["L0_ffn_w2"])
    w2p = np.zeros((2, HIDP, D), np.float32)
    w2p[:, :HID, :] = w2
    d["w2p"] = w2p
    b1 = np.asarray(inp["L0_ffn_b1"])
    b1p = np.zeros((2, F1P), np.float32)
    b1p[:, :HID] = b1[:, :HID]
    b1p[:, HIDP:HIDP + HID] = b1[:, HID:]
    d["b1c"] = np.ascontiguousarray(
        np.stack([b1p[l].reshape(-1, 128).T for l in range(2)], 0))
    d["b2m"] = np.ascontiguousarray(
        np.stack([np.broadcast_to(np.asarray(inp["L0_ffn_b2"][l]), (128, D)) for l in range(2)], 0))
    d["lng_c"] = np.ascontiguousarray(
        np.stack([np.asarray(inp["L0_ffn_ln_g"][l]).reshape(8, 128).T for l in range(2)], 0))
    d["lnb_c"] = np.ascontiguousarray(
        np.stack([np.asarray(inp["L0_ffn_ln_b"][l]).reshape(8, 128).T for l in range(2)], 0))
    d["qsm"] = np.ascontiguousarray(
        np.stack([np.broadcast_to(np.tile(np.asarray(inp["L0_ffn_q_scale"][l]), 8), (128, 512)) for l in range(2)], 0))
    d["ksm"] = np.ascontiguousarray(
        np.stack([np.broadcast_to(np.tile(np.asarray(inp["L0_ffn_k_scale"][l]), 8), (128, 512)) for l in range(2)], 0))
    for k, v in _consts().items():
        d[k] = v
    return d


class MK:
    def __init__(self, dbg=()):
        self.dbg = set(dbg)
        self.nc = bass.Bass("TRN2", target_bir_lowering=False)
        self.es = ExitStack()
        self.q = {e: [] for e in ENGS}
        self.semh, self.semc = {}, {}
        for s in ("pe", "ve", "se", "ge", "bar"):
            self.semh[s] = self.es.enter_context(self.nc.semaphore(s))
            self.semc[s] = 0
        self.lastw = {}
        self.dirty = set()
        self.unsig = {e: 0 for e in ENGS}
        self.drain_scope = 0
        self.drain_sections = set(DRAIN_SECTIONS)

    def _app(self, eng, fn):
        self.q[eng].append(fn)

    def op(self, eng, fn, sig=False):
        ev = None
        if sig:
            s = PROG[eng]
            self.semc[s] += 1
            tgt = self.semc[s]
            h = self.semh[s]
            self._app(eng, lambda e, fn=fn, h=h: fn(e).then_inc(h, 1))
            ev = (s, tgt)
            self.unsig[eng] = 0
        else:
            self._app(eng, fn)
            self.unsig[eng] += 1
        if (AUTO_DRAIN or self.drain_scope) and eng in ("vector", "scalar", "gpsimd"):
            self._app(eng, lambda e: e.drain(fusable=True))
        return ev

    def dacc(self, out, in_, slot):
        if slot not in self.semh:
            self.semh[slot] = self.es.enter_context(self.nc.semaphore(slot))
            self.semc[slot] = 0
        self.semc[slot] += 16
        tgt = self.semc[slot]
        h = self.semh[slot]
        self._app("gpsimd", lambda e, out=out, in_=in_, h=h: e.dma_start(
            out=out, in_=in_, accum_op=ALU.add).then_inc(h, 16))
        self.dirty.add(slot)
        return (slot, tgt)

    def dma(self, eng, out, in_, slot):
        if slot not in self.semh:
            self.semh[slot] = self.es.enter_context(self.nc.semaphore(slot))
            self.semc[slot] = 0
        self.semc[slot] += 16
        tgt = self.semc[slot]
        h = self.semh[slot]
        self._app(eng, lambda e, out=out, in_=in_, h=h: e.dma_start(out=out, in_=in_).then_inc(h, 16))
        self.dirty.add(slot)
        return (slot, tgt)

    def wait(self, eng, *evs):
        for ev in evs:
            if ev is None:
                continue
            s, tgt = ev
            if PROG.get(eng) == s:
                continue
            key = (eng, s)
            if self.lastw.get(key, 0) >= tgt:
                continue
            self.lastw[key] = tgt
            h = self.semh[s]
            self._app(eng, lambda e, h=h, tgt=tgt: e.wait_ge(h, tgt))

    PROD = {"pe": "tensor", "ve": "vector", "se": "scalar", "ge": "gpsimd"}

    def fence(self, eng, on):
        prod = self.PROD.get(on)
        if prod is not None and self.unsig[prod] > 0:
            self.milestone(prod)
        self.wait(eng, (on, self.semc[on]))

    def _ds(self, name):
        self.drain_scope = 1 if (name is not None and name in self.drain_sections) else 0

    def milestone(self, eng):
        if eng == "vector":
            return self.op(eng, lambda e: e.tensor_copy(
                self.stat[:, 90:91], self.stat[:, 91:92]), sig=True)
        if eng == "scalar":
            return self.op(eng, lambda e: e.copy(
                self.stat[0:1, 92:93], self.stat[0:1, 93:94]), sig=True)
        s = PROG[eng]
        self.semc[s] += 1
        tgt = self.semc[s]
        h = self.semh[s]
        self._app(eng, lambda e, h=h: e.drain(fusable=True).then_inc(h, 1))
        self.unsig[eng] = 0
        return (s, tgt)

    def barrier(self):
        for s in sorted(self.dirty):
            self.wait("sync", (s, self.semc[s]))
        self.dirty.clear()
        for e in ENGS:
            h = self.semh["bar"]
            self.semc["bar"] += 1
            if e in ("gpsimd", "sync"):
                self._app(e, lambda en, h=h: en.nop().then_inc(h, 1))
            else:
                self._app(e, lambda en, h=h: en.drain(fusable=True).then_inc(h, 1))
        tot = self.semc["bar"]
        for e in ENGS:
            self._app(e, lambda en, h=self.semh["bar"], tot=tot: en.wait_ge(h, tot))
        for e in ENGS:
            for s in self.semc:
                if s != "bar":
                    self.lastw[(e, s)] = self.semc[s]
            self.lastw[(e, "bar")] = tot
        self._tbevs = [None, None]
        self._tbk = 0

    def tb_get(self):
        k = self._tbk
        self._tbk ^= 1
        return self.TBs[k], k

    def mm(self, out, lh, rh, start=True, stop=True, tp=None, sig=False):
        return self.op("tensor", lambda e, out=out, lh=lh, rh=rh, start=start, stop=stop, tp=tp:
                       e.matmul(out, lh, rh, start=start, stop=stop, tile_position=tp,
                                skip_group_check=True), sig=sig)

    def tr(self, out, in_, ident, sig=False):
        return self.op("tensor", lambda e, out=out, in_=in_, ident=ident:
                       e.transpose(out, in_, ident), sig=sig)

    def _rt(self, eng, ev):
        self.swait(eng, ev)
        return ev

    def act(self, out, in_, f, bias=0.0, scale=1.0, accum=None, sig=False, rt=False):
        ev = self.op("scalar", lambda e, out=out, in_=in_, f=f, bias=bias, scale=scale, accum=accum:
                     e.activation(out, in_, f, bias=bias, scale=scale, accum_out=accum),
                     sig=sig or rt)
        return self._rt("scalar", ev) if rt else ev

    def tt(self, eng, out, a, b, alu, sig=False, rt=False):
        ev = self.op(eng, lambda e, out=out, a=a, b=b, alu=alu:
                     e.tensor_tensor(out, a, b, alu), sig=sig or rt)
        return self._rt(eng, ev) if rt else ev

    def ts(self, eng, out, a, s1, s2=None, op0=ALU.mult, op1=ALU.bypass, sig=False, rt=False):
        ev = self.op(eng, lambda e, out=out, a=a, s1=s1, s2=s2, op0=op0, op1=op1:
                     e.tensor_scalar(out, a, s1, s2, op0, op1), sig=sig or rt)
        return self._rt(eng, ev) if rt else ev

    def cp(self, eng, out, in_, sig=False, rt=False):
        ev = self.op(eng, lambda e, out=out, in_=in_: e.tensor_copy(out, in_), sig=sig or rt)
        return self._rt(eng, ev) if rt else ev

    def swait(self, eng, ev):
        s, tgt = ev
        h = self.semh[s]
        self._app(eng, lambda e, h=h, tgt=tgt: e.wait_ge(h, tgt))
        self.lastw[(eng, s)] = max(self.lastw.get((eng, s), 0), tgt)

    def red(self, eng, out, in_, alu, sig=False):
        ev = self.op(eng, lambda e, out=out, in_=in_, alu=alu:
                     e.tensor_reduce(out, in_, AX.X, alu), sig=True)
        self.swait(eng, ev)
        return ev

    def recip(self, out, in_, sig=False):
        ev = self.op("vector", lambda e, out=out, in_=in_: e.reciprocal(out, in_),
                     sig=True)
        self.swait("vector", ev)
        return ev

    def memset(self, eng, ap, v, sig=False, rt=False):
        ev = self.op(eng, lambda e, ap=ap, v=v: e.memset(ap, v), sig=sig or rt)
        return self._rt(eng, ev) if rt else ev

    def build(self, phases=("p0", "p1", "p2", "f0", "f1")):
        nc = self.nc
        es = self.es
        dbg = self.dbg

        def din(name, shape, dt=F32):
            return nc.dram_tensor(name, list(shape), dt, kind="ExternalInput").ap()

        def dscratch(name, shape, dt=F32):
            kind = "ExternalOutput" if name in dbg else "Internal"
            return nc.dram_tensor(name, list(shape), dt, kind=kind).ap()

        self.Xd = din("x", [S, D])
        wdq = din("wdq", [D, 512]); wuq = din("wuq", [512, D]); wqr = din("wqr", [512, D])
        wkr = din("wkr", [D, D]); wdkv = din("wdkv", [D, 512]); wuk = din("wuk", [512, D])
        wuv = din("wuv", [512, D]); wo = din("wo", [D, D])
        bdqc_d = din("bdq_c", [128, 4]); bdkvc_d = din("bdkv_c", [128, 4])
        buqm_d = din("buq_m", [128, D]); bqrm_d = din("bqr_m", [128, D])
        bkrm_d = din("bkr_m", [128, D]); bukm_d = din("buk_m", [128, D])
        buvm_d = din("buv_m", [128, D]); self.wobm_d = din("wob_m", [128, D])
        wqkv2 = din("wqkv2", [2, D, 1536]); wo2 = din("wo2", [2, 512, D])
        w1p = din("w1p", [2, D, F1P]); w2p = din("w2p", [2, HIDP, D])
        b1c_d = din("b1c", [2, 128, F1P // 128]); b2m_d = din("b2m", [2, 128, D])
        lng_d = din("lng_c", [2, 128, 8]); lnb_d = din("lnb_c", [2, 128, 8])
        qsm_d = din("qsm", [2, 128, 512]); ksm_d = din("ksm", [2, 128, 512])
        identbf_d = din("ident_bf", [128, 128], BF16)
        identf_d = din("ident_f32", [128, 128])
        mlacos_d = din("mla_cos", [S, 512]); mlasin_d = din("mla_sin", [S, 512])
        xpcq_d = din("xp_cq", [S, 512]); xpsq_d = din("xp_sq", [S, 512])
        xpck_d = din("xp_ck", [S, 512]); xpsk_d = din("xp_sk", [S, 512])
        mask_d = din("mask640", [128, 640], BF16)

        self.XN = dscratch("XN", [S, D])
        self.XRES = dscratch("XRES", [S, D])
        self.AOUT = dscratch("AOUT", [S, D])
        self.KH = dscratch("KH", [128, 8, S], BF16)
        self.QH = dscratch("QH", [128, 8, S], BF16)
        self.VH = dscratch("VH", [128, NT, 1040], BF16)
        self.OUT = nc.dram_tensor("out", [S, D], F32, kind="ExternalOutput").ap()

        sb = lambda n, sh, dt=F32: es.enter_context(nc.sbuf_tensor(n, sh, dt))
        ps = lambda n, sh, dt=F32: es.enter_context(nc.psum_tensor(n, sh, dt))

        self.identb = sb("identb", [128, 128], BF16)
        self.identf = sb("identf", [128, 128])
        self.ones_f = sb("ones_f", [128, 128])
        self.ones_b = sb("ones_b", [128, 16], BF16)
        self.mask = sb("maskc", [128, 640], BF16)
        self.bdqc = sb("bdqc", [128, 4]); self.bdkvc = sb("bdkvc", [128, 4])
        self.b2m = sb("b2m_s", [128, D])
        self.b1c = sb("b1c_s", [128, F1P // 128])
        self.lng = sb("lng", [128, 8]); self.lnb = sb("lnb", [128, 8])
        self.qsm = sb("qsm_s", [128, 512]); self.ksm = sb("ksm_s", [128, 512])
        self.stat = sb("stat", [128, 96])
        self.epsb = sb("epsb", [128, 1])
        self.scr64 = sb("scr64", [128, 512])
        self.tmph1 = sb("tmph1", [128, 1024], BF16)
        self.xsb = [sb("xsb0", [128, D]), sb("xsb1", [128, D])]
        self.xnbb = [sb("xnbb0", [128, D], BF16), sb("xnbb1", [128, D], BF16)]
        self.AR = sb("AR", [128, ARENA_COLS], BF16)

        self.sp = [ps("sp0", [128, 1024]), ps("sp1", [128, 1024])]
        self.o0 = ps("o0", [128, 512]); self.o1 = ps("o1", [128, 512])
        self.u0 = ps("u0", [128, 512])
        self.TB = ps("TBp", [128, 1024], BF16)
        self.TBs = [self.TB[:], self.u0[:].bitcast(BF16)]
        self._tbevs = [None, None]
        self._tbk = 0

        ar = self.AR

        def abf(off, n):
            return ar[:, off:off + n]

        def af32(off, n):
            return ar[:, off:off + 2 * n].bitcast(F32)

        self.abf, self.af32 = abf, af32
        self.tsq = af32(34880, 1024)
        self.buqm = abf(36928, 1024)
        self.bqrm = abf(37952, 1024)
        self.bkrm = abf(38976, 1024)
        self.bukm = abf(40000, 1024)
        self.buvm = abf(41024, 1024)

        ev_c = []
        ev_c.append(self.dma("sync", self.identb[:], identbf_d[:, :], "c0"))
        ev_c.append(self.dma("sync", self.identf[:], identf_d[:, :], "c1"))
        ev_c.append(self.dma("sync", self.mask[:], mask_d[:, :], "c2"))
        ev_c.append(self.dma("sync", self.bdqc[:], bdqc_d[:, :], "c3"))
        ev_c.append(self.dma("sync", self.bdkvc[:], bdkvc_d[:, :], "c4"))
        for ci, (t, s_) in enumerate(((self.buqm, buqm_d), (self.bqrm, bqrm_d),
                                      (self.bkrm, bkrm_d), (self.bukm, bukm_d),
                                      (self.buvm, buvm_d))):
            ev_c.append(self.dma("gpsimd", t, s_[:, :], f"c{5 + ci}"))
        self.memset("gpsimd", self.ones_f[:], 1.0)
        self.memset("gpsimd", self.ones_b[:], 1.0)
        self.memset("gpsimd", self.epsb[:], 1e-5)
        self.memset("gpsimd", self.stat[:, 8:24], 0.0)
        self.memset("gpsimd", self.stat[:, 90:94], 0.0, rt=True)
        self.barrier()

        W_OFF = 52000
        self.wdkv_sb = abf(W_OFF, 4096)
        self.wuk_sb = abf(W_OFF + 4096, 4096)
        self.wuv_sb = abf(W_OFF + 8192, 4096)
        self.wkr_sb = abf(W_OFF + 12288, 8192)
        self.wdq_sb = abf(W_OFF, 4096)
        self.wuq_sb = abf(W_OFF + 4096, 4096)
        self.wqr_sb = abf(W_OFF + 8192, 4096)

        if "p0" in phases or "p1" in phases:
            self._p0_p1(wdq, wuq, wqr, wkr, wdkv, wuk, wuv,
                        mlacos_d, mlasin_d)
        if "p2" in phases:
            self._p2(wo)
        if "f0" in phases:
            self._ffn(0, wqkv2, wo2, w1p, w2p, b1c_d, b2m_d, lng_d, lnb_d,
                      qsm_d, ksm_d, xpcq_d, xpsq_d, xpck_d, xpsk_d)
        if "f1" in phases:
            self._ffn(1, wqkv2, wo2, w1p, w2p, b1c_d, b2m_d, lng_d, lnb_d,
                      qsm_d, ksm_d, xpcq_d, xpsq_d, xpck_d, xpsk_d)
        if "p5" in phases:
            self._p5()
        self.barrier()

        block = es.enter_context(nc.Block())
        for name, dec in (("tensor", block.tensor), ("vector", block.vector),
                          ("scalar", block.scalar), ("gpsimd", block.gpsimd),
                          ("sync", block.sync)):
            def body(e, name=name):
                for fn in self.q[name]:
                    fn(e)
            dec(body)
        es.close()
        return nc

    def _tm2fm(self, src_f32, par, dst3, ttl):
        ev_bb = self.cp("vector", self.xnbb[par][:], src_f32, sig=True)
        TBk, kk = self.tb_get()
        self.wait("tensor", ev_bb, self._tbevs[kk])
        ev_tr = None
        for c in range(8):
            ev_tr = self.tr(TBk[:, 128 * c:128 * c + 128],
                            self.xnbb[par][:, 128 * c:128 * c + 128],
                            self.identb[:], sig=(c == 7))
        self.wait("vector", ev_tr)
        src3 = TBk.rearrange("p (c t) -> p c t", t=128)
        ev = self.cp("vector", dst3[:, :, 128 * ttl:128 * ttl + 128], src3, sig=True)
        self._tbevs[kk] = ev
        return ev

    def _p0_p1(self, wdq, wuq, wqr, wkr, wdkv, wuk, wuv, mlacos_d, mlasin_d):
        abf, af32 = self.abf, self.af32
        st = self.stat
        xng = [abf(0, 4096), abf(4096, 4096)]
        ctkv = abf(8192, 2048)
        Kstage = abf(10240, 4096)
        Vstage = [abf(14336, 1056), abf(15392, 1056)]
        Ktm = af32(16448, 1024)
        Rtm = af32(18496, 1024)
        Ktb2 = [abf(20544, 1024), abf(42048, 1024)]
        ksq = af32(21568, 1024)
        t1 = af32(23616, 512); t2 = af32(24640, 512); t3 = af32(25664, 512)
        ctab = [af32(26688, 512), af32(27712, 512)]
        stab = [af32(28736, 512), af32(29760, 512)]
        xnb = [af32(30784, 1024), af32(32832, 1024)]

        ev_w = []
        for wi, (wsb, wd) in enumerate(((self.wdkv_sb, wdkv), (self.wuk_sb, wuk),
                                        (self.wuv_sb, wuv), (self.wkr_sb, wkr))):
            wsrc = wd.rearrange("(c p) f -> p c f", p=128)
            ev_w.append(self.dma("gpsimd", wsb, wsrc, f"w{wi}"))

        ev_vst = [None, None]
        ev_xnst = [None, None]
        gate_xs = [None, None]
        gate_tab = [None, None]
        ev_kdma = None
        pb = [None] * 4
        pbo = [None, None]
        self._kstev = None
        pend_k = [None]
        evkd = [None]

        def flush_k():
            if pend_k[0] is None:
                return
            ttl_p, kp, evb = pend_k[0]
            pend_k[0] = None
            TBk, kk = self.tb_get()
            self.wait("tensor", evb, self._tbevs[kk])
            ev_tr = None
            for c in range(8):
                ev_tr = self.tr(TBk[:, 128 * c:128 * c + 128],
                                Ktb2[kp][:, 128 * c:128 * c + 128],
                                self.identb[:], sig=(c == 7))
            self.wait("vector", ev_tr, evkd[0])
            kd3 = Kstage.rearrange("p (c t) -> p c t", t=512)
            self._kstev = self.cp(
                "vector", kd3[:, :, 128 * ttl_p:128 * ttl_p + 128],
                TBk.rearrange("p (c t) -> p c t", t=128), sig=True)
            self._tbevs[kk] = self._kstev
        ev_ctkv_e = [None] * 4
        for par in range(2):
            v66 = Vstage[par].rearrange("p (h c) -> p h c", c=66)
            ev_vones = self.memset("gpsimd", v66[:, :, 64:65], 1.0, sig=True)

        for g in range(NG):
            gpar = g % 2
            for ttl in range(4):
                tt = 4 * g + ttl
                par = tt % 2
                self.wait("sync", *(gate_xs[par] or ()))
                ev_x = self.dma("sync", self.xsb[par][:],
                                self.Xd[128 * tt:128 * tt + 128, :], f"xs{par}")
                self.wait("vector", ev_x)
                self.wait("vector", ev_x)
                self.fence("vector", "ve")
                ev_sq = self.tt("vector", self.tsq[:], self.xsb[par][:],
                                self.xsb[par][:], ALU.mult, sig=True, rt=True)
                self.red("vector", st[:, 0:1], self.xsb[par][:], ALU.add)
                self.wait("vector", ev_sq)
                self.red("vector", st[:, 1:2], self.tsq[:], ALU.add)
                self.ts("vector", st[:, 2:3], st[:, 0:1], 1.0 / D, rt=True)
                self.ts("vector", st[:, 3:4], st[:, 1:2], 1.0 / D, rt=True)
                self.tt("vector", st[:, 4:5], st[:, 2:3], st[:, 2:3], ALU.mult, rt=True)
                ev_var = self.tt("vector", st[:, 3:4], st[:, 3:4], st[:, 4:5],
                                 ALU.subtract, sig=True, rt=True)
                self.wait("scalar", ev_var)
                self.act(st[:, 5:6], st[:, 3:4], AF.Ln, bias=self.epsb[:, 0:1], rt=True)
                ev_rstd = self.act(st[:, 6:7], st[:, 5:6], AF.Exp, scale=-0.5, sig=True)
                self.wait("vector", ev_rstd, ev_xnst[par])
                ev_xn = self.ts("vector", xnb[par], self.xsb[par][:], st[:, 2:3],
                                st[:, 6:7], ALU.subtract, ALU.mult, sig=True)
                self.swait("vector", ev_xn)
                gate_xs[par] = (ev_xn, ev_sq)
                self.wait("sync", ev_xn)
                ev_xnst[par] = self.dma("sync", self.XN[128 * tt:128 * tt + 128, :],
                                        xnb[par], f"xn{par}")
                dst3 = xng[gpar].rearrange("p (c t) -> p c t", t=512)
                self.fence("vector", "pe")
                xng_ev = self._tm2fm(xnb[par], par, dst3, ttl)

            self.wait("tensor", xng_ev, *ev_w)
            slots = [self.sp[0][:, 0:512], self.sp[0][:, 512:1024],
                     self.sp[1][:, 0:512], self.sp[1][:, 512:1024]]
            ev_ctkv = [None] * 4
            for fc in range(4):
                self.wait("tensor", pb[fc])
                for dc in range(8):
                    ev_ctkv[fc] = self.mm(
                        slots[fc],
                        self.wdkv_sb[:, 512 * dc + 128 * fc:512 * dc + 128 * fc + 128],
                        xng[gpar][:, 512 * dc:512 * dc + 512],
                        start=(dc == 0), stop=(dc == 7), sig=(dc == 7))
            ctkv_b = ctkv
            for fc in range(4):
                self.wait("scalar", ev_ctkv[fc])
                ev_ctkv_e[fc] = self.act(ctkv_b[:, 512 * fc:512 * fc + 512], slots[fc],
                                         AF.Identity, bias=self.bdkvc[:, fc:fc + 1], sig=True)
                pb[fc] = ev_ctkv_e[fc]
            for ttl in range(4):
                tt = 4 * g + ttl
                par = tt % 2
                self.wait("sync", gate_tab[par])
                ev_ct = self.dma("sync", ctab[par],
                                 mlacos_d[128 * tt:128 * tt + 128, :], f"tca{par}")
                ev_st2 = self.dma("sync", stab[par],
                                  mlasin_d[128 * tt:128 * tt + 128, :], f"tcb{par}")
                self.wait("tensor", ev_ctkv_e[3], pbo[0], pbo[1])
                ev_ktc = None
                for fh, bank in ((0, self.o0), (1, self.o1)):
                    for kc in range(4):
                        ev_ktc = self.mm(
                            bank[:, :],
                            ctkv_b[:, 512 * kc + 128 * ttl:512 * kc + 128 * ttl + 128],
                            self.wuk_sb[:, 1024 * kc + 512 * fh:1024 * kc + 512 * fh + 512],
                            start=(kc == 0), stop=(kc == 3), sig=(fh == 1 and kc == 3))
                self.wait("tensor", pb[0], pb[1])
                ev_ktr = None
                for fh in range(2):
                    for dc in range(8):
                        ev_ktr = self.mm(
                            self.sp[0][:, 512 * fh:512 * fh + 512],
                            xng[gpar][:, 512 * dc + 128 * ttl:512 * dc + 128 * ttl + 128],
                            self.wkr_sb[:, 1024 * dc + 512 * fh:1024 * dc + 512 * fh + 512],
                            start=(dc == 0), stop=(dc == 7), sig=(fh == 1 and dc == 7))
                self.wait("tensor", pb[2], pb[3])
                ev_v = None
                for fh in range(2):
                    for kc in range(4):
                        ev_v = self.mm(
                            self.sp[1][:, 512 * fh:512 * fh + 512],
                            ctkv_b[:, 512 * kc + 128 * ttl:512 * kc + 128 * ttl + 128],
                            self.wuv_sb[:, 1024 * kc + 512 * fh:1024 * kc + 512 * fh + 512],
                            start=(kc == 0), stop=(kc == 3), sig=(fh == 1 and kc == 3))
                flush_k()
                self.wait("vector", ev_ktc)
                self.tt("vector", Ktm[:, 0:512], self.o0[:, :], self.bukm[:, 0:512], ALU.add)
                evko = self.tt("vector", Ktm[:, 512:1024], self.o1[:, :],
                               self.bukm[:, 512:1024], ALU.add, sig=True)
                pbo[0] = pbo[1] = evko
                self.wait("vector", ev_ktr)
                self.tt("vector", Rtm[:, 0:512], self.sp[0][:, 0:512], self.bkrm[:, 0:512], ALU.add)
                ev_rtm = self.tt("vector", Rtm[:, 512:1024], self.sp[0][:, 512:1024],
                                 self.bkrm[:, 512:1024], ALU.add, sig=True)
                pb[0] = pb[1] = ev_rtm
                self.wait("vector", ev_v, ev_vst[par], ev_vones)
                v3 = Vstage[par].rearrange("p (h c) -> p h c", c=66)
                self.tt("vector", v3[:, 0:8, 0:64], self.sp[1][:, 0:512],
                        self.buvm[:, 0:512], ALU.add)
                ev_vs = self.tt("vector", v3[:, 8:16, 0:64], self.sp[1][:, 512:1024],
                                self.buvm[:, 512:1024], ALU.add, sig=True)
                pb[2] = pb[3] = ev_vs
                Re = Rtm.rearrange("p (n two) -> p n two", two=2)[:, :, 0:1]
                Ro = Rtm.rearrange("p (n two) -> p n two", two=2)[:, :, 1:2]
                Ke = Ktm.rearrange("p (n two) -> p n two", two=2)[:, :, 0:1]
                Ko = Ktm.rearrange("p (n two) -> p n two", two=2)[:, :, 1:2]
                self.wait("vector", ev_ct, ev_st2, ev_rtm)
                self.fence("vector", "ve")
                self.tt("vector", t1, Re, ctab[par], ALU.mult)
                ev_g1 = self.tt("vector", t2, Ro, stab[par], ALU.mult, sig=True)
                ev_g2 = self.tt("vector", t3, Re, stab[par], ALU.mult, sig=True)
                self.wait("vector", ev_g1)
                selft1 = self.tt("vector", t1, t1, t2, ALU.subtract, sig=True)
                ev_ke = self.tt("vector", Ke, Ke, t1, ALU.add, sig=True)
                self.wait("vector", selft1)
                ev_g3 = self.tt("vector", t2, Ro, ctab[par], ALU.mult, sig=True)
                self.wait("vector", ev_g2, ev_g3)
                gate_tab[par] = ev_g3
                self.tt("vector", t2, t2, t3, ALU.add)
                ev_ko = self.tt("vector", Ko, Ko, t2, ALU.add, sig=True)
                self.wait("vector", ev_ke, ev_ko)
                ev_ksq = self.tt("vector", ksq, Ktm, Ktm, ALU.mult, sig=True)
                self.wait("vector", ev_ksq)
                self.red("vector", st[:, 24:40],
                         ksq.rearrange("p (h d) -> p h d", d=64), ALU.add)
                self.tt("vector", st[:, 8:24], st[:, 8:24], st[:, 24:40], ALU.max, rt=True)
                ev_ktb = self.cp("vector", Ktb2[ttl % 2], Ktm, sig=True)
                pend_k[0] = (ttl, ttl % 2, ev_ktb)
                self.wait("sync", ev_vs)
                vv3 = Vstage[par].rearrange("p (h c) -> p h c", c=66)
                ev_vst[par] = self.dma("sync", self.VH[:, tt, :],
                                       vv3[:, :, 0:65], f"vs{par}")
            flush_k()
            self.wait("sync", self._kstev)
            ev_kdma = self.dma("sync", self.KH[:, :, 512 * g:512 * g + 512],
                               Kstage.rearrange("p (c u t) -> p c u t", c=8, u=4),
                               "kgrp")
            evkd[0] = ev_kdma
        ev_r0 = self.red("vector", st[:, 49:50], st[:, 8:24], ALU.max)
        self.wait("tensor", ev_r0, pbo[0], self._tbevs[1])
        ev_t = self.tr(self.u0[0:1, 0:128], st[:, 49:50], self.identf[:], sig=True)
        self.wait("vector", ev_t)
        self.red("vector", st[0:1, 48:49], self.u0[0:1, 0:128], ALU.max)

        self.barrier()

        ev_w = []
        for wi, (wsb, wd) in enumerate(((self.wdq_sb, wdq), (self.wuq_sb, wuq),
                                        (self.wqr_sb, wqr))):
            wsrc = wd.rearrange("(c p) f -> p c f", p=128)
            ev_w.append(self.dma("gpsimd", wsb, wsrc, f"w{wi}"))
        ctq_b = ctkv
        Qtm, qsq = Ktm, ksq
        gate_xs = [None, None]
        gate_tab = [None, None]
        ev_qdma = None
        pend_k[0] = None
        evkd[0] = None
        self._kstev = None
        pb = [None] * 4
        pbo = [None, None]
        for g in range(NG):
            gpar = g % 2
            self.memset("vector", st[:, 50:66], 0.0)
            for ttl in range(4):
                tt = 4 * g + ttl
                par = tt % 2
                self.wait("sync", gate_xs[par])
                ev_x = self.dma("sync", self.xsb[par][:],
                                self.XN[128 * tt:128 * tt + 128, :], f"xs{par}")
                self.wait("vector", ev_x)
                self.fence("vector", "pe")
                dst3 = xng[gpar].rearrange("p (c t) -> p c t", t=512)
                gate_xs[par] = self._tm2fm(self.xsb[par][:], par, dst3, ttl)
            self.wait("tensor", gate_xs[0], gate_xs[1], *ev_w)
            ev_ctq = [None] * 4
            slots = [self.sp[0][:, 0:512], self.sp[0][:, 512:1024],
                     self.sp[1][:, 0:512], self.sp[1][:, 512:1024]]
            for fc in range(4):
                self.wait("tensor", pb[fc])
                for dc in range(8):
                    ev_ctq[fc] = self.mm(
                        slots[fc],
                        self.wdq_sb[:, 512 * dc + 128 * fc:512 * dc + 128 * fc + 128],
                        xng[gpar][:, 512 * dc:512 * dc + 512],
                        start=(dc == 0), stop=(dc == 7), sig=(dc == 7))
            for fc in range(4):
                self.wait("scalar", ev_ctq[fc])
                pb[fc] = self.act(ctq_b[:, 512 * fc:512 * fc + 512], slots[fc],
                                  AF.Identity, bias=self.bdqc[:, fc:fc + 1], sig=True)
            for ttl in range(4):
                tt = 4 * g + ttl
                par = tt % 2
                self.wait("sync", gate_tab[par])
                ev_ct = self.dma("sync", ctab[par],
                                 mlacos_d[128 * tt:128 * tt + 128, :], f"tca{par}")
                ev_st2 = self.dma("sync", stab[par],
                                  mlasin_d[128 * tt:128 * tt + 128, :], f"tcb{par}")
                self.wait("tensor", pb[3], pbo[0], pbo[1])
                ev_qtc = None
                for fh, bank in ((0, self.o0), (1, self.o1)):
                    for kc in range(4):
                        ev_qtc = self.mm(
                            bank[:, :],
                            ctq_b[:, 512 * kc + 128 * ttl:512 * kc + 128 * ttl + 128],
                            self.wuq_sb[:, 1024 * kc + 512 * fh:1024 * kc + 512 * fh + 512],
                            start=(kc == 0), stop=(kc == 3), sig=(fh == 1 and kc == 3))
                self.wait("tensor", pb[0], pb[1])
                ev_qr = None
                for fh in range(2):
                    for kc in range(4):
                        ev_qr = self.mm(
                            self.sp[0][:, 512 * fh:512 * fh + 512],
                            ctq_b[:, 512 * kc + 128 * ttl:512 * kc + 128 * ttl + 128],
                            self.wqr_sb[:, 1024 * kc + 512 * fh:1024 * kc + 512 * fh + 512],
                            start=(kc == 0), stop=(kc == 3), sig=(fh == 1 and kc == 3))
                flush_k()
                self.wait("vector", ev_qtc)
                self.tt("vector", Qtm[:, 0:512], self.o0[:, :], self.buqm[:, 0:512], ALU.add)
                evqo = self.tt("vector", Qtm[:, 512:1024], self.o1[:, :],
                               self.buqm[:, 512:1024], ALU.add, sig=True)
                pbo[0] = pbo[1] = evqo
                self.wait("vector", ev_qr)
                self.tt("vector", Rtm[:, 0:512], self.sp[0][:, 0:512], self.bqrm[:, 0:512], ALU.add)
                ev_rtm = self.tt("vector", Rtm[:, 512:1024], self.sp[0][:, 512:1024],
                                 self.bqrm[:, 512:1024], ALU.add, sig=True)
                pb[0] = pb[1] = ev_rtm
                Re = Rtm.rearrange("p (n two) -> p n two", two=2)[:, :, 0:1]
                Ro = Rtm.rearrange("p (n two) -> p n two", two=2)[:, :, 1:2]
                Qe = Qtm.rearrange("p (n two) -> p n two", two=2)[:, :, 0:1]
                Qo = Qtm.rearrange("p (n two) -> p n two", two=2)[:, :, 1:2]
                self.wait("vector", ev_ct, ev_st2, ev_rtm)
                self.fence("vector", "ve")
                self.tt("vector", t1, Re, ctab[par], ALU.mult)
                ev_g1 = self.tt("vector", t2, Ro, stab[par], ALU.mult, sig=True)
                ev_g2 = self.tt("vector", t3, Re, stab[par], ALU.mult, sig=True)
                self.wait("vector", ev_g1)
                selft1 = self.tt("vector", t1, t1, t2, ALU.subtract, sig=True)
                self.tt("vector", Qe, Qe, t1, ALU.add)
                self.wait("vector", selft1)
                ev_g3 = self.tt("vector", t2, Ro, ctab[par], ALU.mult, sig=True)
                self.wait("vector", ev_g2, ev_g3)
                gate_tab[par] = ev_g3
                self.tt("vector", t2, t2, t3, ALU.add)
                ev_qo = self.tt("vector", Qo, Qo, t2, ALU.add, sig=True)
                self.wait("vector", ev_qo)
                ev_qsq = self.tt("vector", qsq, Qtm, Qtm, ALU.mult, sig=True)
                self.wait("vector", ev_qsq)
                self.red("vector", st[:, 24:40],
                         qsq.rearrange("p (h d) -> p h d", d=64), ALU.add)
                self.tt("vector", st[:, 50:66], st[:, 50:66], st[:, 24:40], ALU.max, rt=True)
                ev_qtb = self.cp("vector", Ktb2[ttl % 2], Qtm, sig=True)
                pend_k[0] = (ttl, ttl % 2, ev_qtb)
            flush_k()
            self.wait("sync", self._kstev)
            ev_qdma = self.dma("sync", self.QH[:, :, 512 * g:512 * g + 512],
                               Kstage.rearrange("p (c u t) -> p c u t", c=8, u=4),
                               "kgrp")
            evkd[0] = ev_qdma
            ev_r = self.red("vector", st[:, 49:50], st[:, 50:66], ALU.max, sig=True)
            self.wait("tensor", ev_r, self._tbevs[1])
            ev_t = self.tr(self.u0[0:1, 0:128], st[:, 49:50], self.identf[:], sig=True)
            self.wait("vector", ev_t)
            self.red("vector", st[0:1, 67:68], self.u0[0:1, 0:128], ALU.max)
            ev_p = self.tt("vector", st[0:1, 68:69], st[0:1, 67:68], st[0:1, 48:49],
                           ALU.mult, sig=True, rt=True)
            self.wait("scalar", ev_p)
            self.act(st[0:1, 69:70], st[0:1, 68:69], AF.Ln, rt=True)
            ev_sq = self.act(st[0:1, 70:71], st[0:1, 69:70], AF.Exp, scale=0.5, sig=True)
            self.wait("vector", ev_sq)
            ev_m = self.ts("vector", st[0:1, 71:72], st[0:1, 70:71], -ALPHA, sig=True)
            self.wait("tensor", ev_m)
            ev_b = self.mm(self.u0[:, 0:1], self.ones_f[0:1, 0:128],
                           st[0:1, 71:72], sig=True)
            self.wait("vector", ev_b)
            self._u0ev = self.cp("vector", st[:, 40 + g:41 + g], self.u0[:, 0:1],
                                 sig=True)
            self._tbevs[1] = self._u0ev
        self.barrier()

    def _p2(self, wo):
        abf, af32 = self.abf, self.af32
        st = self.stat
        kc_sb = [abf(0, 4096), abf(4096, 4096)]
        qc_sb = [abf(8192, 4096), abf(12288, 4096)]
        vp_sb = [abf(16384, 4160), abf(20544, 4160)]
        expS = [abf(24704, 1024), abf(25728, 1024)]
        O_fm = abf(26752, 32768)
        asb = [af32(59520, 1024), af32(61568, 1024)]
        wo_sb = abf(63616, 8192)
        wobm = af32(71808, 1024)
        rb = af32(73856, 512)

        ev_wo = self.dma("gpsimd", wo_sb, wo.rearrange("(c p) f -> p c f", p=128), "w0")
        ev_wobm = self.dma("sync", wobm, self.wobm_d[:, :], "cw")

        def load_hp(hp):
            par = hp % 2
            evk = self.dma("sync", kc_sb[par], self.KH[:, hp, :], f"kc{par}")
            evq = self.dma("sync", qc_sb[par], self.QH[:, hp, :], f"qc{par}")
            evv = self.dma("sync", vp_sb[par],
                           self.VH[:, :, 130 * hp:130 * hp + 130], f"vc{par}")
            return (evk, evq, evv)

        ev_oh = [None, None]
        evE_prev = [None, None]
        ofin = [None, None]
        pend = load_hp(0)
        for hp in range(8):
            par = hp % 2
            evk, evq, evv = pend
            if hp < 7:
                self.fence("sync", "pe")
                pend = load_hp(hp + 1)
            self.wait("tensor", evk, evq, evv)
            kc, qc, vp = kc_sb[par], qc_sb[par], vp_sb[par]
            for g in range(NG):
                self.wait("tensor", evE_prev[0], evE_prev[1], ofin[0], ofin[1])
                evS = [None] * 32
                evE = [None] * 32
                evPV = [None] * 32

                def pv(kt):
                    self.wait("tensor", evE[kt])
                    self.mm(self.o0[0:65, :], vp[:, 130 * kt:130 * kt + 65],
                            expS[kt % 2][:, 0:512],
                            start=(kt == 0), stop=(kt == 31))
                    evPV[kt] = self.mm(self.o1[0:65, :], vp[:, 130 * kt + 65:130 * kt + 130],
                                       expS[kt % 2][:, 512:1024],
                                       start=(kt == 0), stop=(kt == 31), sig=True)

                for kt in range(32):
                    if kt >= 2:
                        self.wait("tensor", evE[kt - 2])
                    self.mm(self.sp[kt % 2][:, 0:512],
                            kc[0:64, 128 * kt:128 * kt + 128],
                            qc[0:64, 512 * g:512 * g + 512], tp=(0, 0))
                    evS[kt] = self.mm(self.sp[kt % 2][:, 512:1024],
                                      kc[64:128, 128 * kt:128 * kt + 128],
                                      qc[64:128, 512 * g:512 * g + 512],
                                      tp=(64, 0), sig=True)
                    if kt >= 1:
                        pv(kt - 1)
                    self.wait("scalar", evS[kt])
                    if kt >= 2:
                        self.wait("scalar", evPV[kt - 2])
                    evE[kt] = self.act(expS[kt % 2][:], self.sp[kt % 2][:, 0:1024],
                                       AF.Exp, bias=st[:, 40 + g:41 + g],
                                       scale=ALPHA, sig=True)
                pv(31)
                evE_prev[0], evE_prev[1] = evE[30], evE[31]
                for hodd, bank in ((0, self.o0), (1, self.o1)):
                    self.wait("vector", evPV[31])
                    evr = self.recip(self.scr64[64:65, :], bank[64:65, :], sig=True)
                    self.wait("tensor", evr)
                    evb = self.mm(self.u0[0:64, :], self.ones_f[64:65, 0:64],
                                  self.scr64[64:65, :], tp=(64, 0), sig=True)
                    self.wait("vector", evb)
                    evrb = self.cp("vector", rb[0:64, :], self.u0[0:64, :], sig=True)
                    cols = slice(4096 * hp + 512 * g, 4096 * hp + 512 * g + 512)
                    if hodd == 0:
                        ofin[0] = self.tt("vector", O_fm[0:64, cols], bank[0:64, :],
                                          rb[0:64, :], ALU.mult, sig=True)
                    else:
                        hpar = g % 2
                        self.wait("vector", ev_oh[hpar])
                        evh = self.tt("vector",
                                      self.tmph1[0:64, 512 * hpar:512 * hpar + 512],
                                      bank[0:64, :], rb[0:64, :], ALU.mult, sig=True)
                        self.wait("gpsimd", evh)
                        ofin[1] = evh
                        ev_oh[hpar] = self.dma(
                            "gpsimd", O_fm[64:128, cols],
                            self.tmph1[0:64, 512 * hpar:512 * hpar + 512], f"oh{hpar}")
        self.wait("tensor", ev_oh[0], ev_oh[1])
        self.fence("tensor", "ve")
        self.fence("tensor", "se")
        self.wait("tensor", ev_wo)
        ev_ast = [None, None]
        ev_e = [None, None]
        for tt in range(NT):
            par = tt % 2
            psum = self.sp[par]
            if ev_e[par] is not None:
                self.wait("tensor", ev_e[par])
            ev = None
            for fh in range(2):
                for c in range(8):
                    ev = self.mm(psum[:, 512 * fh:512 * fh + 512],
                                 O_fm[:, 4096 * c + 128 * tt:4096 * c + 128 * tt + 128],
                                 wo_sb[:, 1024 * c + 512 * fh:1024 * c + 512 * fh + 512],
                                 start=(c == 0), stop=(c == 7),
                                 sig=(fh == 1 and c == 7))
            self.wait("vector", ev, ev_ast[par], ev_wobm)
            self.tt("vector", asb[par][:, 0:512], psum[:, 0:512],
                    wobm[:, 0:512], ALU.add)
            ev_e[par] = self.tt("vector", asb[par][:, 512:1024], psum[:, 512:1024],
                                wobm[:, 512:1024], ALU.add, sig=True)
            self.wait("sync", ev_e[par])
            ev_ast[par] = self.dma("sync", self.AOUT[128 * tt:128 * tt + 128, :],
                                   asb[par], f"ao{par}")
        self.barrier()

    def _ffn(self, l, wqkv2, wo2, w1p, w2p, b1c_d, b2m_d, lng_d, lnb_d,
             qsm_d, ksm_d, xpcq_d, xpsq_d, xpck_d, xpsk_d):
        abf, af32 = self.abf, self.af32
        st = self.stat
        XIN = self.XN if l == 0 else self.XRES
        self.dma("sync", self.b1c[:], b1c_d[l], "lc0")
        self.dma("sync", self.b2m[:], b2m_d[l], "lc1")
        self.dma("sync", self.lng[:], lng_d[l], "lc2")
        self.dma("sync", self.lnb[:], lnb_d[l], "lc3")
        ev_qs = self.dma("sync", self.qsm[:], qsm_d[l], "lc4")
        ev_ks = self.dma("sync", self.ksm[:], ksm_d[l], "lc5")

        x_fm = abf(0, 32832)
        aa = abf(0, 22528)
        w1blk = [abf(22528, 4096), abf(26624, 4096)]
        gelu_t = abf(30720, 512)
        xr2 = [af32(22528, 1024), af32(26624, 1024)]
        wqkv_sb = abf(32832, 12288)
        k_ring = abf(45120, 4096)
        v_ring = abf(49216, 4096)
        q_fm = abf(53312, 2048)
        O_fmw = abf(55360, 2048)
        wo2_sb = abf(57408, 4096)
        cqb = af32(61504, 512)
        sqb = af32(62528, 512)
        ckb = af32(63552, 512)
        skb = af32(64576, 512)
        ln_fm = abf(32832, 32768)
        qtm2 = [af32(65600, 512), af32(80448, 512)]
        qsq2 = af32(66624, 512)
        qrot2 = [af32(67648, 512), af32(81472, 512)]
        qtb2 = [abf(68672, 512), abf(82496, 512)]
        Sm = [[af32(69184, 640), af32(70464, 640)],
              [af32(83008, 640), af32(84288, 640)]]
        pbf = [[abf(71744, 640), abf(72384, 640)],
               [abf(85568, 640), abf(86208, 640)]]
        pT = [abf(73024, 640), abf(73664, 640)]
        t1 = af32(74304, 512)
        t2 = af32(75328, 512)
        xr = [af32(76352, 1024), af32(78400, 1024)]
        hsq8 = abf(65600, 4096)
        srow = af32(69696, 1024)
        rstdbc = af32(71744, 512)
        omegabc = af32(72768, 512)
        t1c = af32(73792, 512)
        t2c = af32(74816, 512)
        w2sb = abf(65600, 22528)

        def build_xfm(src_dram):
            self._ds("fA")
            z3 = x_fm.rearrange("p (c t) -> p c t", t=4104)
            self.fence("gpsimd", "pe")
            ev_z = self.memset("gpsimd", z3[:, :, 0:1], 0.0, sig=True)
            self.wait("vector", ev_z)
            gate_xs = [None, None]
            for tt in range(NT):
                par = tt % 2
                self.wait("sync", gate_xs[par])
                ev_x = self.dma("sync", self.xsb[par][:],
                                src_dram[128 * tt:128 * tt + 128, :], f"xs{par}")
                self.wait("vector", ev_x)
                self.fence("vector", "pe")
                ev_bb = self.cp("vector", self.xnbb[par][:], self.xsb[par][:],
                                sig=True)
                gate_xs[par] = ev_bb
                TBk, kk = self.tb_get()
                self.wait("tensor", ev_bb, self._tbevs[kk])
                ev_tr = None
                for c in range(8):
                    ev_tr = self.tr(TBk[:, 128 * c:128 * c + 128],
                                    self.xnbb[par][:, 128 * c:128 * c + 128],
                                    self.identb[:], sig=(c == 7))
                self.wait("vector", ev_tr)
                self._tbevs[kk] = self.cp(
                    "vector", z3[:, :, 1 + 128 * tt:1 + 128 * tt + 128],
                    TBk.rearrange("p (c t) -> p c t", t=128), sig=True)
            self._ds(None)

        build_xfm(XIN)
        ev_wqkv = self.dma("gpsimd", wqkv_sb,
                           wqkv2[l].rearrange("(c p) f -> p c f", p=128), "w1")
        if WO2_SYNC_CAST:
            wo2_f32 = self.af32(80448, 4096)
            evt = self.dma("sync", wo2_f32,
                           wo2[l].rearrange("(c p) f -> p c f", p=128), "w2f")
            self.wait("vector", evt)
            ev_wo2 = self.cp("vector", wo2_sb, wo2_f32, sig=True)
        else:
            ev_wo2 = self.dma("gpsimd", wo2_sb,
                              wo2[l].rearrange("(c p) f -> p c f", p=128), "w2")
        self.barrier()

        self._gate_xt = None
        ev_xstw = [None, None]
        bankev = [None, None, None, None]
        oev = [None, None]
        pbf_war = [[None, None], [None, None]]
        slots4 = [self.sp[0][:, 0:512], self.sp[0][:, 512:1024],
                  self.sp[1][:, 0:512], self.sp[1][:, 512:1024]]
        for w in range(NG):
            ks = w % 2
            self.wait("tensor", ev_wqkv)
            self._ds("fqkv")
            pend_tr = [None]

            def flush_tr():
                if pend_tr[0] is None:
                    return
                ttl_p, fg_p, pp_p, evb_p = pend_tr[0]
                pend_tr[0] = None
                TBk, kk = self.tb_get()
                self.wait("tensor", evb_p, self._tbevs[kk])
                ev_tr = None
                for c in range(4):
                    ev_tr = self.tr(TBk[:, 128 * c:128 * c + 128],
                                    qtb2[pp_p][:, 128 * c:128 * c + 128],
                                    self.identb[:], sig=(c == 3))
                self.wait("vector", ev_tr)
                if fg_p == 0:
                    d3 = q_fm.rearrange("p (c t) -> p c t", t=512)
                    self._tbevs[kk] = self.cp(
                        "vector", d3[:, :, 128 * ttl_p:128 * ttl_p + 128],
                        TBk[:, 0:512].rearrange("p (c t) -> p c t", t=128),
                        sig=True)
                else:
                    d3 = k_ring.rearrange("p (s c t) -> p s c t", s=2, t=512)
                    self._tbevs[kk] = self.cp(
                        "vector", d3[:, ksp[0], :, 128 * ttl_p:128 * ttl_p + 128],
                        TBk[:, 0:512].rearrange("p (c t) -> p c t", t=128),
                        sig=True)

            ksp = [ks]
            for ttl in range(4):
                gt = 4 * w + ttl
                self.wait("sync", self._gate_xt)
                evtabs = [self.dma("sync", cqb, xpcq_d[128 * gt:128 * gt + 128, :], "tq0"),
                          self.dma("sync", sqb, xpsq_d[128 * gt:128 * gt + 128, :], "tq1"),
                          self.dma("sync", ckb, xpck_d[128 * gt:128 * gt + 128, :], "tq2"),
                          self.dma("sync", skb, xpsk_d[128 * gt:128 * gt + 128, :], "tq3")]
                for fg in range(3):
                    sidx = (3 * ttl + fg) % 4
                    bank = slots4[sidx]
                    self.wait("tensor", bankev[sidx])
                    ev_mm = None
                    for dc in range(8):
                        off = 4104 * dc + (0 if dc >= 4 else 1) + 128 * gt
                        ev_mm = self.mm(bank, x_fm[:, off:off + 128],
                                        wqkv_sb[:, 1536 * dc + 512 * fg:1536 * dc + 512 * fg + 512],
                                        start=(dc == 0), stop=(dc == 7), sig=(dc == 7))
                    flush_tr()
                    if fg == 2:
                        self.wait("vector", ev_mm)
                        bankev[sidx] = self.cp(
                            "vector",
                            v_ring[:, 2048 * ks + 512 * ttl:2048 * ks + 512 * ttl + 512],
                            bank, sig=True)
                        continue
                    pp = fg % 2
                    qtm, qrot = qtm2[pp], qrot2[pp]
                    tabs = (cqb, sqb) if fg == 0 else (ckb, skb)
                    smap = self.qsm if fg == 0 else self.ksm
                    ev_sm = ev_qs if fg == 0 else ev_ks
                    self.wait("vector", ev_mm)
                    evq = self.cp("vector", qtm, bank, sig=True)
                    bankev[sidx] = evq
                    self.tt("vector", qsq2, qtm, qtm, ALU.mult)
                    self.red("vector", st[:, 24:32],
                             qsq2.rearrange("p (h d) -> p h d", d=64), ALU.add)
                    evmx = self.op("vector", lambda e: e.tensor_scalar_max(
                        st[:, 24:32], st[:, 24:32], 1e-24), sig=True)
                    self.swait("vector", evmx)
                    self.wait("scalar", evmx)
                    self.act(st[:, 32:40], st[:, 24:32], AF.Ln, rt=True)
                    evrn = self.act(st[:, 24:32], st[:, 32:40], AF.Exp,
                                    scale=-0.5, sig=True)
                    self.wait("vector", evrn, ev_sm, *evtabs)
                    q3 = qtm.rearrange("p (h z) -> p h z", z=64)
                    rnb = bass.AP(self.stat, 24, [[96, 128], [1, 8], [0, 64]])
                    self.tt("vector", q3, q3, rnb, ALU.mult)
                    self.tt("vector", qtm, qtm, smap[:], ALU.mult)
                    r3 = qrot.rearrange("p (h z) -> p h z", z=64)
                    c3 = tabs[0].rearrange("p (h z) -> p h z", z=64)
                    s3 = tabs[1].rearrange("p (h z) -> p h z", z=64)
                    e_, o_ = q3[:, :, 0:32], q3[:, :, 32:64]
                    ce, se_ = c3[:, :, 0:32], s3[:, :, 0:32]
                    t13 = t1[:, 0:256].rearrange("p (h z) -> p h z", z=32)
                    t23 = t2[:, 0:256].rearrange("p (h z) -> p h z", z=32)
                    self.tt("vector", t13, e_, ce, ALU.mult)
                    self.tt("vector", t23, o_, se_, ALU.mult)
                    self.tt("vector", r3[:, :, 0:32], t13, t23, ALU.subtract)
                    self.tt("vector", t13, o_, ce, ALU.mult)
                    ev5 = self.tt("vector", t23, e_, se_, ALU.mult, sig=True)
                    self._gate_xt = ev5
                    self.tt("vector", r3[:, :, 32:64], t13, t23, ALU.add, rt=True)
                    evb = self.cp("vector", qtb2[pp], qrot, sig=True)
                    pend_tr[0] = (ttl, fg, pp, evb)
            flush_tr()
            self._ds("fattn")
            self.wait("tensor", self._tbevs[0], self._tbevs[1])
            for hp in range(4):
                ev_pv_last = [None, None]

                def p1a(a):
                    apar = a % 2
                    sb_ = 24 + 8 * apar
                    qb = 512 * w + 128 * a
                    kw0 = qb - 512
                    c0 = max(0, -kw0)
                    kstart = max(0, kw0)
                    width = qb + 128 - kstart
                    nb = width // 128
                    segs = []
                    j = kstart
                    pc = 0
                    while j < qb + 128:
                        wslot = (j // 512) % 2
                        tloc = j % 512
                        n = min(512 - tloc, qb + 128 - j)
                        n = min(n, (512 if pc < 512 else 1024) - pc)
                        segs.append((pc, n, wslot, tloc))
                        j += n
                        pc += n
                    evnb2 = [None, None]
                    for hh in range(2):
                        bank = self.sp[hh]
                        self.wait("tensor", bankev[2 * hh], bankev[2 * hh + 1])
                        qrow = slice(0, 64) if hh == 0 else slice(64, 128)
                        tp = (0, 0) if hh == 0 else (64, 0)
                        ev_s = None
                        for si, (pc, n, wslot, tloc) in enumerate(segs):
                            ev_s = self.mm(
                                bank[:, pc:pc + n],
                                q_fm[qrow, 512 * hp + 128 * a:512 * hp + 128 * a + 128],
                                k_ring[qrow, 2048 * wslot + 512 * hp + tloc:
                                       2048 * wslot + 512 * hp + tloc + n],
                                tp=tp, sig=(si == len(segs) - 1))
                        self.wait("vector", ev_s)
                        evm = self.tt("vector", Sm[apar][hh][:, 0:width],
                                      bank[:, 0:width],
                                      self.mask[:, c0:c0 + width], ALU.add, sig=True)
                        bankev[2 * hh] = bankev[2 * hh + 1] = evm
                        self.red("vector", st[:, sb_ + hh:sb_ + 1 + hh],
                                 Sm[apar][hh][:, 0:width], ALU.max, sig=True)
                        evnb2[hh] = self.ts("vector", st[:, sb_ + 2 + hh:sb_ + 3 + hh],
                                            st[:, sb_ + hh:sb_ + 1 + hh], -QSC,
                                            sig=True)
                    return [a, apar, width, nb, kstart, evnb2, None, None]

                def p1b(ctx):
                    a, apar, width, nb, kstart, evnb2, _, _ = ctx
                    sb_ = 24 + 8 * apar
                    evex2 = [None, None]
                    for hh in range(2):
                        self.wait("scalar", evnb2[hh], pbf_war[apar][hh])
                        evex2[hh] = self.act(pbf[apar][hh][:, 0:width],
                                             Sm[apar][hh][:, 0:width],
                                             AF.Exp, bias=st[:, sb_ + 2 + hh:sb_ + 3 + hh],
                                             scale=QSC,
                                             accum=st[:, sb_ + 4 + hh:sb_ + 5 + hh],
                                             sig=True)
                    ctx[6] = evex2

                def p2a(ctx):
                    a, apar, width, nb, kstart, _, evex2, _ = ctx
                    sb_ = 24 + 8 * apar
                    evnm2 = [None, None]
                    for hh in range(2):
                        self.wait("vector", evex2[hh])
                        evrc = self.recip(st[:, sb_ + 6 + hh:sb_ + 7 + hh],
                                          st[:, sb_ + 4 + hh:sb_ + 5 + hh], sig=True)
                        self.wait("scalar", evrc)
                        evnm2[hh] = self.act(pbf[apar][hh][:, 0:width],
                                             pbf[apar][hh][:, 0:width],
                                             AF.Copy,
                                             scale=st[:, sb_ + 6 + hh:sb_ + 7 + hh],
                                             sig=True)
                    ctx[7] = evnm2

                def p2b(ctx):
                    a, apar, width, nb, kstart, _, _, evnm2 = ctx
                    for hh in range(2):
                        TBk, kk = self.tb_get()
                        self.wait("tensor", evnm2[hh], self._tbevs[kk])
                        ev_tr = None
                        for kb in range(nb):
                            ev_tr = self.tr(TBk[:, 128 * kb:128 * kb + 128],
                                            pbf[apar][hh][:, 128 * kb:128 * kb + 128],
                                            self.identb[:], sig=(kb == nb - 1))
                        pbf_war[apar][hh] = ev_tr
                        self.wait("vector", ev_tr)
                        evpt = self.cp("vector", pT[hh][:, 0:width],
                                       TBk[:, 0:width], sig=True)
                        self._tbevs[kk] = evpt
                        self.wait("tensor", evpt)
                        if a == 0:
                            self.wait("tensor", oev[hh])
                        obank = self.o0 if hh == 0 else self.o1
                        orow = slice(0, 64) if hh == 0 else slice(64, 128)
                        otp = (0, 0) if hh == 0 else (0, 64)
                        h = 2 * hp + hh
                        ev_pv = None
                        for kb in range(nb):
                            j = kstart + 128 * kb
                            wslot = (j // 512) % 2
                            tl = (j % 512) // 128
                            vcol = 2048 * wslot + 512 * tl + 64 * h
                            ev_pv = self.mm(obank[orow, 128 * a:128 * a + 128],
                                            v_ring[:, vcol:vcol + 64],
                                            pT[hh][:, 128 * kb:128 * kb + 128],
                                            start=(kb == 0), stop=(kb == nb - 1),
                                            tp=otp, sig=(kb == nb - 1))
                        if a == 3:
                            ev_pv_last[hh] = ev_pv

                prev = None
                for a in range(5):
                    if prev is not None:
                        p2a(prev)
                    cur = None
                    if a < 4:
                        cur = p1a(a)
                    if prev is not None:
                        p2b(prev)
                    if cur is not None:
                        p1b(cur)
                    prev = cur
                self.wait("vector", ev_pv_last[0], ev_pv_last[1])
                oev[0] = self.cp("vector", O_fmw[0:64, 512 * hp:512 * hp + 512],
                                 self.o0[0:64, :], sig=True)
                oev[1] = self.cp("vector", O_fmw[64:128, 512 * hp:512 * hp + 512],
                                 self.o1[64:128, :], sig=True)
            if w == 0 and "FDBG" in self.dbg and l == 0:
                fd = self.nc.dram_tensor("FDBG", [128, 16384], BF16,
                                         kind="ExternalOutput").ap()
                evm1 = self.milestone("vector")
                self.wait("sync", evm1)
                self.dma("sync", fd[:, 0:4104], x_fm[:, 0:4104], "fdbg0")
                self.dma("sync", fd[:, 4104:6152], q_fm, "fdbg1")
                self.dma("sync", fd[:, 6152:8200], k_ring[:, 0:2048], "fdbg2")
                self.dma("sync", fd[:, 8200:10248], v_ring[:, 0:2048], "fdbg3")
                self.dma("sync", fd[:, 10248:12296], O_fmw, "fdbg4")
                self.dma("sync", fd[:, 12296:12936], pT[0], "fdbg5")
                self.dma("sync", fd[:, 13000:13080],
                         self.stat[:, 0:40].bitcast(BF16), "fdbg6")
            self._ds("fwo")
            self.wait("tensor", ev_wo2)
            for ttl in range(4):
                tt = 4 * w + ttl
                par = ttl % 2
                self.wait("sync", ev_xstw[par])
                ev_x = self.dma("sync", xr[par], XIN[128 * tt:128 * tt + 128, :],
                                f"xr{par}")
                self.wait("tensor", bankev[2 * par], bankev[2 * par + 1],
                          oev[0], oev[1])
                ev = None
                for fh in range(2):
                    for oc in range(4):
                        ev = self.mm(self.sp[par][:, 512 * fh:512 * fh + 512],
                                     O_fmw[:, 512 * oc + 128 * ttl:512 * oc + 128 * ttl + 128],
                                     wo2_sb[:, 1024 * oc + 512 * fh:1024 * oc + 512 * fh + 512],
                                     start=(oc == 0), stop=(oc == 3),
                                     sig=(fh == 1 and oc == 3))
                self.wait("vector", ev, ev_x, ev_xstw[par])
                evu = self.tt("vector", xr[par], xr[par], self.sp[par][:, 0:1024],
                              ALU.add, sig=True)
                bankev[2 * par] = bankev[2 * par + 1] = evu
                self.wait("sync", evu)
                if w == 0 and ttl == 0 and l == 0 and "FDBG2" in self.dbg:
                    fd2 = self.nc.dram_tensor("FDBG2", [128, 1024], F32,
                                              kind="ExternalOutput").ap()
                    fd3 = self.nc.dram_tensor("FDBG3", [128, 6144], BF16,
                                              kind="ExternalOutput").ap()
                    self.dma("sync", fd2[:, :], xr[par], "fdbg7")
                    self.dma("sync", fd3[:, 0:4096], wo2_sb, "fdbg8")
                    self.dma("sync", fd3[:, 4096:6144], O_fmw, "fdbg9")
                ev_xstw[par] = self.dma("sync", self.XRES[128 * tt:128 * tt + 128, :],
                                        xr[par], f"xo{par}")
            self._ds(None)
        self.barrier()

        build_xfm(self.XRES)
        self.barrier()

        self._ds("fC1")
        for gq in range(NG):
            evh = None
            for dc in range(8):
                off = 4104 * dc + (0 if dc >= 4 else 1) + 512 * gq
                evh = self.tt("vector", hsq8[:, 512 * dc:512 * dc + 512],
                              x_fm[:, off:off + 512], x_fm[:, off:off + 512],
                              ALU.mult, sig=(dc == 7))
            self.fence("tensor", "ve")
            self.fence("tensor", "se")
            evsum = None
            for dc in range(8):
                off = 4104 * dc + (0 if dc >= 4 else 1) + 512 * gq
                evsum = self.mm(self.u0[0:1, :], self.ones_b[:, 0:1],
                                x_fm[:, off:off + 512],
                                start=(dc == 0), stop=(dc == 7), sig=(dc == 7))
            self.wait("tensor", evh)
            evsq = None
            for dc in range(8):
                evsq = self.mm(self.o0[0:1, :], self.ones_b[:, 0:1],
                               hsq8[:, 512 * dc:512 * dc + 512],
                               start=(dc == 0), stop=(dc == 7), sig=(dc == 7))
            mean_r = srow[0:1, 0:512]
            var_r = srow[0:1, 512:1024]
            self.wait("vector", evsum, evsq)
            self.ts("vector", mean_r, self.u0[0:1, :], 1.0 / D, rt=True)
            self.ts("vector", var_r, self.o0[0:1, :], 1.0 / D, rt=True)
            self.tt("vector", self.scr64[0:1, 0:512], mean_r, mean_r, ALU.mult, rt=True)
            evv = self.tt("vector", var_r, var_r, self.scr64[0:1, 0:512],
                          ALU.subtract, sig=True, rt=True)
            self.wait("scalar", evv)
            self.act(self.scr64[0:1, 0:512], var_r, AF.Ln, bias=self.epsb[0:1, 0:1], rt=True)
            evrs = self.act(var_r, self.scr64[0:1, 0:512], AF.Exp, scale=-0.5, sig=True)
            self.wait("vector", evrs)
            evom = self.tt("vector", mean_r, mean_r, var_r, ALU.mult, sig=True)
            self.wait("tensor", evom)
            evb1 = self.mm(self.o1[:, :], self.ones_f[0:1, 0:128], var_r, sig=True)
            evb2 = self.mm(self.u0[:, :], self.ones_f[0:1, 0:128], mean_r, sig=True)
            self.fence("vector", "ge")
            self.wait("vector", evb1, evb2)
            self.cp("vector", rstdbc, self.o1[:, :])
            evbc = self.cp("vector", omegabc, self.u0[:, :], sig=True)
            self.wait("vector", evbc)
            for dc in range(8):
                off = 4104 * dc + (0 if dc >= 4 else 1) + 512 * gq
                self.fence("vector", "ve")
                evp1 = self.tt("vector", t1c, x_fm[:, off:off + 512], rstdbc,
                               ALU.mult, sig=True)
                self.wait("vector", evp1)
                self.tt("vector", t2c, t1c, omegabc, ALU.subtract)
                self.ts("vector",
                        ln_fm[:, 4096 * dc + 512 * gq:4096 * dc + 512 * gq + 512],
                        t2c, self.lng[:, dc:dc + 1], self.lnb[:, dc:dc + 1],
                        ALU.mult, ALU.add)
        self._ds(None)
        self.barrier()

        self._ds("fC2")
        for th in range(4):
            tok0 = 1024 * th
            slots = [self.sp[0][:, 0:512], self.sp[0][:, 512:1024],
                     self.sp[1][:, 0:512], self.sp[1][:, 512:1024],
                     self.o0[:, :], self.o1[:, :]]
            ev_ev = [None] * 6
            ev_wb = [None, None]
            for hb in range(11):
                wpar = hb % 2
                self.fence("gpsimd", "pe")
                ev_wb[wpar] = self.dma(
                    "gpsimd", w1blk[wpar],
                    w1p[l].rearrange("(c p) f -> p c f", p=128)[:, :, 512 * hb:512 * hb + 512],
                    f"w1b{wpar}")
                self.wait("tensor", ev_wb[wpar])
                for hcl in range(4):
                    hc = 4 * hb + hcl
                    for g2 in range(2):
                        sidx = (8 * hb + hcl * 2 + g2) % 6
                        if ev_ev[sidx] is not None:
                            self.wait("tensor", ev_ev[sidx])
                        ev = None
                        for dc in range(8):
                            ev = self.mm(
                                slots[sidx],
                                w1blk[wpar][:, 512 * dc + 128 * hcl:512 * dc + 128 * hcl + 128],
                                ln_fm[:, 4096 * dc + tok0 + 512 * g2:
                                      4096 * dc + tok0 + 512 * g2 + 512],
                                start=(dc == 0), stop=(dc == 7), sig=(dc == 7))
                        self.wait("scalar", ev)
                        if hc < NHC:
                            acol = 1024 * hc + 512 * g2
                            ev_ev[sidx] = self.act(aa[:, acol:acol + 512], slots[sidx],
                                                   AF.Identity,
                                                   bias=self.b1c[:, hc:hc + 1], sig=True)
                        else:
                            acol = 1024 * (hc - NHC) + 512 * g2
                            self.fence("scalar", "ve")
                            evg = self.act(gelu_t[:], slots[sidx], AF.Gelu,
                                           bias=self.b1c[:, hc:hc + 1], sig=True)
                            self.wait("vector", evg)
                            ev_ev[sidx] = self.tt("vector", aa[:, acol:acol + 512],
                                                  aa[:, acol:acol + 512], gelu_t[:],
                                                  ALU.mult, sig=True)
            self.barrier()
            self.fence("gpsimd", "pe")
            ev_w2 = self.dma("gpsimd", w2sb,
                             w2p[l].rearrange("(c p) f -> p c f", p=128), "w2s")
            self.wait("tensor", ev_w2)
            ev_xst = [None, None]
            ev_ee = [None, None]
            for ttl in range(8):
                tt = 8 * th + ttl
                par = ttl % 2
                self.wait("sync", ev_xst[par])
                ev_x = self.dma("sync", xr2[par],
                                self.XRES[128 * tt:128 * tt + 128, :], f"xq{par}")
                if ev_ee[par] is not None:
                    self.wait("tensor", ev_ee[par])
                ev = None
                for fh in range(2):
                    for hc in range(NHC):
                        ev = self.mm(
                            self.sp[par][:, 512 * fh:512 * fh + 512],
                            aa[:, 1024 * hc + 128 * ttl:1024 * hc + 128 * ttl + 128],
                            w2sb[:, 1024 * hc + 512 * fh:1024 * hc + 512 * fh + 512],
                            start=(hc == 0), stop=(hc == NHC - 1),
                            sig=(fh == 1 and hc == NHC - 1))
                self.wait("vector", ev, ev_x, ev_xst[par])
                self.tt("vector", xr2[par], xr2[par], self.b2m[:], ALU.add)
                ev_ee[par] = self.tt("vector", xr2[par], xr2[par],
                                     self.sp[par][:, 0:1024], ALU.add, sig=True)
                if l == 1:
                    self.wait("gpsimd", ev_ee[par])
                    e1 = self.dacc(xr2[par], self.Xd[128 * tt:128 * tt + 128, :],
                                   f"qa{par}")
                    self.wait("gpsimd", e1)
                    e2 = self.dacc(xr2[par], self.AOUT[128 * tt:128 * tt + 128, :],
                                   f"qb{par}")
                    self.wait("sync", e2)
                    ev_xst[par] = self.dma("sync",
                                           self.OUT[128 * tt:128 * tt + 128, :],
                                           xr2[par], f"xqs{par}")
                else:
                    self.wait("sync", ev_ee[par])
                    ev_xst[par] = self.dma("sync",
                                           self.XRES[128 * tt:128 * tt + 128, :],
                                           xr2[par], f"xqs{par}")
            self.barrier()
        self._ds(None)

    def _p5(self):
        af32 = self.af32
        b1 = [af32(0, 1024), af32(2048, 1024)]
        b2 = [af32(4096, 1024), af32(6144, 1024)]
        b3 = [af32(8192, 1024), af32(10240, 1024)]
        ob = [af32(12288, 1024), af32(14336, 1024)]
        ev_st = [None, None]
        gate5 = [None, None]
        for tt in range(NT):
            par = tt % 2
            sl = slice(128 * tt, 128 * tt + 128)
            self.wait("sync", gate5[par])
            e1 = self.dma("sync", b1[par], self.Xd[sl, :], f"p5a{par}")
            e2 = self.dma("sync", b2[par], self.AOUT[sl, :], f"p5b{par}")
            e3 = self.dma("sync", b3[par], self.XRES[sl, :], f"p5c{par}")
            self.wait("vector", e1, e2, e3, ev_st[par])
            self.tt("vector", ob[par], b1[par], b2[par], ALU.add)
            ev = self.tt("vector", ob[par], ob[par], b3[par], ALU.add, sig=True)
            gate5[par] = ev
            self.wait("sync", ev)
            ev_st[par] = self.dma("sync", self.OUT[sl, :], ob[par], f"os{par}")


_BUILD_CACHE = {}


def _get_nc(dbg=(), phases=("p0", "p1", "p2", "f0", "f1")):
    key = (tuple(sorted(dbg)), tuple(phases))
    if key not in _BUILD_CACHE:
        _BUILD_CACHE[key] = MK(dbg).build(phases)
    return _BUILD_CACHE[key]


def kernel(**inputs):
    nc = _get_nc()
    shared = _prep_shared(inputs)
    x = np.asarray(inputs["L0_attn_x"], np.float32)
    in_maps = []
    for c in range(8):
        m = dict(shared)
        m["x"] = np.ascontiguousarray(x[c])
        in_maps.append(m)
    res = run_bass_kernel_spmd(nc, in_maps, core_ids=list(range(8)))
    out = np.stack([r["out"] for r in res.results], 0)
    return out.astype(np.float32)



# version 11 (final)
# speedup vs baseline: 1.1758x; 1.1758x over previous
import math
from contextlib import ExitStack

import numpy as np
import ml_dtypes

import concourse.bass as bass
import concourse.mybir as mybir
from concourse.bass_utils import run_bass_kernel_spmd

F32 = mybir.dt.float32
BF16 = mybir.dt.bfloat16
AF = mybir.ActivationFunctionType
ALU = mybir.AluOpType
AX = mybir.AxisListType
BFNP = ml_dtypes.bfloat16

S = 4096
D = 1024
NT = S // 128
NG = S // 512
ALPHA = 1.0 / math.sqrt(16.0 + 512.0 + 512.0)
HID = 2730
HIDP = 2816
NHC = HIDP // 128
F1P = 2 * HIDP
QSC = 8.0
NEG = -1e30

ENGS = ("tensor", "vector", "scalar", "gpsimd", "sync")
AUTO_DRAIN = False
DRAIN_SECTIONS = ()
WO2_SYNC_CAST = False

from contextlib import contextmanager
PROG = {"tensor": "pe", "vector": "ve", "scalar": "se", "gpsimd": "ge"}
ARENA_COLS = 90624


def _host_consts():
    c = {}
    c["ident_bf"] = np.eye(128, dtype=BFNP)
    c["ident_f32"] = np.eye(128, dtype=np.float32)
    pos = np.arange(S, dtype=np.float64)[:, None]
    ids = np.arange(D // 2, dtype=np.float64)
    theta = 10000.0 ** (-2.0 * ids) / (D // 2)
    r = pos * theta[None, :]
    c["mla_cos"] = np.cos(r).astype(np.float32)
    c["mla_sin"] = np.sin(r).astype(np.float32)
    half = np.arange(0, 64, 2, dtype=np.float64)
    inv_freq = 1.0 / 10000.0 ** (half / 64.0)
    t = np.arange(S, dtype=np.float64)
    fr = t[:, None] * inv_freq[None, :]
    fr = np.concatenate([fr, fr], -1)
    base = (half + 0.4 * 64.0) / (1.4 * 64.0)
    power = (t - S // 2) / 256.0
    sc = base[None, :] ** power[:, None]
    sc = np.concatenate([sc, sc], -1)
    cf, sf = np.cos(fr), np.sin(fr)
    tile8 = lambda a: np.tile(a.astype(np.float32), (1, 8))
    c["xp_cq"] = tile8(cf * sc)
    c["xp_sq"] = tile8(sf * sc)
    c["xp_ck"] = tile8(cf / sc)
    c["xp_sk"] = tile8(sf / sc)
    p = np.arange(128)[:, None]
    cc = np.arange(640)[None, :]
    keep = (p <= cc) & (cc <= p + 512)
    c["mask640"] = np.where(keep, 0.0, NEG).astype(BFNP)
    return c


_CONSTS = None


def _consts():
    global _CONSTS
    if _CONSTS is None:
        _CONSTS = _host_consts()
    return _CONSTS


def _prep_shared(inp):
    d = {}
    d["wdq"] = inp["L0_attn_Wd_q_w"]
    d["wuq"] = inp["L0_attn_Wu_q_w"]
    d["wqr"] = inp["L0_attn_Wq_r_w"]
    d["wkr"] = inp["L0_attn_Wk_r_w"]
    d["wdkv"] = inp["L0_attn_Wd_kv_w"]
    d["wuk"] = inp["L0_attn_Wu_k_w"]
    d["wuv"] = inp["L0_attn_Wu_v_w"]
    d["wo"] = inp["L0_attn_Wo_w"]
    chunkcol = lambda b: np.ascontiguousarray(np.asarray(b).reshape(-1, 128).T)
    bmap = lambda b: np.ascontiguousarray(np.broadcast_to(np.asarray(b), (128, b.shape[-1])))
    d["bdq_c"] = chunkcol(inp["L0_attn_Wd_q_b"])
    d["bdkv_c"] = chunkcol(inp["L0_attn_Wd_kv_b"])
    d["buq_m"] = bmap(inp["L0_attn_Wu_q_b"])
    d["bqr_m"] = bmap(inp["L0_attn_Wq_r_b"])
    d["bkr_m"] = bmap(inp["L0_attn_Wk_r_b"])
    d["buk_m"] = bmap(inp["L0_attn_Wu_k_b"])
    d["buv_m"] = bmap(inp["L0_attn_Wu_v_b"])
    d["wob_m"] = bmap(inp["L0_attn_Wo_b"])
    d["wqkv2"] = inp["L0_ffn_wqkv"]
    d["wo2"] = inp["L0_ffn_wo"]
    w1 = np.asarray(inp["L0_ffn_w1"])
    w1p = np.zeros((2, D, F1P), np.float32)
    w1p[:, :, :HID] = w1[:, :, :HID]
    w1p[:, :, HIDP:HIDP + HID] = w1[:, :, HID:]
    d["w1p"] = w1p
    w2 = np.asarray(inp["L0_ffn_w2"])
    w2p = np.zeros((2, HIDP, D), np.float32)
    w2p[:, :HID, :] = w2
    d["w2p"] = w2p
    b1 = np.asarray(inp["L0_ffn_b1"])
    b1p = np.zeros((2, F1P), np.float32)
    b1p[:, :HID] = b1[:, :HID]
    b1p[:, HIDP:HIDP + HID] = b1[:, HID:]
    d["b1c"] = np.ascontiguousarray(
        np.stack([b1p[l].reshape(-1, 128).T for l in range(2)], 0))
    d["b2m"] = np.ascontiguousarray(
        np.stack([np.broadcast_to(np.asarray(inp["L0_ffn_b2"][l]), (128, D)) for l in range(2)], 0))
    d["lng_c"] = np.ascontiguousarray(
        np.stack([np.asarray(inp["L0_ffn_ln_g"][l]).reshape(8, 128).T for l in range(2)], 0))
    d["lnb_c"] = np.ascontiguousarray(
        np.stack([np.asarray(inp["L0_ffn_ln_b"][l]).reshape(8, 128).T for l in range(2)], 0))
    d["qsm"] = np.ascontiguousarray(
        np.stack([np.broadcast_to(np.tile(np.asarray(inp["L0_ffn_q_scale"][l]), 8), (128, 512)) for l in range(2)], 0))
    d["ksm"] = np.ascontiguousarray(
        np.stack([np.broadcast_to(np.tile(np.asarray(inp["L0_ffn_k_scale"][l]), 8), (128, 512)) for l in range(2)], 0))
    for k, v in _consts().items():
        d[k] = v
    return d


class MK:
    def __init__(self, dbg=()):
        self.dbg = set(dbg)
        self.nc = bass.Bass("TRN2", target_bir_lowering=False)
        self.es = ExitStack()
        self.q = {e: [] for e in ENGS}
        self.semh, self.semc = {}, {}
        for s in ("pe", "ve", "se", "ge", "bar"):
            self.semh[s] = self.es.enter_context(self.nc.semaphore(s))
            self.semc[s] = 0
        self.lastw = {}
        self.dirty = set()
        self.unsig = {e: 0 for e in ENGS}
        self.drain_scope = 0
        self.drain_sections = set(DRAIN_SECTIONS)

    def _app(self, eng, fn):
        self.q[eng].append(fn)

    def op(self, eng, fn, sig=False):
        ev = None
        if sig:
            s = PROG[eng]
            self.semc[s] += 1
            tgt = self.semc[s]
            h = self.semh[s]
            self._app(eng, lambda e, fn=fn, h=h: fn(e).then_inc(h, 1))
            ev = (s, tgt)
            self.unsig[eng] = 0
        else:
            self._app(eng, fn)
            self.unsig[eng] += 1
        if (AUTO_DRAIN or self.drain_scope) and eng in ("vector", "scalar", "gpsimd"):
            self._app(eng, lambda e: e.drain(fusable=True))
        return ev

    def dacc(self, out, in_, slot):
        if slot not in self.semh:
            self.semh[slot] = self.es.enter_context(self.nc.semaphore(slot))
            self.semc[slot] = 0
        self.semc[slot] += 16
        tgt = self.semc[slot]
        h = self.semh[slot]
        self._app("gpsimd", lambda e, out=out, in_=in_, h=h: e.dma_start(
            out=out, in_=in_, accum_op=ALU.add).then_inc(h, 16))
        self.dirty.add(slot)
        return (slot, tgt)

    def dma(self, eng, out, in_, slot):
        if slot not in self.semh:
            self.semh[slot] = self.es.enter_context(self.nc.semaphore(slot))
            self.semc[slot] = 0
        self.semc[slot] += 16
        tgt = self.semc[slot]
        h = self.semh[slot]
        self._app(eng, lambda e, out=out, in_=in_, h=h: e.dma_start(out=out, in_=in_).then_inc(h, 16))
        self.dirty.add(slot)
        return (slot, tgt)

    def wait(self, eng, *evs):
        for ev in evs:
            if ev is None:
                continue
            s, tgt = ev
            if PROG.get(eng) == s:
                continue
            key = (eng, s)
            if self.lastw.get(key, 0) >= tgt:
                continue
            self.lastw[key] = tgt
            h = self.semh[s]
            self._app(eng, lambda e, h=h, tgt=tgt: e.wait_ge(h, tgt))

    PROD = {"pe": "tensor", "ve": "vector", "se": "scalar", "ge": "gpsimd"}

    def fence(self, eng, on):
        prod = self.PROD.get(on)
        if prod is not None and self.unsig[prod] > 0:
            self.milestone(prod)
        self.wait(eng, (on, self.semc[on]))

    def _ds(self, name):
        self.drain_scope = 1 if (name is not None and name in self.drain_sections) else 0

    def milestone(self, eng):
        if eng == "vector":
            return self.op(eng, lambda e: e.tensor_copy(
                self.stat[:, 90:91], self.stat[:, 91:92]), sig=True)
        if eng == "scalar":
            return self.op(eng, lambda e: e.copy(
                self.stat[0:1, 92:93], self.stat[0:1, 93:94]), sig=True)
        s = PROG[eng]
        self.semc[s] += 1
        tgt = self.semc[s]
        h = self.semh[s]
        self._app(eng, lambda e, h=h: e.drain(fusable=True).then_inc(h, 1))
        self.unsig[eng] = 0
        return (s, tgt)

    def barrier(self):
        for s in sorted(self.dirty):
            self.wait("sync", (s, self.semc[s]))
        self.dirty.clear()
        for e in ENGS:
            h = self.semh["bar"]
            self.semc["bar"] += 1
            if e in ("gpsimd", "sync"):
                self._app(e, lambda en, h=h: en.nop().then_inc(h, 1))
            else:
                self._app(e, lambda en, h=h: en.drain(fusable=True).then_inc(h, 1))
        tot = self.semc["bar"]
        for e in ENGS:
            self._app(e, lambda en, h=self.semh["bar"], tot=tot: en.wait_ge(h, tot))
        for e in ENGS:
            for s in self.semc:
                if s != "bar":
                    self.lastw[(e, s)] = self.semc[s]
            self.lastw[(e, "bar")] = tot
        self._tbevs = [None, None]
        self._tbk = 0

    def tb_get(self):
        k = self._tbk
        self._tbk ^= 1
        return self.TBs[k], k

    def mm(self, out, lh, rh, start=True, stop=True, tp=None, sig=False):
        return self.op("tensor", lambda e, out=out, lh=lh, rh=rh, start=start, stop=stop, tp=tp:
                       e.matmul(out, lh, rh, start=start, stop=stop, tile_position=tp,
                                skip_group_check=True), sig=sig)

    def tr(self, out, in_, ident, sig=False):
        return self.op("tensor", lambda e, out=out, in_=in_, ident=ident:
                       e.transpose(out, in_, ident), sig=sig)

    def _rt(self, eng, ev):
        self.swait(eng, ev)
        return ev

    def act(self, out, in_, f, bias=0.0, scale=1.0, accum=None, sig=False, rt=False):
        ev = self.op("scalar", lambda e, out=out, in_=in_, f=f, bias=bias, scale=scale, accum=accum:
                     e.activation(out, in_, f, bias=bias, scale=scale, accum_out=accum),
                     sig=sig or rt)
        return self._rt("scalar", ev) if rt else ev

    def tt(self, eng, out, a, b, alu, sig=False, rt=False):
        ev = self.op(eng, lambda e, out=out, a=a, b=b, alu=alu:
                     e.tensor_tensor(out, a, b, alu), sig=sig or rt)
        return self._rt(eng, ev) if rt else ev

    def ts(self, eng, out, a, s1, s2=None, op0=ALU.mult, op1=ALU.bypass, sig=False, rt=False):
        ev = self.op(eng, lambda e, out=out, a=a, s1=s1, s2=s2, op0=op0, op1=op1:
                     e.tensor_scalar(out, a, s1, s2, op0, op1), sig=sig or rt)
        return self._rt(eng, ev) if rt else ev

    def cp(self, eng, out, in_, sig=False, rt=False):
        ev = self.op(eng, lambda e, out=out, in_=in_: e.tensor_copy(out, in_), sig=sig or rt)
        return self._rt(eng, ev) if rt else ev

    def swait(self, eng, ev):
        s, tgt = ev
        h = self.semh[s]
        self._app(eng, lambda e, h=h, tgt=tgt: e.wait_ge(h, tgt))
        self.lastw[(eng, s)] = max(self.lastw.get((eng, s), 0), tgt)

    def red(self, eng, out, in_, alu, sig=False):
        ev = self.op(eng, lambda e, out=out, in_=in_, alu=alu:
                     e.tensor_reduce(out, in_, AX.X, alu), sig=True)
        self.swait(eng, ev)
        return ev

    def recip(self, out, in_, sig=False):
        ev = self.op("vector", lambda e, out=out, in_=in_: e.reciprocal(out, in_),
                     sig=True)
        self.swait("vector", ev)
        return ev

    def memset(self, eng, ap, v, sig=False, rt=False):
        ev = self.op(eng, lambda e, ap=ap, v=v: e.memset(ap, v), sig=sig or rt)
        return self._rt(eng, ev) if rt else ev

    def build(self, phases=("p0", "p1", "p2", "f0", "f1")):
        nc = self.nc
        es = self.es
        dbg = self.dbg

        def din(name, shape, dt=F32):
            return nc.dram_tensor(name, list(shape), dt, kind="ExternalInput").ap()

        def dscratch(name, shape, dt=F32):
            kind = "ExternalOutput" if name in dbg else "Internal"
            return nc.dram_tensor(name, list(shape), dt, kind=kind).ap()

        self.Xd = din("x", [S, D])
        wdq = din("wdq", [D, 512]); wuq = din("wuq", [512, D]); wqr = din("wqr", [512, D])
        wkr = din("wkr", [D, D]); wdkv = din("wdkv", [D, 512]); wuk = din("wuk", [512, D])
        wuv = din("wuv", [512, D]); wo = din("wo", [D, D])
        bdqc_d = din("bdq_c", [128, 4]); bdkvc_d = din("bdkv_c", [128, 4])
        buqm_d = din("buq_m", [128, D]); bqrm_d = din("bqr_m", [128, D])
        bkrm_d = din("bkr_m", [128, D]); bukm_d = din("buk_m", [128, D])
        buvm_d = din("buv_m", [128, D]); self.wobm_d = din("wob_m", [128, D])
        wqkv2 = din("wqkv2", [2, D, 1536]); wo2 = din("wo2", [2, 512, D])
        w1p = din("w1p", [2, D, F1P]); w2p = din("w2p", [2, HIDP, D])
        b1c_d = din("b1c", [2, 128, F1P // 128]); b2m_d = din("b2m", [2, 128, D])
        lng_d = din("lng_c", [2, 128, 8]); lnb_d = din("lnb_c", [2, 128, 8])
        qsm_d = din("qsm", [2, 128, 512]); ksm_d = din("ksm", [2, 128, 512])
        identbf_d = din("ident_bf", [128, 128], BF16)
        identf_d = din("ident_f32", [128, 128])
        mlacos_d = din("mla_cos", [S, 512]); mlasin_d = din("mla_sin", [S, 512])
        xpcq_d = din("xp_cq", [S, 512]); xpsq_d = din("xp_sq", [S, 512])
        xpck_d = din("xp_ck", [S, 512]); xpsk_d = din("xp_sk", [S, 512])
        mask_d = din("mask640", [128, 640], BF16)

        self.XN = dscratch("XN", [S, D])
        self.XRES = dscratch("XRES", [S, D])
        self.AOUT = dscratch("AOUT", [S, D])
        self.KH = dscratch("KH", [128, 8, S], BF16)
        self.QH = dscratch("QH", [128, 8, S], BF16)
        self.VH = dscratch("VH", [128, NT, 1040], BF16)
        self.OUT = nc.dram_tensor("out", [S, D], F32, kind="ExternalOutput").ap()

        sb = lambda n, sh, dt=F32: es.enter_context(nc.sbuf_tensor(n, sh, dt))
        ps = lambda n, sh, dt=F32: es.enter_context(nc.psum_tensor(n, sh, dt))

        self.identb = sb("identb", [128, 128], BF16)
        self.identf = sb("identf", [128, 128])
        self.ones_f = sb("ones_f", [128, 128])
        self.ones_b = sb("ones_b", [128, 16], BF16)
        self.mask = sb("maskc", [128, 640], BF16)
        self.bdqc = sb("bdqc", [128, 4]); self.bdkvc = sb("bdkvc", [128, 4])
        self.b2m = sb("b2m_s", [128, D])
        self.b1c = sb("b1c_s", [128, F1P // 128])
        self.lng = sb("lng", [128, 8]); self.lnb = sb("lnb", [128, 8])
        self.qsm = sb("qsm_s", [128, 512]); self.ksm = sb("ksm_s", [128, 512])
        self.stat = sb("stat", [128, 96])
        self.epsb = sb("epsb", [128, 1])
        self.scr64 = sb("scr64", [128, 512])
        self.tmph1 = sb("tmph1", [128, 1024], BF16)
        self.xsb = [sb("xsb0", [128, D]), sb("xsb1", [128, D])]
        self.xnbb = [sb("xnbb0", [128, D], BF16), sb("xnbb1", [128, D], BF16)]
        self.AR = sb("AR", [128, ARENA_COLS], BF16)

        self.sp = [ps("sp0", [128, 1024]), ps("sp1", [128, 1024])]
        self.o0 = ps("o0", [128, 512]); self.o1 = ps("o1", [128, 512])
        self.u0 = ps("u0", [128, 512])
        self.TB = ps("TBp", [128, 1024], BF16)
        self.TBs = [self.TB[:], self.u0[:].bitcast(BF16)]
        self._tbevs = [None, None]
        self._tbk = 0

        ar = self.AR

        def abf(off, n):
            return ar[:, off:off + n]

        def af32(off, n):
            return ar[:, off:off + 2 * n].bitcast(F32)

        self.abf, self.af32 = abf, af32
        self.tsq = af32(34880, 1024)
        self.buqm = abf(36928, 1024)
        self.bqrm = abf(37952, 1024)
        self.bkrm = abf(38976, 1024)
        self.bukm = abf(40000, 1024)
        self.buvm = abf(41024, 1024)

        ev_c = []
        ev_c.append(self.dma("sync", self.identb[:], identbf_d[:, :], "c0"))
        ev_c.append(self.dma("sync", self.identf[:], identf_d[:, :], "c1"))
        ev_c.append(self.dma("sync", self.mask[:], mask_d[:, :], "c2"))
        ev_c.append(self.dma("sync", self.bdqc[:], bdqc_d[:, :], "c3"))
        ev_c.append(self.dma("sync", self.bdkvc[:], bdkvc_d[:, :], "c4"))
        for ci, (t, s_) in enumerate(((self.buqm, buqm_d), (self.bqrm, bqrm_d),
                                      (self.bkrm, bkrm_d), (self.bukm, bukm_d),
                                      (self.buvm, buvm_d))):
            ev_c.append(self.dma("gpsimd", t, s_[:, :], f"c{5 + ci}"))
        self.memset("gpsimd", self.ones_f[:], 1.0)
        self.memset("gpsimd", self.ones_b[:], 1.0)
        self.memset("gpsimd", self.epsb[:], 1e-5)
        self.memset("gpsimd", self.stat[:, 8:24], 0.0)
        self.memset("gpsimd", self.stat[:, 90:94], 0.0, rt=True)
        self.barrier()

        W_OFF = 52000
        self.wdkv_sb = abf(W_OFF, 4096)
        self.wuk_sb = abf(W_OFF + 4096, 4096)
        self.wuv_sb = abf(W_OFF + 8192, 4096)
        self.wkr_sb = abf(W_OFF + 12288, 8192)
        self.wdq_sb = abf(W_OFF, 4096)
        self.wuq_sb = abf(W_OFF + 4096, 4096)
        self.wqr_sb = abf(W_OFF + 8192, 4096)

        if "p0" in phases or "p1" in phases:
            self._p0_p1(wdq, wuq, wqr, wkr, wdkv, wuk, wuv,
                        mlacos_d, mlasin_d)
        if "p2" in phases:
            self._p2(wo)
        if "f0" in phases:
            self._ffn(0, wqkv2, wo2, w1p, w2p, b1c_d, b2m_d, lng_d, lnb_d,
                      qsm_d, ksm_d, xpcq_d, xpsq_d, xpck_d, xpsk_d)
        if "f1" in phases:
            self._ffn(1, wqkv2, wo2, w1p, w2p, b1c_d, b2m_d, lng_d, lnb_d,
                      qsm_d, ksm_d, xpcq_d, xpsq_d, xpck_d, xpsk_d)
        if "p5" in phases:
            self._p5()
        self.barrier()

        block = es.enter_context(nc.Block())
        for name, dec in (("tensor", block.tensor), ("vector", block.vector),
                          ("scalar", block.scalar), ("gpsimd", block.gpsimd),
                          ("sync", block.sync)):
            def body(e, name=name):
                for fn in self.q[name]:
                    fn(e)
            dec(body)
        es.close()
        return nc

    def _tm2fm(self, src_f32, par, dst3, ttl):
        ev_bb = self.cp("vector", self.xnbb[par][:], src_f32, sig=True)
        TBk, kk = self.tb_get()
        self.wait("tensor", ev_bb, self._tbevs[kk])
        ev_tr = None
        for c in range(8):
            ev_tr = self.tr(TBk[:, 128 * c:128 * c + 128],
                            self.xnbb[par][:, 128 * c:128 * c + 128],
                            self.identb[:], sig=(c == 7))
        self.wait("vector", ev_tr)
        src3 = TBk.rearrange("p (c t) -> p c t", t=128)
        ev = self.cp("vector", dst3[:, :, 128 * ttl:128 * ttl + 128], src3, sig=True)
        self._tbevs[kk] = ev
        return ev

    def _p0_p1(self, wdq, wuq, wqr, wkr, wdkv, wuk, wuv, mlacos_d, mlasin_d):
        abf, af32 = self.abf, self.af32
        st = self.stat
        xng = [abf(0, 4096), abf(4096, 4096)]
        ctkv = abf(8192, 2048)
        Kstage = abf(10240, 4096)
        Vstage = [abf(14336, 1056), abf(15392, 1056)]
        Ktm = af32(16448, 1024)
        Rtm = af32(18496, 1024)
        Ktb2 = [abf(20544, 1024), abf(42048, 1024)]
        ksq = af32(21568, 1024)
        t1 = af32(23616, 512); t2 = af32(24640, 512); t3 = af32(25664, 512)
        ctab = [af32(26688, 512), af32(27712, 512)]
        stab = [af32(28736, 512), af32(29760, 512)]
        xnb = [af32(30784, 1024), af32(32832, 1024)]

        ev_w = []
        for wi, (wsb, wd) in enumerate(((self.wdkv_sb, wdkv), (self.wuk_sb, wuk),
                                        (self.wuv_sb, wuv), (self.wkr_sb, wkr))):
            wsrc = wd.rearrange("(c p) f -> p c f", p=128)
            ev_w.append(self.dma("gpsimd", wsb, wsrc, f"w{wi}"))

        ev_vst = [None, None]
        ev_xnst = [None, None]
        gate_xs = [None, None]
        gate_tab = [None, None]
        ev_kdma = None
        pb = [None] * 4
        pbo = [None, None]
        self._kstev = None
        pend_k = [None]
        evkd = [None]

        def flush_k():
            if pend_k[0] is None:
                return
            ttl_p, kp, evb = pend_k[0]
            pend_k[0] = None
            TBk, kk = self.tb_get()
            self.wait("tensor", evb, self._tbevs[kk])
            ev_tr = None
            for c in range(8):
                ev_tr = self.tr(TBk[:, 128 * c:128 * c + 128],
                                Ktb2[kp][:, 128 * c:128 * c + 128],
                                self.identb[:], sig=(c == 7))
            self.wait("vector", ev_tr, evkd[0])
            kd3 = Kstage.rearrange("p (c t) -> p c t", t=512)
            self._kstev = self.cp(
                "vector", kd3[:, :, 128 * ttl_p:128 * ttl_p + 128],
                TBk.rearrange("p (c t) -> p c t", t=128), sig=True)
            self._tbevs[kk] = self._kstev
        ev_ctkv_e = [None] * 4
        for par in range(2):
            v66 = Vstage[par].rearrange("p (h c) -> p h c", c=66)
            ev_vones = self.memset("gpsimd", v66[:, :, 64:65], 1.0, sig=True)

        for g in range(NG):
            gpar = g % 2
            for ttl in range(4):
                tt = 4 * g + ttl
                par = tt % 2
                self.wait("sync", *(gate_xs[par] or ()))
                ev_x = self.dma("sync", self.xsb[par][:],
                                self.Xd[128 * tt:128 * tt + 128, :], f"xs{par}")
                self.wait("vector", ev_x)
                self.wait("vector", ev_x)
                self.fence("vector", "ve")
                ev_sq = self.tt("vector", self.tsq[:], self.xsb[par][:],
                                self.xsb[par][:], ALU.mult, sig=True, rt=True)
                self.red("vector", st[:, 0:1], self.xsb[par][:], ALU.add)
                self.wait("vector", ev_sq)
                self.red("vector", st[:, 1:2], self.tsq[:], ALU.add)
                self.ts("vector", st[:, 2:3], st[:, 0:1], 1.0 / D, rt=True)
                self.ts("vector", st[:, 3:4], st[:, 1:2], 1.0 / D, rt=True)
                self.tt("vector", st[:, 4:5], st[:, 2:3], st[:, 2:3], ALU.mult, rt=True)
                ev_var = self.tt("vector", st[:, 3:4], st[:, 3:4], st[:, 4:5],
                                 ALU.subtract, sig=True, rt=True)
                self.wait("scalar", ev_var)
                self.act(st[:, 5:6], st[:, 3:4], AF.Ln, bias=self.epsb[:, 0:1], rt=True)
                ev_rstd = self.act(st[:, 6:7], st[:, 5:6], AF.Exp, scale=-0.5, sig=True)
                self.wait("vector", ev_rstd, ev_xnst[par])
                ev_xn = self.ts("vector", xnb[par], self.xsb[par][:], st[:, 2:3],
                                st[:, 6:7], ALU.subtract, ALU.mult, sig=True)
                self.swait("vector", ev_xn)
                gate_xs[par] = (ev_xn, ev_sq)
                self.wait("sync", ev_xn)
                ev_xnst[par] = self.dma("sync", self.XN[128 * tt:128 * tt + 128, :],
                                        xnb[par], f"xn{par}")
                dst3 = xng[gpar].rearrange("p (c t) -> p c t", t=512)
                self.fence("vector", "pe")
                xng_ev = self._tm2fm(xnb[par], par, dst3, ttl)

            self.wait("tensor", xng_ev, *ev_w)
            slots = [self.sp[0][:, 0:512], self.sp[0][:, 512:1024],
                     self.sp[1][:, 0:512], self.sp[1][:, 512:1024]]
            ev_ctkv = [None] * 4
            for fc in range(4):
                self.wait("tensor", pb[fc])
                for dc in range(8):
                    ev_ctkv[fc] = self.mm(
                        slots[fc],
                        self.wdkv_sb[:, 512 * dc + 128 * fc:512 * dc + 128 * fc + 128],
                        xng[gpar][:, 512 * dc:512 * dc + 512],
                        start=(dc == 0), stop=(dc == 7), sig=(dc == 7))
            ctkv_b = ctkv
            for fc in range(4):
                self.wait("scalar", ev_ctkv[fc])
                ev_ctkv_e[fc] = self.act(ctkv_b[:, 512 * fc:512 * fc + 512], slots[fc],
                                         AF.Identity, bias=self.bdkvc[:, fc:fc + 1], sig=True)
                pb[fc] = ev_ctkv_e[fc]
            for ttl in range(4):
                tt = 4 * g + ttl
                par = tt % 2
                self.wait("sync", gate_tab[par])
                ev_ct = self.dma("sync", ctab[par],
                                 mlacos_d[128 * tt:128 * tt + 128, :], f"tca{par}")
                ev_st2 = self.dma("sync", stab[par],
                                  mlasin_d[128 * tt:128 * tt + 128, :], f"tcb{par}")
                self.wait("tensor", ev_ctkv_e[3], pbo[0], pbo[1])
                ev_ktc = None
                for fh, bank in ((0, self.o0), (1, self.o1)):
                    for kc in range(4):
                        ev_ktc = self.mm(
                            bank[:, :],
                            ctkv_b[:, 512 * kc + 128 * ttl:512 * kc + 128 * ttl + 128],
                            self.wuk_sb[:, 1024 * kc + 512 * fh:1024 * kc + 512 * fh + 512],
                            start=(kc == 0), stop=(kc == 3), sig=(fh == 1 and kc == 3))
                self.wait("tensor", pb[0], pb[1])
                ev_ktr = None
                for fh in range(2):
                    for dc in range(8):
                        ev_ktr = self.mm(
                            self.sp[0][:, 512 * fh:512 * fh + 512],
                            xng[gpar][:, 512 * dc + 128 * ttl:512 * dc + 128 * ttl + 128],
                            self.wkr_sb[:, 1024 * dc + 512 * fh:1024 * dc + 512 * fh + 512],
                            start=(dc == 0), stop=(dc == 7), sig=(fh == 1 and dc == 7))
                self.wait("tensor", pb[2], pb[3])
                ev_v = None
                for fh in range(2):
                    for kc in range(4):
                        ev_v = self.mm(
                            self.sp[1][:, 512 * fh:512 * fh + 512],
                            ctkv_b[:, 512 * kc + 128 * ttl:512 * kc + 128 * ttl + 128],
                            self.wuv_sb[:, 1024 * kc + 512 * fh:1024 * kc + 512 * fh + 512],
                            start=(kc == 0), stop=(kc == 3), sig=(fh == 1 and kc == 3))
                flush_k()
                self.wait("vector", ev_ktc)
                self.tt("vector", Ktm[:, 0:512], self.o0[:, :], self.bukm[:, 0:512], ALU.add)
                evko = self.tt("vector", Ktm[:, 512:1024], self.o1[:, :],
                               self.bukm[:, 512:1024], ALU.add, sig=True)
                pbo[0] = pbo[1] = evko
                self.wait("vector", ev_ktr)
                self.tt("vector", Rtm[:, 0:512], self.sp[0][:, 0:512], self.bkrm[:, 0:512], ALU.add)
                ev_rtm = self.tt("vector", Rtm[:, 512:1024], self.sp[0][:, 512:1024],
                                 self.bkrm[:, 512:1024], ALU.add, sig=True)
                pb[0] = pb[1] = ev_rtm
                self.wait("vector", ev_v, ev_vst[par], ev_vones)
                v3 = Vstage[par].rearrange("p (h c) -> p h c", c=66)
                self.tt("vector", v3[:, 0:8, 0:64], self.sp[1][:, 0:512],
                        self.buvm[:, 0:512], ALU.add)
                ev_vs = self.tt("vector", v3[:, 8:16, 0:64], self.sp[1][:, 512:1024],
                                self.buvm[:, 512:1024], ALU.add, sig=True)
                pb[2] = pb[3] = ev_vs
                Re = Rtm.rearrange("p (n two) -> p n two", two=2)[:, :, 0:1]
                Ro = Rtm.rearrange("p (n two) -> p n two", two=2)[:, :, 1:2]
                Ke = Ktm.rearrange("p (n two) -> p n two", two=2)[:, :, 0:1]
                Ko = Ktm.rearrange("p (n two) -> p n two", two=2)[:, :, 1:2]
                self.wait("vector", ev_ct, ev_st2, ev_rtm)
                self.fence("vector", "ve")
                self.tt("vector", t1, Re, ctab[par], ALU.mult)
                ev_g1 = self.tt("vector", t2, Ro, stab[par], ALU.mult, sig=True)
                ev_g2 = self.tt("vector", t3, Re, stab[par], ALU.mult, sig=True)
                self.wait("vector", ev_g1)
                selft1 = self.tt("vector", t1, t1, t2, ALU.subtract, sig=True)
                ev_ke = self.tt("vector", Ke, Ke, t1, ALU.add, sig=True)
                self.wait("vector", selft1)
                ev_g3 = self.tt("vector", t2, Ro, ctab[par], ALU.mult, sig=True)
                self.wait("vector", ev_g2, ev_g3)
                gate_tab[par] = ev_g3
                self.tt("vector", t2, t2, t3, ALU.add)
                ev_ko = self.tt("vector", Ko, Ko, t2, ALU.add, sig=True)
                self.wait("vector", ev_ke, ev_ko)
                ev_ksq = self.tt("vector", ksq, Ktm, Ktm, ALU.mult, sig=True)
                self.wait("vector", ev_ksq)
                self.red("vector", st[:, 24:40],
                         ksq.rearrange("p (h d) -> p h d", d=64), ALU.add)
                self.tt("vector", st[:, 8:24], st[:, 8:24], st[:, 24:40], ALU.max, rt=True)
                ev_ktb = self.cp("vector", Ktb2[ttl % 2], Ktm, sig=True)
                pend_k[0] = (ttl, ttl % 2, ev_ktb)
                self.wait("sync", ev_vs)
                vv3 = Vstage[par].rearrange("p (h c) -> p h c", c=66)
                ev_vst[par] = self.dma("sync", self.VH[:, tt, :],
                                       vv3[:, :, 0:65], f"vs{par}")
            flush_k()
            self.wait("sync", self._kstev)
            ev_kdma = self.dma("sync", self.KH[:, :, 512 * g:512 * g + 512],
                               Kstage.rearrange("p (c u t) -> p c u t", c=8, u=4),
                               "kgrp")
            evkd[0] = ev_kdma
        ev_r0 = self.red("vector", st[:, 49:50], st[:, 8:24], ALU.max)
        self.wait("tensor", ev_r0, pbo[0], self._tbevs[1])
        ev_t = self.tr(self.u0[0:1, 0:128], st[:, 49:50], self.identf[:], sig=True)
        self.wait("vector", ev_t)
        self.red("vector", st[0:1, 48:49], self.u0[0:1, 0:128], ALU.max)

        self.barrier()

        ev_w = []
        for wi, (wsb, wd) in enumerate(((self.wdq_sb, wdq), (self.wuq_sb, wuq),
                                        (self.wqr_sb, wqr))):
            wsrc = wd.rearrange("(c p) f -> p c f", p=128)
            ev_w.append(self.dma("gpsimd", wsb, wsrc, f"w{wi}"))
        ctq_b = ctkv
        Qtm, qsq = Ktm, ksq
        gate_xs = [None, None]
        gate_tab = [None, None]
        ev_qdma = None
        pend_k[0] = None
        evkd[0] = None
        self._kstev = None
        pb = [None] * 4
        pbo = [None, None]
        for g in range(NG):
            gpar = g % 2
            self.memset("vector", st[:, 50:66], 0.0)
            for ttl in range(4):
                tt = 4 * g + ttl
                par = tt % 2
                self.wait("sync", gate_xs[par])
                ev_x = self.dma("sync", self.xsb[par][:],
                                self.XN[128 * tt:128 * tt + 128, :], f"xs{par}")
                self.wait("vector", ev_x)
                self.fence("vector", "pe")
                dst3 = xng[gpar].rearrange("p (c t) -> p c t", t=512)
                gate_xs[par] = self._tm2fm(self.xsb[par][:], par, dst3, ttl)
            self.wait("tensor", gate_xs[0], gate_xs[1], *ev_w)
            ev_ctq = [None] * 4
            slots = [self.sp[0][:, 0:512], self.sp[0][:, 512:1024],
                     self.sp[1][:, 0:512], self.sp[1][:, 512:1024]]
            for fc in range(4):
                self.wait("tensor", pb[fc])
                for dc in range(8):
                    ev_ctq[fc] = self.mm(
                        slots[fc],
                        self.wdq_sb[:, 512 * dc + 128 * fc:512 * dc + 128 * fc + 128],
                        xng[gpar][:, 512 * dc:512 * dc + 512],
                        start=(dc == 0), stop=(dc == 7), sig=(dc == 7))
            for fc in range(4):
                self.wait("scalar", ev_ctq[fc])
                pb[fc] = self.act(ctq_b[:, 512 * fc:512 * fc + 512], slots[fc],
                                  AF.Identity, bias=self.bdqc[:, fc:fc + 1], sig=True)
            for ttl in range(4):
                tt = 4 * g + ttl
                par = tt % 2
                self.wait("sync", gate_tab[par])
                ev_ct = self.dma("sync", ctab[par],
                                 mlacos_d[128 * tt:128 * tt + 128, :], f"tca{par}")
                ev_st2 = self.dma("sync", stab[par],
                                  mlasin_d[128 * tt:128 * tt + 128, :], f"tcb{par}")
                self.wait("tensor", pb[3], pbo[0], pbo[1])
                ev_qtc = None
                for fh, bank in ((0, self.o0), (1, self.o1)):
                    for kc in range(4):
                        ev_qtc = self.mm(
                            bank[:, :],
                            ctq_b[:, 512 * kc + 128 * ttl:512 * kc + 128 * ttl + 128],
                            self.wuq_sb[:, 1024 * kc + 512 * fh:1024 * kc + 512 * fh + 512],
                            start=(kc == 0), stop=(kc == 3), sig=(fh == 1 and kc == 3))
                self.wait("tensor", pb[0], pb[1])
                ev_qr = None
                for fh in range(2):
                    for kc in range(4):
                        ev_qr = self.mm(
                            self.sp[0][:, 512 * fh:512 * fh + 512],
                            ctq_b[:, 512 * kc + 128 * ttl:512 * kc + 128 * ttl + 128],
                            self.wqr_sb[:, 1024 * kc + 512 * fh:1024 * kc + 512 * fh + 512],
                            start=(kc == 0), stop=(kc == 3), sig=(fh == 1 and kc == 3))
                flush_k()
                self.wait("vector", ev_qtc)
                self.tt("vector", Qtm[:, 0:512], self.o0[:, :], self.buqm[:, 0:512], ALU.add)
                evqo = self.tt("vector", Qtm[:, 512:1024], self.o1[:, :],
                               self.buqm[:, 512:1024], ALU.add, sig=True)
                pbo[0] = pbo[1] = evqo
                self.wait("vector", ev_qr)
                self.tt("vector", Rtm[:, 0:512], self.sp[0][:, 0:512], self.bqrm[:, 0:512], ALU.add)
                ev_rtm = self.tt("vector", Rtm[:, 512:1024], self.sp[0][:, 512:1024],
                                 self.bqrm[:, 512:1024], ALU.add, sig=True)
                pb[0] = pb[1] = ev_rtm
                Re = Rtm.rearrange("p (n two) -> p n two", two=2)[:, :, 0:1]
                Ro = Rtm.rearrange("p (n two) -> p n two", two=2)[:, :, 1:2]
                Qe = Qtm.rearrange("p (n two) -> p n two", two=2)[:, :, 0:1]
                Qo = Qtm.rearrange("p (n two) -> p n two", two=2)[:, :, 1:2]
                self.wait("vector", ev_ct, ev_st2, ev_rtm)
                self.fence("vector", "ve")
                self.tt("vector", t1, Re, ctab[par], ALU.mult)
                ev_g1 = self.tt("vector", t2, Ro, stab[par], ALU.mult, sig=True)
                ev_g2 = self.tt("vector", t3, Re, stab[par], ALU.mult, sig=True)
                self.wait("vector", ev_g1)
                selft1 = self.tt("vector", t1, t1, t2, ALU.subtract, sig=True)
                self.tt("vector", Qe, Qe, t1, ALU.add)
                self.wait("vector", selft1)
                ev_g3 = self.tt("vector", t2, Ro, ctab[par], ALU.mult, sig=True)
                self.wait("vector", ev_g2, ev_g3)
                gate_tab[par] = ev_g3
                self.tt("vector", t2, t2, t3, ALU.add)
                ev_qo = self.tt("vector", Qo, Qo, t2, ALU.add, sig=True)
                self.wait("vector", ev_qo)
                ev_qsq = self.tt("vector", qsq, Qtm, Qtm, ALU.mult, sig=True)
                self.wait("vector", ev_qsq)
                self.red("vector", st[:, 24:40],
                         qsq.rearrange("p (h d) -> p h d", d=64), ALU.add)
                self.tt("vector", st[:, 50:66], st[:, 50:66], st[:, 24:40], ALU.max, rt=True)
                ev_qtb = self.cp("vector", Ktb2[ttl % 2], Qtm, sig=True)
                pend_k[0] = (ttl, ttl % 2, ev_qtb)
            flush_k()
            self.wait("sync", self._kstev)
            ev_qdma = self.dma("sync", self.QH[:, :, 512 * g:512 * g + 512],
                               Kstage.rearrange("p (c u t) -> p c u t", c=8, u=4),
                               "kgrp")
            evkd[0] = ev_qdma
            ev_r = self.red("vector", st[:, 49:50], st[:, 50:66], ALU.max, sig=True)
            self.wait("tensor", ev_r, self._tbevs[1])
            ev_t = self.tr(self.u0[0:1, 0:128], st[:, 49:50], self.identf[:], sig=True)
            self.wait("vector", ev_t)
            self.red("vector", st[0:1, 67:68], self.u0[0:1, 0:128], ALU.max)
            ev_p = self.tt("vector", st[0:1, 68:69], st[0:1, 67:68], st[0:1, 48:49],
                           ALU.mult, sig=True, rt=True)
            self.wait("scalar", ev_p)
            self.act(st[0:1, 69:70], st[0:1, 68:69], AF.Ln, rt=True)
            ev_sq = self.act(st[0:1, 70:71], st[0:1, 69:70], AF.Exp, scale=0.5, sig=True)
            self.wait("vector", ev_sq)
            ev_m = self.ts("vector", st[0:1, 71:72], st[0:1, 70:71], -ALPHA, sig=True)
            self.wait("tensor", ev_m)
            ev_b = self.mm(self.u0[:, 0:1], self.ones_f[0:1, 0:128],
                           st[0:1, 71:72], sig=True)
            self.wait("vector", ev_b)
            self._u0ev = self.cp("vector", st[:, 40 + g:41 + g], self.u0[:, 0:1],
                                 sig=True)
            self._tbevs[1] = self._u0ev
        self.barrier()

    def _p2(self, wo):
        abf, af32 = self.abf, self.af32
        st = self.stat
        kc_sb = [abf(0, 4096), abf(4096, 4096)]
        qc_sb = [abf(8192, 4096), abf(12288, 4096)]
        vp_sb = [abf(16384, 4160), abf(20544, 4160)]
        expS = [abf(24704, 1024), abf(25728, 1024)]
        O_fm = abf(26752, 32768)
        asb = [af32(59520, 1024), af32(61568, 1024)]
        wo_sb = abf(63616, 8192)
        wobm = af32(71808, 1024)
        rb = af32(73856, 512)

        ev_wo = self.dma("gpsimd", wo_sb, wo.rearrange("(c p) f -> p c f", p=128), "w0")
        ev_wobm = self.dma("sync", wobm, self.wobm_d[:, :], "cw")

        def load_hp(hp):
            par = hp % 2
            evk = self.dma("sync", kc_sb[par], self.KH[:, hp, :], f"kc{par}")
            evq = self.dma("sync", qc_sb[par], self.QH[:, hp, :], f"qc{par}")
            evv = self.dma("sync", vp_sb[par],
                           self.VH[:, :, 130 * hp:130 * hp + 130], f"vc{par}")
            return (evk, evq, evv)

        ev_oh = [None, None]
        evE_prev = [None, None]
        ofin = [None, None]
        pend = load_hp(0)
        for hp in range(8):
            par = hp % 2
            evk, evq, evv = pend
            if hp < 7:
                self.fence("sync", "pe")
                pend = load_hp(hp + 1)
            self.wait("tensor", evk, evq, evv)
            kc, qc, vp = kc_sb[par], qc_sb[par], vp_sb[par]
            for g in range(NG):
                self.wait("tensor", evE_prev[0], evE_prev[1], ofin[0], ofin[1])
                evS = [None] * 32
                evE = [None] * 32
                evPV = [None] * 32

                def pv(kt):
                    self.wait("tensor", evE[kt])
                    self.mm(self.o0[0:65, :], vp[:, 130 * kt:130 * kt + 65],
                            expS[kt % 2][:, 0:512],
                            start=(kt == 0), stop=(kt == 31))
                    evPV[kt] = self.mm(self.o1[0:65, :], vp[:, 130 * kt + 65:130 * kt + 130],
                                       expS[kt % 2][:, 512:1024],
                                       start=(kt == 0), stop=(kt == 31), sig=True)

                for kt in range(32):
                    if kt >= 2:
                        self.wait("tensor", evE[kt - 2])
                    self.mm(self.sp[kt % 2][:, 0:512],
                            kc[0:64, 128 * kt:128 * kt + 128],
                            qc[0:64, 512 * g:512 * g + 512], tp=(0, 0))
                    evS[kt] = self.mm(self.sp[kt % 2][:, 512:1024],
                                      kc[64:128, 128 * kt:128 * kt + 128],
                                      qc[64:128, 512 * g:512 * g + 512],
                                      tp=(64, 0), sig=True)
                    if kt >= 1:
                        pv(kt - 1)
                    self.wait("scalar", evS[kt])
                    if kt >= 2:
                        self.wait("scalar", evPV[kt - 2])
                    evE[kt] = self.act(expS[kt % 2][:], self.sp[kt % 2][:, 0:1024],
                                       AF.Exp, bias=st[:, 40 + g:41 + g],
                                       scale=ALPHA, sig=True)
                pv(31)
                evE_prev[0], evE_prev[1] = evE[30], evE[31]
                for hodd, bank in ((0, self.o0), (1, self.o1)):
                    self.wait("vector", evPV[31])
                    evr = self.recip(self.scr64[64:65, :], bank[64:65, :], sig=True)
                    self.wait("tensor", evr)
                    evb = self.mm(self.u0[0:64, :], self.ones_f[64:65, 0:64],
                                  self.scr64[64:65, :], tp=(64, 0), sig=True)
                    self.wait("vector", evb)
                    evrb = self.cp("vector", rb[0:64, :], self.u0[0:64, :], sig=True)
                    cols = slice(4096 * hp + 512 * g, 4096 * hp + 512 * g + 512)
                    if hodd == 0:
                        ofin[0] = self.tt("vector", O_fm[0:64, cols], bank[0:64, :],
                                          rb[0:64, :], ALU.mult, sig=True)
                    else:
                        hpar = g % 2
                        self.wait("vector", ev_oh[hpar])
                        evh = self.tt("vector",
                                      self.tmph1[0:64, 512 * hpar:512 * hpar + 512],
                                      bank[0:64, :], rb[0:64, :], ALU.mult, sig=True)
                        self.wait("gpsimd", evh)
                        ofin[1] = evh
                        ev_oh[hpar] = self.dma(
                            "gpsimd", O_fm[64:128, cols],
                            self.tmph1[0:64, 512 * hpar:512 * hpar + 512], f"oh{hpar}")
        self.wait("tensor", ev_oh[0], ev_oh[1])
        self.fence("tensor", "ve")
        self.fence("tensor", "se")
        self.wait("tensor", ev_wo)
        ev_ast = [None, None]
        ev_e = [None, None]
        for tt in range(NT):
            par = tt % 2
            psum = self.sp[par]
            if ev_e[par] is not None:
                self.wait("tensor", ev_e[par])
            ev = None
            for fh in range(2):
                for c in range(8):
                    ev = self.mm(psum[:, 512 * fh:512 * fh + 512],
                                 O_fm[:, 4096 * c + 128 * tt:4096 * c + 128 * tt + 128],
                                 wo_sb[:, 1024 * c + 512 * fh:1024 * c + 512 * fh + 512],
                                 start=(c == 0), stop=(c == 7),
                                 sig=(fh == 1 and c == 7))
            self.wait("vector", ev, ev_ast[par], ev_wobm)
            self.tt("vector", asb[par][:, 0:512], psum[:, 0:512],
                    wobm[:, 0:512], ALU.add)
            ev_e[par] = self.tt("vector", asb[par][:, 512:1024], psum[:, 512:1024],
                                wobm[:, 512:1024], ALU.add, sig=True)
            self.wait("sync", ev_e[par])
            ev_ast[par] = self.dma("sync", self.AOUT[128 * tt:128 * tt + 128, :],
                                   asb[par], f"ao{par}")
        self.barrier()

    def _ffn(self, l, wqkv2, wo2, w1p, w2p, b1c_d, b2m_d, lng_d, lnb_d,
             qsm_d, ksm_d, xpcq_d, xpsq_d, xpck_d, xpsk_d):
        abf, af32 = self.abf, self.af32
        st = self.stat
        XIN = self.XN if l == 0 else self.XRES
        self.dma("sync", self.b1c[:], b1c_d[l], "lc0")
        self.dma("sync", self.b2m[:], b2m_d[l], "lc1")
        self.dma("sync", self.lng[:], lng_d[l], "lc2")
        self.dma("sync", self.lnb[:], lnb_d[l], "lc3")
        ev_qs = self.dma("sync", self.qsm[:], qsm_d[l], "lc4")
        ev_ks = self.dma("sync", self.ksm[:], ksm_d[l], "lc5")

        x_fm = abf(0, 32832)
        aa = abf(0, 22528)
        w1blk = [abf(22528, 4096), abf(26624, 4096)]
        gelu_t = abf(30720, 512)
        xr2 = [af32(22528, 1024), af32(26624, 1024)]
        wqkv_sb = abf(32832, 12288)
        k_ring = abf(45120, 4096)
        v_ring = abf(49216, 4096)
        q_fm = abf(53312, 2048)
        O_fmw = abf(55360, 2048)
        wo2_sb = abf(57408, 4096)
        cqb = af32(61504, 512)
        sqb = af32(62528, 512)
        ckb = af32(63552, 512)
        skb = af32(64576, 512)
        ln_fm = abf(32832, 32768)
        qtm2 = [af32(65600, 512), af32(80448, 512)]
        qsq2 = af32(66624, 512)
        qrot2 = [af32(67648, 512), af32(81472, 512)]
        qtb2 = [abf(68672, 512), abf(82496, 512)]
        Sm = [[af32(69184, 640), af32(70464, 640)],
              [af32(83008, 640), af32(84288, 640)]]
        pbf = [[abf(71744, 640), abf(72384, 640)],
               [abf(85568, 640), abf(86208, 640)]]
        pT = [abf(73024, 640), abf(73664, 640)]
        t1 = af32(74304, 512)
        t2 = af32(75328, 512)
        xr = [af32(76352, 1024), af32(78400, 1024)]
        hsq8 = abf(65600, 4096)
        srow = af32(69696, 1024)
        rstdbc = af32(71744, 512)
        omegabc = af32(72768, 512)
        t1c = af32(73792, 512)
        t2c = af32(74816, 512)
        w2sb = abf(65600, 22528)

        def build_xfm(src_dram):
            self._ds("fA")
            z3 = x_fm.rearrange("p (c t) -> p c t", t=4104)
            self.fence("gpsimd", "pe")
            ev_z = self.memset("gpsimd", z3[:, :, 0:1], 0.0, sig=True)
            self.wait("vector", ev_z)
            gate_xs = [None, None]
            for tt in range(NT):
                par = tt % 2
                self.wait("sync", gate_xs[par])
                ev_x = self.dma("sync", self.xsb[par][:],
                                src_dram[128 * tt:128 * tt + 128, :], f"xs{par}")
                self.wait("vector", ev_x)
                self.fence("vector", "pe")
                ev_bb = self.cp("vector", self.xnbb[par][:], self.xsb[par][:],
                                sig=True)
                gate_xs[par] = ev_bb
                TBk, kk = self.tb_get()
                self.wait("tensor", ev_bb, self._tbevs[kk])
                ev_tr = None
                for c in range(8):
                    ev_tr = self.tr(TBk[:, 128 * c:128 * c + 128],
                                    self.xnbb[par][:, 128 * c:128 * c + 128],
                                    self.identb[:], sig=(c == 7))
                self.wait("vector", ev_tr)
                self._tbevs[kk] = self.cp(
                    "vector", z3[:, :, 1 + 128 * tt:1 + 128 * tt + 128],
                    TBk.rearrange("p (c t) -> p c t", t=128), sig=True)
            self._ds(None)

        build_xfm(XIN)
        ev_wqkv = self.dma("gpsimd", wqkv_sb,
                           wqkv2[l].rearrange("(c p) f -> p c f", p=128), "w1")
        if WO2_SYNC_CAST:
            wo2_f32 = self.af32(80448, 4096)
            evt = self.dma("sync", wo2_f32,
                           wo2[l].rearrange("(c p) f -> p c f", p=128), "w2f")
            self.wait("vector", evt)
            ev_wo2 = self.cp("vector", wo2_sb, wo2_f32, sig=True)
        else:
            ev_wo2 = self.dma("gpsimd", wo2_sb,
                              wo2[l].rearrange("(c p) f -> p c f", p=128), "w2")
        self.barrier()

        self._gate_xt = None
        ev_xstw = [None, None]
        bankev = [None, None, None, None]
        oev = [None, None]
        pbf_war = [[None, None], [None, None]]
        slots4 = [self.sp[0][:, 0:512], self.sp[0][:, 512:1024],
                  self.sp[1][:, 0:512], self.sp[1][:, 512:1024]]
        for w in range(NG):
            ks = w % 2
            self.wait("tensor", ev_wqkv)
            self._ds("fqkv")
            pend_tr = [None]

            def flush_tr():
                if pend_tr[0] is None:
                    return
                ttl_p, fg_p, pp_p, evb_p = pend_tr[0]
                pend_tr[0] = None
                TBk, kk = self.tb_get()
                self.wait("tensor", evb_p, self._tbevs[kk])
                ev_tr = None
                for c in range(4):
                    ev_tr = self.tr(TBk[:, 128 * c:128 * c + 128],
                                    qtb2[pp_p][:, 128 * c:128 * c + 128],
                                    self.identb[:], sig=(c == 3))
                self.wait("vector", ev_tr)
                if fg_p == 0:
                    d3 = q_fm.rearrange("p (c t) -> p c t", t=512)
                    self._tbevs[kk] = self.cp(
                        "vector", d3[:, :, 128 * ttl_p:128 * ttl_p + 128],
                        TBk[:, 0:512].rearrange("p (c t) -> p c t", t=128),
                        sig=True)
                else:
                    d3 = k_ring.rearrange("p (s c t) -> p s c t", s=2, t=512)
                    self._tbevs[kk] = self.cp(
                        "vector", d3[:, ksp[0], :, 128 * ttl_p:128 * ttl_p + 128],
                        TBk[:, 0:512].rearrange("p (c t) -> p c t", t=128),
                        sig=True)

            ksp = [ks]
            for ttl in range(4):
                gt = 4 * w + ttl
                self.wait("sync", self._gate_xt)
                evtabs = [self.dma("sync", cqb, xpcq_d[128 * gt:128 * gt + 128, :], "tq0"),
                          self.dma("sync", sqb, xpsq_d[128 * gt:128 * gt + 128, :], "tq1"),
                          self.dma("sync", ckb, xpck_d[128 * gt:128 * gt + 128, :], "tq2"),
                          self.dma("sync", skb, xpsk_d[128 * gt:128 * gt + 128, :], "tq3")]
                def qk_pre(fg, bank, sidx, ev_mm):
                    pp = fg % 2
                    sb_ = 24 + 16 * pp
                    qtm = qtm2[pp]
                    self.wait("vector", ev_mm)
                    evq = self.cp("vector", qtm, bank, sig=True)
                    bankev[sidx] = evq
                    self.tt("vector", qsq2, qtm, qtm, ALU.mult)
                    self.red("vector", st[:, sb_:sb_ + 8],
                             qsq2.rearrange("p (h d) -> p h d", d=64), ALU.add)
                    evmx = self.op("vector", lambda e, sb_=sb_: e.tensor_scalar_max(
                        st[:, sb_:sb_ + 8], st[:, sb_:sb_ + 8], 1e-24), sig=True)
                    self.swait("vector", evmx)
                    self.wait("scalar", evmx)
                    self.act(st[:, sb_ + 8:sb_ + 16], st[:, sb_:sb_ + 8],
                             AF.Ln, rt=True)
                    evrn = self.act(st[:, sb_:sb_ + 8], st[:, sb_ + 8:sb_ + 16],
                                    AF.Exp, scale=-0.5, sig=True)
                    return (fg, pp, sb_, evrn)

                def qk_post(ctx):
                    fg, pp, sb_, evrn = ctx
                    qtm, qrot = qtm2[pp], qrot2[pp]
                    tabs = (cqb, sqb) if fg == 0 else (ckb, skb)
                    smap = self.qsm if fg == 0 else self.ksm
                    ev_sm = ev_qs if fg == 0 else ev_ks
                    self.wait("vector", evrn, ev_sm, *evtabs)
                    q3 = qtm.rearrange("p (h z) -> p h z", z=64)
                    rnb = bass.AP(self.stat, sb_, [[96, 128], [1, 8], [0, 64]])
                    self.tt("vector", q3, q3, rnb, ALU.mult)
                    self.tt("vector", qtm, qtm, smap[:], ALU.mult)
                    r3 = qrot.rearrange("p (h z) -> p h z", z=64)
                    c3 = tabs[0].rearrange("p (h z) -> p h z", z=64)
                    s3 = tabs[1].rearrange("p (h z) -> p h z", z=64)
                    e_, o_ = q3[:, :, 0:32], q3[:, :, 32:64]
                    ce, se_ = c3[:, :, 0:32], s3[:, :, 0:32]
                    t13 = t1[:, 0:256].rearrange("p (h z) -> p h z", z=32)
                    t23 = t2[:, 0:256].rearrange("p (h z) -> p h z", z=32)
                    self.tt("vector", t13, e_, ce, ALU.mult)
                    self.tt("vector", t23, o_, se_, ALU.mult)
                    self.tt("vector", r3[:, :, 0:32], t13, t23, ALU.subtract)
                    self.tt("vector", t13, o_, ce, ALU.mult)
                    ev5 = self.tt("vector", t23, e_, se_, ALU.mult, sig=True)
                    self._gate_xt = ev5
                    self.tt("vector", r3[:, :, 32:64], t13, t23, ALU.add, rt=True)
                    evb = self.cp("vector", qtb2[pp], qrot, sig=True)
                    pend_tr[0] = (ttl, fg, pp, evb)

                pend_qk = []
                for fg in range(3):
                    sidx = (3 * ttl + fg) % 4
                    bank = slots4[sidx]
                    self.wait("tensor", bankev[sidx])
                    ev_mm = None
                    for dc in range(8):
                        off = 4104 * dc + (0 if dc >= 4 else 1) + 128 * gt
                        ev_mm = self.mm(bank, x_fm[:, off:off + 128],
                                        wqkv_sb[:, 1536 * dc + 512 * fg:1536 * dc + 512 * fg + 512],
                                        start=(dc == 0), stop=(dc == 7), sig=(dc == 7))
                    flush_tr()
                    if fg == 2:
                        self.wait("vector", ev_mm)
                        bankev[sidx] = self.cp(
                            "vector",
                            v_ring[:, 2048 * ks + 512 * ttl:2048 * ks + 512 * ttl + 512],
                            bank, sig=True)
                        continue
                    ctx = qk_pre(fg, bank, sidx, ev_mm)
                    if pend_qk:
                        qk_post(pend_qk.pop())
                    pend_qk.append(ctx)
                if pend_qk:
                    qk_post(pend_qk.pop())
            flush_tr()
            self._ds("fattn")
            self.wait("tensor", self._tbevs[0], self._tbevs[1])
            for hp in range(4):
                ev_pv_last = [None, None]

                def p1a(a):
                    apar = a % 2
                    sb_ = 24 + 8 * apar
                    qb = 512 * w + 128 * a
                    kw0 = qb - 512
                    c0 = max(0, -kw0)
                    kstart = max(0, kw0)
                    width = qb + 128 - kstart
                    nb = width // 128
                    segs = []
                    j = kstart
                    pc = 0
                    while j < qb + 128:
                        wslot = (j // 512) % 2
                        tloc = j % 512
                        n = min(512 - tloc, qb + 128 - j)
                        n = min(n, (512 if pc < 512 else 1024) - pc)
                        segs.append((pc, n, wslot, tloc))
                        j += n
                        pc += n
                    evnb2 = [None, None]
                    for hh in range(2):
                        bank = self.sp[hh]
                        self.wait("tensor", bankev[2 * hh], bankev[2 * hh + 1])
                        qrow = slice(0, 64) if hh == 0 else slice(64, 128)
                        tp = (0, 0) if hh == 0 else (64, 0)
                        ev_s = None
                        for si, (pc, n, wslot, tloc) in enumerate(segs):
                            ev_s = self.mm(
                                bank[:, pc:pc + n],
                                q_fm[qrow, 512 * hp + 128 * a:512 * hp + 128 * a + 128],
                                k_ring[qrow, 2048 * wslot + 512 * hp + tloc:
                                       2048 * wslot + 512 * hp + tloc + n],
                                tp=tp, sig=(si == len(segs) - 1))
                        self.wait("vector", ev_s)
                        evm = self.tt("vector", Sm[apar][hh][:, 0:width],
                                      bank[:, 0:width],
                                      self.mask[:, c0:c0 + width], ALU.add, sig=True)
                        bankev[2 * hh] = bankev[2 * hh + 1] = evm
                        self.red("vector", st[:, sb_ + hh:sb_ + 1 + hh],
                                 Sm[apar][hh][:, 0:width], ALU.max, sig=True)
                        evnb2[hh] = self.ts("vector", st[:, sb_ + 2 + hh:sb_ + 3 + hh],
                                            st[:, sb_ + hh:sb_ + 1 + hh], -QSC,
                                            sig=True)
                    return [a, apar, width, nb, kstart, evnb2, None, None]

                def p1b(ctx):
                    a, apar, width, nb, kstart, evnb2, _, _ = ctx
                    sb_ = 24 + 8 * apar
                    evex2 = [None, None]
                    for hh in range(2):
                        self.wait("scalar", evnb2[hh], pbf_war[apar][hh])
                        evex2[hh] = self.act(pbf[apar][hh][:, 0:width],
                                             Sm[apar][hh][:, 0:width],
                                             AF.Exp, bias=st[:, sb_ + 2 + hh:sb_ + 3 + hh],
                                             scale=QSC,
                                             accum=st[:, sb_ + 4 + hh:sb_ + 5 + hh],
                                             sig=True)
                    ctx[6] = evex2

                def p2a(ctx):
                    a, apar, width, nb, kstart, _, evex2, _ = ctx
                    sb_ = 24 + 8 * apar
                    evnm2 = [None, None]
                    for hh in range(2):
                        self.wait("vector", evex2[hh])
                        evrc = self.recip(st[:, sb_ + 6 + hh:sb_ + 7 + hh],
                                          st[:, sb_ + 4 + hh:sb_ + 5 + hh], sig=True)
                        self.wait("scalar", evrc)
                        evnm2[hh] = self.act(pbf[apar][hh][:, 0:width],
                                             pbf[apar][hh][:, 0:width],
                                             AF.Copy,
                                             scale=st[:, sb_ + 6 + hh:sb_ + 7 + hh],
                                             sig=True)
                    ctx[7] = evnm2

                def p2b(ctx):
                    a, apar, width, nb, kstart, _, _, evnm2 = ctx
                    for hh in range(2):
                        TBk, kk = self.tb_get()
                        self.wait("tensor", evnm2[hh], self._tbevs[kk])
                        ev_tr = None
                        for kb in range(nb):
                            ev_tr = self.tr(TBk[:, 128 * kb:128 * kb + 128],
                                            pbf[apar][hh][:, 128 * kb:128 * kb + 128],
                                            self.identb[:], sig=(kb == nb - 1))
                        pbf_war[apar][hh] = ev_tr
                        self.wait("vector", ev_tr)
                        evpt = self.cp("vector", pT[hh][:, 0:width],
                                       TBk[:, 0:width], sig=True)
                        self._tbevs[kk] = evpt
                        self.wait("tensor", evpt)
                        if a == 0:
                            self.wait("tensor", oev[hh])
                        obank = self.o0 if hh == 0 else self.o1
                        orow = slice(0, 64) if hh == 0 else slice(64, 128)
                        otp = (0, 0) if hh == 0 else (0, 64)
                        h = 2 * hp + hh
                        ev_pv = None
                        for kb in range(nb):
                            j = kstart + 128 * kb
                            wslot = (j // 512) % 2
                            tl = (j % 512) // 128
                            vcol = 2048 * wslot + 512 * tl + 64 * h
                            ev_pv = self.mm(obank[orow, 128 * a:128 * a + 128],
                                            v_ring[:, vcol:vcol + 64],
                                            pT[hh][:, 128 * kb:128 * kb + 128],
                                            start=(kb == 0), stop=(kb == nb - 1),
                                            tp=otp, sig=(kb == nb - 1))
                        if a == 3:
                            ev_pv_last[hh] = ev_pv

                prev = None
                for a in range(5):
                    if prev is not None:
                        p2a(prev)
                    cur = None
                    if a < 4:
                        cur = p1a(a)
                    if prev is not None:
                        p2b(prev)
                    if cur is not None:
                        p1b(cur)
                    prev = cur
                self.wait("vector", ev_pv_last[0], ev_pv_last[1])
                oev[0] = self.cp("vector", O_fmw[0:64, 512 * hp:512 * hp + 512],
                                 self.o0[0:64, :], sig=True)
                oev[1] = self.cp("vector", O_fmw[64:128, 512 * hp:512 * hp + 512],
                                 self.o1[64:128, :], sig=True)
            if w == 0 and "FDBG" in self.dbg and l == 0:
                fd = self.nc.dram_tensor("FDBG", [128, 16384], BF16,
                                         kind="ExternalOutput").ap()
                evm1 = self.milestone("vector")
                self.wait("sync", evm1)
                self.dma("sync", fd[:, 0:4104], x_fm[:, 0:4104], "fdbg0")
                self.dma("sync", fd[:, 4104:6152], q_fm, "fdbg1")
                self.dma("sync", fd[:, 6152:8200], k_ring[:, 0:2048], "fdbg2")
                self.dma("sync", fd[:, 8200:10248], v_ring[:, 0:2048], "fdbg3")
                self.dma("sync", fd[:, 10248:12296], O_fmw, "fdbg4")
                self.dma("sync", fd[:, 12296:12936], pT[0], "fdbg5")
                self.dma("sync", fd[:, 13000:13080],
                         self.stat[:, 0:40].bitcast(BF16), "fdbg6")
            self._ds("fwo")
            self.wait("tensor", ev_wo2)
            for ttl in range(4):
                tt = 4 * w + ttl
                par = ttl % 2
                self.wait("sync", ev_xstw[par])
                ev_x = self.dma("sync", xr[par], XIN[128 * tt:128 * tt + 128, :],
                                f"xr{par}")
                self.wait("tensor", bankev[2 * par], bankev[2 * par + 1],
                          oev[0], oev[1])
                ev = None
                for fh in range(2):
                    for oc in range(4):
                        ev = self.mm(self.sp[par][:, 512 * fh:512 * fh + 512],
                                     O_fmw[:, 512 * oc + 128 * ttl:512 * oc + 128 * ttl + 128],
                                     wo2_sb[:, 1024 * oc + 512 * fh:1024 * oc + 512 * fh + 512],
                                     start=(oc == 0), stop=(oc == 3),
                                     sig=(fh == 1 and oc == 3))
                self.wait("vector", ev, ev_x, ev_xstw[par])
                evu = self.tt("vector", xr[par], xr[par], self.sp[par][:, 0:1024],
                              ALU.add, sig=True)
                bankev[2 * par] = bankev[2 * par + 1] = evu
                self.wait("sync", evu)
                if w == 0 and ttl == 0 and l == 0 and "FDBG2" in self.dbg:
                    fd2 = self.nc.dram_tensor("FDBG2", [128, 1024], F32,
                                              kind="ExternalOutput").ap()
                    fd3 = self.nc.dram_tensor("FDBG3", [128, 6144], BF16,
                                              kind="ExternalOutput").ap()
                    self.dma("sync", fd2[:, :], xr[par], "fdbg7")
                    self.dma("sync", fd3[:, 0:4096], wo2_sb, "fdbg8")
                    self.dma("sync", fd3[:, 4096:6144], O_fmw, "fdbg9")
                ev_xstw[par] = self.dma("sync", self.XRES[128 * tt:128 * tt + 128, :],
                                        xr[par], f"xo{par}")
            self._ds(None)
        self.barrier()

        build_xfm(self.XRES)
        self.barrier()

        self._ds("fC1")
        for gq in range(NG):
            evh = None
            for dc in range(8):
                off = 4104 * dc + (0 if dc >= 4 else 1) + 512 * gq
                evh = self.tt("vector", hsq8[:, 512 * dc:512 * dc + 512],
                              x_fm[:, off:off + 512], x_fm[:, off:off + 512],
                              ALU.mult, sig=(dc == 7))
            self.fence("tensor", "ve")
            self.fence("tensor", "se")
            evsum = None
            for dc in range(8):
                off = 4104 * dc + (0 if dc >= 4 else 1) + 512 * gq
                evsum = self.mm(self.u0[0:1, :], self.ones_b[:, 0:1],
                                x_fm[:, off:off + 512],
                                start=(dc == 0), stop=(dc == 7), sig=(dc == 7))
            self.wait("tensor", evh)
            evsq = None
            for dc in range(8):
                evsq = self.mm(self.o0[0:1, :], self.ones_b[:, 0:1],
                               hsq8[:, 512 * dc:512 * dc + 512],
                               start=(dc == 0), stop=(dc == 7), sig=(dc == 7))
            mean_r = srow[0:1, 0:512]
            var_r = srow[0:1, 512:1024]
            self.wait("vector", evsum, evsq)
            self.ts("vector", mean_r, self.u0[0:1, :], 1.0 / D, rt=True)
            self.ts("vector", var_r, self.o0[0:1, :], 1.0 / D, rt=True)
            self.tt("vector", self.scr64[0:1, 0:512], mean_r, mean_r, ALU.mult, rt=True)
            evv = self.tt("vector", var_r, var_r, self.scr64[0:1, 0:512],
                          ALU.subtract, sig=True, rt=True)
            self.wait("scalar", evv)
            self.act(self.scr64[0:1, 0:512], var_r, AF.Ln, bias=self.epsb[0:1, 0:1], rt=True)
            evrs = self.act(var_r, self.scr64[0:1, 0:512], AF.Exp, scale=-0.5, sig=True)
            self.wait("vector", evrs)
            evom = self.tt("vector", mean_r, mean_r, var_r, ALU.mult, sig=True)
            self.wait("tensor", evom)
            evb1 = self.mm(self.o1[:, :], self.ones_f[0:1, 0:128], var_r, sig=True)
            evb2 = self.mm(self.u0[:, :], self.ones_f[0:1, 0:128], mean_r, sig=True)
            self.fence("vector", "ge")
            self.wait("vector", evb1, evb2)
            self.cp("vector", rstdbc, self.o1[:, :])
            evbc = self.cp("vector", omegabc, self.u0[:, :], sig=True)
            self.wait("vector", evbc)
            for dc in range(8):
                off = 4104 * dc + (0 if dc >= 4 else 1) + 512 * gq
                self.fence("vector", "ve")
                evp1 = self.tt("vector", t1c, x_fm[:, off:off + 512], rstdbc,
                               ALU.mult, sig=True)
                self.wait("vector", evp1)
                self.tt("vector", t2c, t1c, omegabc, ALU.subtract)
                self.ts("vector",
                        ln_fm[:, 4096 * dc + 512 * gq:4096 * dc + 512 * gq + 512],
                        t2c, self.lng[:, dc:dc + 1], self.lnb[:, dc:dc + 1],
                        ALU.mult, ALU.add)
        self._ds(None)
        self.barrier()

        self._ds("fC2")
        for th in range(4):
            tok0 = 1024 * th
            slots = [self.sp[0][:, 0:512], self.sp[0][:, 512:1024],
                     self.sp[1][:, 0:512], self.sp[1][:, 512:1024],
                     self.o0[:, :], self.o1[:, :]]
            ev_ev = [None] * 6
            ev_wb = [None, None]
            for hb in range(11):
                wpar = hb % 2
                self.fence("gpsimd", "pe")
                ev_wb[wpar] = self.dma(
                    "gpsimd", w1blk[wpar],
                    w1p[l].rearrange("(c p) f -> p c f", p=128)[:, :, 512 * hb:512 * hb + 512],
                    f"w1b{wpar}")
                self.wait("tensor", ev_wb[wpar])
                for hcl in range(4):
                    hc = 4 * hb + hcl
                    for g2 in range(2):
                        sidx = (8 * hb + hcl * 2 + g2) % 6
                        if ev_ev[sidx] is not None:
                            self.wait("tensor", ev_ev[sidx])
                        ev = None
                        for dc in range(8):
                            ev = self.mm(
                                slots[sidx],
                                w1blk[wpar][:, 512 * dc + 128 * hcl:512 * dc + 128 * hcl + 128],
                                ln_fm[:, 4096 * dc + tok0 + 512 * g2:
                                      4096 * dc + tok0 + 512 * g2 + 512],
                                start=(dc == 0), stop=(dc == 7), sig=(dc == 7))
                        self.wait("scalar", ev)
                        if hc < NHC:
                            acol = 1024 * hc + 512 * g2
                            ev_ev[sidx] = self.act(aa[:, acol:acol + 512], slots[sidx],
                                                   AF.Identity,
                                                   bias=self.b1c[:, hc:hc + 1], sig=True)
                        else:
                            acol = 1024 * (hc - NHC) + 512 * g2
                            self.fence("scalar", "ve")
                            evg = self.act(gelu_t[:], slots[sidx], AF.Gelu,
                                           bias=self.b1c[:, hc:hc + 1], sig=True)
                            self.wait("vector", evg)
                            ev_ev[sidx] = self.tt("vector", aa[:, acol:acol + 512],
                                                  aa[:, acol:acol + 512], gelu_t[:],
                                                  ALU.mult, sig=True)
            self.barrier()
            self.fence("gpsimd", "pe")
            ev_w2 = self.dma("gpsimd", w2sb,
                             w2p[l].rearrange("(c p) f -> p c f", p=128), "w2s")
            self.wait("tensor", ev_w2)
            ev_xst = [None, None]
            ev_ee = [None, None]
            for ttl in range(8):
                tt = 8 * th + ttl
                par = ttl % 2
                self.wait("sync", ev_xst[par])
                ev_x = self.dma("sync", xr2[par],
                                self.XRES[128 * tt:128 * tt + 128, :], f"xq{par}")
                if ev_ee[par] is not None:
                    self.wait("tensor", ev_ee[par])
                ev = None
                for fh in range(2):
                    for hc in range(NHC):
                        ev = self.mm(
                            self.sp[par][:, 512 * fh:512 * fh + 512],
                            aa[:, 1024 * hc + 128 * ttl:1024 * hc + 128 * ttl + 128],
                            w2sb[:, 1024 * hc + 512 * fh:1024 * hc + 512 * fh + 512],
                            start=(hc == 0), stop=(hc == NHC - 1),
                            sig=(fh == 1 and hc == NHC - 1))
                self.wait("vector", ev, ev_x, ev_xst[par])
                self.tt("vector", xr2[par], xr2[par], self.b2m[:], ALU.add)
                ev_ee[par] = self.tt("vector", xr2[par], xr2[par],
                                     self.sp[par][:, 0:1024], ALU.add, sig=True)
                if l == 1:
                    self.wait("gpsimd", ev_ee[par])
                    e1 = self.dacc(xr2[par], self.Xd[128 * tt:128 * tt + 128, :],
                                   f"qa{par}")
                    self.wait("gpsimd", e1)
                    e2 = self.dacc(xr2[par], self.AOUT[128 * tt:128 * tt + 128, :],
                                   f"qb{par}")
                    self.wait("sync", e2)
                    ev_xst[par] = self.dma("sync",
                                           self.OUT[128 * tt:128 * tt + 128, :],
                                           xr2[par], f"xqs{par}")
                else:
                    self.wait("sync", ev_ee[par])
                    ev_xst[par] = self.dma("sync",
                                           self.XRES[128 * tt:128 * tt + 128, :],
                                           xr2[par], f"xqs{par}")
            self.barrier()
        self._ds(None)

    def _p5(self):
        af32 = self.af32
        b1 = [af32(0, 1024), af32(2048, 1024)]
        b2 = [af32(4096, 1024), af32(6144, 1024)]
        b3 = [af32(8192, 1024), af32(10240, 1024)]
        ob = [af32(12288, 1024), af32(14336, 1024)]
        ev_st = [None, None]
        gate5 = [None, None]
        for tt in range(NT):
            par = tt % 2
            sl = slice(128 * tt, 128 * tt + 128)
            self.wait("sync", gate5[par])
            e1 = self.dma("sync", b1[par], self.Xd[sl, :], f"p5a{par}")
            e2 = self.dma("sync", b2[par], self.AOUT[sl, :], f"p5b{par}")
            e3 = self.dma("sync", b3[par], self.XRES[sl, :], f"p5c{par}")
            self.wait("vector", e1, e2, e3, ev_st[par])
            self.tt("vector", ob[par], b1[par], b2[par], ALU.add)
            ev = self.tt("vector", ob[par], ob[par], b3[par], ALU.add, sig=True)
            gate5[par] = ev
            self.wait("sync", ev)
            ev_st[par] = self.dma("sync", self.OUT[sl, :], ob[par], f"os{par}")


_BUILD_CACHE = {}


def _get_nc(dbg=(), phases=("p0", "p1", "p2", "f0", "f1")):
    key = (tuple(sorted(dbg)), tuple(phases))
    if key not in _BUILD_CACHE:
        _BUILD_CACHE[key] = MK(dbg).build(phases)
    return _BUILD_CACHE[key]


def kernel(**inputs):
    nc = _get_nc()
    shared = _prep_shared(inputs)
    x = np.asarray(inputs["L0_attn_x"], np.float32)
    in_maps = []
    for c in range(8):
        m = dict(shared)
        m["x"] = np.ascontiguousarray(x[c])
        in_maps.append(m)
    res = run_bass_kernel_spmd(nc, in_maps, core_ids=list(range(8)))
    out = np.stack([r["out"] for r in res.results], 0)
    return out.astype(np.float32)



# version 12
# speedup vs baseline: 1.1877x; 1.0101x over previous
import math
from contextlib import ExitStack

import numpy as np
import ml_dtypes

import concourse.bass as bass
import concourse.mybir as mybir
from concourse.bass_utils import run_bass_kernel_spmd

F32 = mybir.dt.float32
BF16 = mybir.dt.bfloat16
AF = mybir.ActivationFunctionType
ALU = mybir.AluOpType
AX = mybir.AxisListType
BFNP = ml_dtypes.bfloat16

S = 4096
D = 1024
NT = S // 128
NG = S // 512
ALPHA = 1.0 / math.sqrt(16.0 + 512.0 + 512.0)
HID = 2730
HIDP = 2816
NHC = HIDP // 128
F1P = 2 * HIDP
QSC = 8.0
NEG = -1e30

ENGS = ("tensor", "vector", "scalar", "gpsimd", "sync")
AUTO_DRAIN = False
DRAIN_SECTIONS = ()
WO2_SYNC_CAST = False

from contextlib import contextmanager
PROG = {"tensor": "pe", "vector": "ve", "scalar": "se", "gpsimd": "ge"}
ARENA_COLS = 90624


def _host_consts():
    c = {}
    c["ident_bf"] = np.eye(128, dtype=BFNP)
    c["ident_f32"] = np.eye(128, dtype=np.float32)
    pos = np.arange(S, dtype=np.float64)[:, None]
    ids = np.arange(D // 2, dtype=np.float64)
    theta = 10000.0 ** (-2.0 * ids) / (D // 2)
    r = pos * theta[None, :]
    c["mla_cos"] = np.cos(r).astype(np.float32)
    c["mla_sin"] = np.sin(r).astype(np.float32)
    half = np.arange(0, 64, 2, dtype=np.float64)
    inv_freq = 1.0 / 10000.0 ** (half / 64.0)
    t = np.arange(S, dtype=np.float64)
    fr = t[:, None] * inv_freq[None, :]
    fr = np.concatenate([fr, fr], -1)
    base = (half + 0.4 * 64.0) / (1.4 * 64.0)
    power = (t - S // 2) / 256.0
    sc = base[None, :] ** power[:, None]
    sc = np.concatenate([sc, sc], -1)
    cf, sf = np.cos(fr), np.sin(fr)
    tile8 = lambda a: np.tile(a.astype(np.float32), (1, 8))
    c["xp_cq"] = tile8(cf * sc)
    c["xp_sq"] = tile8(sf * sc)
    c["xp_ck"] = tile8(cf / sc)
    c["xp_sk"] = tile8(sf / sc)
    p = np.arange(128)[:, None]
    cc = np.arange(640)[None, :]
    keep = (p <= cc) & (cc <= p + 512)
    c["mask640"] = np.where(keep, 0.0, NEG).astype(BFNP)
    return c


_CONSTS = None


def _consts():
    global _CONSTS
    if _CONSTS is None:
        _CONSTS = _host_consts()
    return _CONSTS


def _prep_shared(inp):
    d = {}
    d["wdq"] = inp["L0_attn_Wd_q_w"]
    d["wuq"] = inp["L0_attn_Wu_q_w"]
    d["wqr"] = inp["L0_attn_Wq_r_w"]
    d["wkr"] = inp["L0_attn_Wk_r_w"]
    d["wdkv"] = inp["L0_attn_Wd_kv_w"]
    d["wuk"] = inp["L0_attn_Wu_k_w"]
    d["wuv"] = inp["L0_attn_Wu_v_w"]
    d["wo"] = inp["L0_attn_Wo_w"]
    chunkcol = lambda b: np.ascontiguousarray(np.asarray(b).reshape(-1, 128).T)
    bmap = lambda b: np.ascontiguousarray(np.broadcast_to(np.asarray(b), (128, b.shape[-1])))
    d["bdq_c"] = chunkcol(inp["L0_attn_Wd_q_b"])
    d["bdkv_c"] = chunkcol(inp["L0_attn_Wd_kv_b"])
    d["buq_m"] = bmap(inp["L0_attn_Wu_q_b"])
    d["bqr_m"] = bmap(inp["L0_attn_Wq_r_b"])
    d["bkr_m"] = bmap(inp["L0_attn_Wk_r_b"])
    d["buk_m"] = bmap(inp["L0_attn_Wu_k_b"])
    d["buv_m"] = bmap(inp["L0_attn_Wu_v_b"])
    d["wob_m"] = bmap(inp["L0_attn_Wo_b"])
    d["wqkv2"] = inp["L0_ffn_wqkv"]
    d["wo2"] = inp["L0_ffn_wo"]
    w1 = np.asarray(inp["L0_ffn_w1"])
    w1p = np.zeros((2, D, F1P), np.float32)
    w1p[:, :, :HID] = w1[:, :, :HID]
    w1p[:, :, HIDP:HIDP + HID] = w1[:, :, HID:]
    d["w1p"] = w1p
    w2 = np.asarray(inp["L0_ffn_w2"])
    w2p = np.zeros((2, HIDP, D), np.float32)
    w2p[:, :HID, :] = w2
    d["w2p"] = w2p
    b1 = np.asarray(inp["L0_ffn_b1"])
    b1p = np.zeros((2, F1P), np.float32)
    b1p[:, :HID] = b1[:, :HID]
    b1p[:, HIDP:HIDP + HID] = b1[:, HID:]
    d["b1c"] = np.ascontiguousarray(
        np.stack([b1p[l].reshape(-1, 128).T for l in range(2)], 0))
    d["b2m"] = np.ascontiguousarray(
        np.stack([np.broadcast_to(np.asarray(inp["L0_ffn_b2"][l]), (128, D)) for l in range(2)], 0))
    d["lng_c"] = np.ascontiguousarray(
        np.stack([np.asarray(inp["L0_ffn_ln_g"][l]).reshape(8, 128).T for l in range(2)], 0))
    d["lnb_c"] = np.ascontiguousarray(
        np.stack([np.asarray(inp["L0_ffn_ln_b"][l]).reshape(8, 128).T for l in range(2)], 0))
    d["qsm"] = np.ascontiguousarray(
        np.stack([np.broadcast_to(np.tile(np.asarray(inp["L0_ffn_q_scale"][l]), 8), (128, 512)) for l in range(2)], 0))
    d["ksm"] = np.ascontiguousarray(
        np.stack([np.broadcast_to(np.tile(np.asarray(inp["L0_ffn_k_scale"][l]), 8), (128, 512)) for l in range(2)], 0))
    for k, v in _consts().items():
        d[k] = v
    return d


class MK:
    def __init__(self, dbg=()):
        self.dbg = set(dbg)
        self.nc = bass.Bass("TRN2", target_bir_lowering=False)
        self.es = ExitStack()
        self.q = {e: [] for e in ENGS}
        self.semh, self.semc = {}, {}
        for s in ("pe", "ve", "se", "ge", "bar"):
            self.semh[s] = self.es.enter_context(self.nc.semaphore(s))
            self.semc[s] = 0
        self.lastw = {}
        self.dirty = set()
        self.unsig = {e: 0 for e in ENGS}
        self.drain_scope = 0
        self.drain_sections = set(DRAIN_SECTIONS)

    def _app(self, eng, fn):
        self.q[eng].append(fn)

    def op(self, eng, fn, sig=False):
        ev = None
        if sig:
            s = PROG[eng]
            self.semc[s] += 1
            tgt = self.semc[s]
            h = self.semh[s]
            self._app(eng, lambda e, fn=fn, h=h: fn(e).then_inc(h, 1))
            ev = (s, tgt)
            self.unsig[eng] = 0
        else:
            self._app(eng, fn)
            self.unsig[eng] += 1
        if (AUTO_DRAIN or self.drain_scope) and eng in ("vector", "scalar", "gpsimd"):
            self._app(eng, lambda e: e.drain(fusable=True))
        return ev

    def dacc(self, out, in_, slot):
        if slot not in self.semh:
            self.semh[slot] = self.es.enter_context(self.nc.semaphore(slot))
            self.semc[slot] = 0
        self.semc[slot] += 16
        tgt = self.semc[slot]
        h = self.semh[slot]
        self._app("gpsimd", lambda e, out=out, in_=in_, h=h: e.dma_start(
            out=out, in_=in_, accum_op=ALU.add).then_inc(h, 16))
        self.dirty.add(slot)
        return (slot, tgt)

    def dma(self, eng, out, in_, slot):
        if slot not in self.semh:
            self.semh[slot] = self.es.enter_context(self.nc.semaphore(slot))
            self.semc[slot] = 0
        self.semc[slot] += 16
        tgt = self.semc[slot]
        h = self.semh[slot]
        self._app(eng, lambda e, out=out, in_=in_, h=h: e.dma_start(out=out, in_=in_).then_inc(h, 16))
        self.dirty.add(slot)
        return (slot, tgt)

    def wait(self, eng, *evs):
        for ev in evs:
            if ev is None:
                continue
            s, tgt = ev
            if PROG.get(eng) == s:
                continue
            key = (eng, s)
            if self.lastw.get(key, 0) >= tgt:
                continue
            self.lastw[key] = tgt
            h = self.semh[s]
            self._app(eng, lambda e, h=h, tgt=tgt: e.wait_ge(h, tgt))

    PROD = {"pe": "tensor", "ve": "vector", "se": "scalar", "ge": "gpsimd"}

    def fence(self, eng, on):
        prod = self.PROD.get(on)
        if prod is not None and self.unsig[prod] > 0:
            self.milestone(prod)
        self.wait(eng, (on, self.semc[on]))

    def _ds(self, name):
        self.drain_scope = 1 if (name is not None and name in self.drain_sections) else 0

    def milestone(self, eng):
        if eng == "vector":
            return self.op(eng, lambda e: e.tensor_copy(
                self.stat[:, 90:91], self.stat[:, 91:92]), sig=True)
        if eng == "scalar":
            return self.op(eng, lambda e: e.copy(
                self.stat[0:1, 92:93], self.stat[0:1, 93:94]), sig=True)
        s = PROG[eng]
        self.semc[s] += 1
        tgt = self.semc[s]
        h = self.semh[s]
        self._app(eng, lambda e, h=h: e.drain(fusable=True).then_inc(h, 1))
        self.unsig[eng] = 0
        return (s, tgt)

    def barrier(self):
        for s in sorted(self.dirty):
            self.wait("sync", (s, self.semc[s]))
        self.dirty.clear()
        for e in ENGS:
            h = self.semh["bar"]
            self.semc["bar"] += 1
            if e in ("gpsimd", "sync"):
                self._app(e, lambda en, h=h: en.nop().then_inc(h, 1))
            else:
                self._app(e, lambda en, h=h: en.drain(fusable=True).then_inc(h, 1))
        tot = self.semc["bar"]
        for e in ENGS:
            self._app(e, lambda en, h=self.semh["bar"], tot=tot: en.wait_ge(h, tot))
        for e in ENGS:
            for s in self.semc:
                if s != "bar":
                    self.lastw[(e, s)] = self.semc[s]
            self.lastw[(e, "bar")] = tot
        self._tbevs = [None, None]
        self._tbk = 0

    def tb_get(self):
        k = self._tbk
        self._tbk ^= 1
        return self.TBs[k], k

    def mm(self, out, lh, rh, start=True, stop=True, tp=None, sig=False):
        return self.op("tensor", lambda e, out=out, lh=lh, rh=rh, start=start, stop=stop, tp=tp:
                       e.matmul(out, lh, rh, start=start, stop=stop, tile_position=tp,
                                skip_group_check=True), sig=sig)

    def tr(self, out, in_, ident, sig=False):
        return self.op("tensor", lambda e, out=out, in_=in_, ident=ident:
                       e.transpose(out, in_, ident), sig=sig)

    def _rt(self, eng, ev):
        self.swait(eng, ev)
        return ev

    def act(self, out, in_, f, bias=0.0, scale=1.0, accum=None, sig=False, rt=False):
        ev = self.op("scalar", lambda e, out=out, in_=in_, f=f, bias=bias, scale=scale, accum=accum:
                     e.activation(out, in_, f, bias=bias, scale=scale, accum_out=accum),
                     sig=sig or rt)
        return self._rt("scalar", ev) if rt else ev

    def tt(self, eng, out, a, b, alu, sig=False, rt=False):
        ev = self.op(eng, lambda e, out=out, a=a, b=b, alu=alu:
                     e.tensor_tensor(out, a, b, alu), sig=sig or rt)
        return self._rt(eng, ev) if rt else ev

    def ts(self, eng, out, a, s1, s2=None, op0=ALU.mult, op1=ALU.bypass, sig=False, rt=False):
        ev = self.op(eng, lambda e, out=out, a=a, s1=s1, s2=s2, op0=op0, op1=op1:
                     e.tensor_scalar(out, a, s1, s2, op0, op1), sig=sig or rt)
        return self._rt(eng, ev) if rt else ev

    def cp(self, eng, out, in_, sig=False, rt=False):
        ev = self.op(eng, lambda e, out=out, in_=in_: e.tensor_copy(out, in_), sig=sig or rt)
        return self._rt(eng, ev) if rt else ev

    def swait(self, eng, ev):
        s, tgt = ev
        h = self.semh[s]
        self._app(eng, lambda e, h=h, tgt=tgt: e.wait_ge(h, tgt))
        self.lastw[(eng, s)] = max(self.lastw.get((eng, s), 0), tgt)

    def red(self, eng, out, in_, alu, sig=False):
        ev = self.op(eng, lambda e, out=out, in_=in_, alu=alu:
                     e.tensor_reduce(out, in_, AX.X, alu), sig=True)
        self.swait(eng, ev)
        return ev

    def recip(self, out, in_, sig=False):
        ev = self.op("vector", lambda e, out=out, in_=in_: e.reciprocal(out, in_),
                     sig=True)
        self.swait("vector", ev)
        return ev

    def memset(self, eng, ap, v, sig=False, rt=False):
        ev = self.op(eng, lambda e, ap=ap, v=v: e.memset(ap, v), sig=sig or rt)
        return self._rt(eng, ev) if rt else ev

    def build(self, phases=("p0", "p1", "p2", "f0", "f1")):
        nc = self.nc
        es = self.es
        dbg = self.dbg

        def din(name, shape, dt=F32):
            return nc.dram_tensor(name, list(shape), dt, kind="ExternalInput").ap()

        def dscratch(name, shape, dt=F32):
            kind = "ExternalOutput" if name in dbg else "Internal"
            return nc.dram_tensor(name, list(shape), dt, kind=kind).ap()

        self.Xd = din("x", [S, D])
        wdq = din("wdq", [D, 512]); wuq = din("wuq", [512, D]); wqr = din("wqr", [512, D])
        wkr = din("wkr", [D, D]); wdkv = din("wdkv", [D, 512]); wuk = din("wuk", [512, D])
        wuv = din("wuv", [512, D]); wo = din("wo", [D, D])
        bdqc_d = din("bdq_c", [128, 4]); bdkvc_d = din("bdkv_c", [128, 4])
        buqm_d = din("buq_m", [128, D]); bqrm_d = din("bqr_m", [128, D])
        bkrm_d = din("bkr_m", [128, D]); bukm_d = din("buk_m", [128, D])
        buvm_d = din("buv_m", [128, D]); self.wobm_d = din("wob_m", [128, D])
        wqkv2 = din("wqkv2", [2, D, 1536]); wo2 = din("wo2", [2, 512, D])
        w1p = din("w1p", [2, D, F1P]); w2p = din("w2p", [2, HIDP, D])
        b1c_d = din("b1c", [2, 128, F1P // 128]); b2m_d = din("b2m", [2, 128, D])
        lng_d = din("lng_c", [2, 128, 8]); lnb_d = din("lnb_c", [2, 128, 8])
        qsm_d = din("qsm", [2, 128, 512]); ksm_d = din("ksm", [2, 128, 512])
        identbf_d = din("ident_bf", [128, 128], BF16)
        identf_d = din("ident_f32", [128, 128])
        mlacos_d = din("mla_cos", [S, 512]); mlasin_d = din("mla_sin", [S, 512])
        xpcq_d = din("xp_cq", [S, 512]); xpsq_d = din("xp_sq", [S, 512])
        xpck_d = din("xp_ck", [S, 512]); xpsk_d = din("xp_sk", [S, 512])
        mask_d = din("mask640", [128, 640], BF16)

        self.XN = dscratch("XN", [S, D])
        self.XRES = dscratch("XRES", [S, D])
        self.AOUT = dscratch("AOUT", [S, D])
        self.KH = dscratch("KH", [128, 8, S], BF16)
        self.QH = dscratch("QH", [128, 8, S], BF16)
        self.VH = dscratch("VH", [128, NT, 1040], BF16)
        self.OUT = nc.dram_tensor("out", [S, D], F32, kind="ExternalOutput").ap()

        sb = lambda n, sh, dt=F32: es.enter_context(nc.sbuf_tensor(n, sh, dt))
        ps = lambda n, sh, dt=F32: es.enter_context(nc.psum_tensor(n, sh, dt))

        self.identb = sb("identb", [128, 128], BF16)
        self.identf = sb("identf", [128, 128])
        self.ones_f = sb("ones_f", [128, 128])
        self.ones_b = sb("ones_b", [128, 16], BF16)
        self.mask = sb("maskc", [128, 640], BF16)
        self.bdqc = sb("bdqc", [128, 4]); self.bdkvc = sb("bdkvc", [128, 4])
        self.b2m = sb("b2m_s", [128, D])
        self.b1c = sb("b1c_s", [128, F1P // 128])
        self.lng = sb("lng", [128, 8]); self.lnb = sb("lnb", [128, 8])
        self.qsm = sb("qsm_s", [128, 512]); self.ksm = sb("ksm_s", [128, 512])
        self.stat = sb("stat", [128, 96])
        self.epsb = sb("epsb", [128, 1])
        self.scr64 = sb("scr64", [128, 512])
        self.tmph1 = sb("tmph1", [128, 1024], BF16)
        self.xsb = [sb("xsb0", [128, D]), sb("xsb1", [128, D])]
        self.xnbb = [sb("xnbb0", [128, D], BF16), sb("xnbb1", [128, D], BF16)]
        self.AR = sb("AR", [128, ARENA_COLS], BF16)

        self.sp = [ps("sp0", [128, 1024]), ps("sp1", [128, 1024])]
        self.o0 = ps("o0", [128, 512]); self.o1 = ps("o1", [128, 512])
        self.u0 = ps("u0", [128, 512])
        self.TB = ps("TBp", [128, 1024], BF16)
        self.TBs = [self.TB[:], self.u0[:].bitcast(BF16)]
        self._tbevs = [None, None]
        self._tbk = 0

        ar = self.AR

        def abf(off, n):
            return ar[:, off:off + n]

        def af32(off, n):
            return ar[:, off:off + 2 * n].bitcast(F32)

        self.abf, self.af32 = abf, af32
        self.tsq = af32(34880, 1024)
        self.buqm = abf(36928, 1024)
        self.bqrm = abf(37952, 1024)
        self.bkrm = abf(38976, 1024)
        self.bukm = abf(40000, 1024)
        self.buvm = abf(41024, 1024)

        ev_c = []
        ev_c.append(self.dma("sync", self.identb[:], identbf_d[:, :], "c0"))
        ev_c.append(self.dma("sync", self.identf[:], identf_d[:, :], "c1"))
        ev_c.append(self.dma("sync", self.mask[:], mask_d[:, :], "c2"))
        ev_c.append(self.dma("sync", self.bdqc[:], bdqc_d[:, :], "c3"))
        ev_c.append(self.dma("sync", self.bdkvc[:], bdkvc_d[:, :], "c4"))
        for ci, (t, s_) in enumerate(((self.buqm, buqm_d), (self.bqrm, bqrm_d),
                                      (self.bkrm, bkrm_d), (self.bukm, bukm_d),
                                      (self.buvm, buvm_d))):
            ev_c.append(self.dma("gpsimd", t, s_[:, :], f"c{5 + ci}"))
        self.memset("gpsimd", self.ones_f[:], 1.0)
        self.memset("gpsimd", self.ones_b[:], 1.0)
        self.memset("gpsimd", self.epsb[:], 1e-5)
        self.memset("gpsimd", self.stat[:, 8:24], 0.0)
        self.memset("gpsimd", self.stat[:, 90:94], 0.0, rt=True)
        self.barrier()

        W_OFF = 52000
        self.wdkv_sb = abf(W_OFF, 4096)
        self.wuk_sb = abf(W_OFF + 4096, 4096)
        self.wuv_sb = abf(W_OFF + 8192, 4096)
        self.wkr_sb = abf(W_OFF + 12288, 8192)
        self.wdq_sb = abf(W_OFF, 4096)
        self.wuq_sb = abf(W_OFF + 4096, 4096)
        self.wqr_sb = abf(W_OFF + 8192, 4096)

        if "p0" in phases or "p1" in phases:
            self._p0_p1(wdq, wuq, wqr, wkr, wdkv, wuk, wuv,
                        mlacos_d, mlasin_d)
        if "p2" in phases:
            self._p2(wo)
        if "f0" in phases:
            self._ffn(0, wqkv2, wo2, w1p, w2p, b1c_d, b2m_d, lng_d, lnb_d,
                      qsm_d, ksm_d, xpcq_d, xpsq_d, xpck_d, xpsk_d)
        if "f1" in phases:
            self._ffn(1, wqkv2, wo2, w1p, w2p, b1c_d, b2m_d, lng_d, lnb_d,
                      qsm_d, ksm_d, xpcq_d, xpsq_d, xpck_d, xpsk_d)
        if "p5" in phases:
            self._p5()
        self.barrier()

        block = es.enter_context(nc.Block())
        for name, dec in (("tensor", block.tensor), ("vector", block.vector),
                          ("scalar", block.scalar), ("gpsimd", block.gpsimd),
                          ("sync", block.sync)):
            def body(e, name=name):
                for fn in self.q[name]:
                    fn(e)
            dec(body)
        es.close()
        return nc

    def _tm2fm(self, src_f32, par, dst3, ttl):
        ev_bb = self.cp("vector", self.xnbb[par][:], src_f32, sig=True)
        TBk, kk = self.tb_get()
        self.wait("tensor", ev_bb, self._tbevs[kk])
        ev_tr = None
        for c in range(8):
            ev_tr = self.tr(TBk[:, 128 * c:128 * c + 128],
                            self.xnbb[par][:, 128 * c:128 * c + 128],
                            self.identb[:], sig=(c == 7))
        self.wait("vector", ev_tr)
        src3 = TBk.rearrange("p (c t) -> p c t", t=128)
        ev = self.cp("vector", dst3[:, :, 128 * ttl:128 * ttl + 128], src3, sig=True)
        self._tbevs[kk] = ev
        return ev

    def _p0_p1(self, wdq, wuq, wqr, wkr, wdkv, wuk, wuv, mlacos_d, mlasin_d):
        abf, af32 = self.abf, self.af32
        st = self.stat
        xng = [abf(0, 4096), abf(4096, 4096)]
        ctkv = abf(8192, 2048)
        Kstage = abf(10240, 4096)
        Vstage = [abf(14336, 1056), abf(15392, 1056)]
        Ktm = af32(16448, 1024)
        Rtm = af32(18496, 1024)
        Ktb2 = [abf(20544, 1024), abf(42048, 1024)]
        ksq = af32(21568, 1024)
        t1 = af32(23616, 512); t2 = af32(24640, 512); t3 = af32(25664, 512)
        ctab = [af32(26688, 512), af32(27712, 512)]
        stab = [af32(28736, 512), af32(29760, 512)]
        xnb = [af32(30784, 1024), af32(32832, 1024)]

        ev_w = []
        for wi, (wsb, wd) in enumerate(((self.wdkv_sb, wdkv), (self.wuk_sb, wuk),
                                        (self.wuv_sb, wuv), (self.wkr_sb, wkr))):
            wsrc = wd.rearrange("(c p) f -> p c f", p=128)
            ev_w.append(self.dma("gpsimd", wsb, wsrc, f"w{wi}"))

        ev_vst = [None, None]
        ev_xnst = [None, None]
        gate_xs = [None, None]
        gate_tab = [None, None]
        ev_kdma = None
        pb = [None] * 4
        pbo = [None, None]
        self._kstev = None
        pend_k = [None]
        evkd = [None]

        def flush_k():
            if pend_k[0] is None:
                return
            ttl_p, kp, evb = pend_k[0]
            pend_k[0] = None
            TBk, kk = self.tb_get()
            self.wait("tensor", evb, self._tbevs[kk])
            ev_tr = None
            for c in range(8):
                ev_tr = self.tr(TBk[:, 128 * c:128 * c + 128],
                                Ktb2[kp][:, 128 * c:128 * c + 128],
                                self.identb[:], sig=(c == 7))
            self.wait("vector", ev_tr, evkd[0])
            kd3 = Kstage.rearrange("p (c t) -> p c t", t=512)
            self._kstev = self.cp(
                "vector", kd3[:, :, 128 * ttl_p:128 * ttl_p + 128],
                TBk.rearrange("p (c t) -> p c t", t=128), sig=True)
            self._tbevs[kk] = self._kstev
        ev_ctkv_e = [None] * 4
        for par in range(2):
            v66 = Vstage[par].rearrange("p (h c) -> p h c", c=66)
            ev_vones = self.memset("gpsimd", v66[:, :, 64:65], 1.0, sig=True)

        for g in range(NG):
            gpar = g % 2
            for ttl in range(4):
                tt = 4 * g + ttl
                par = tt % 2
                self.wait("sync", *(gate_xs[par] or ()))
                ev_x = self.dma("sync", self.xsb[par][:],
                                self.Xd[128 * tt:128 * tt + 128, :], f"xs{par}")
                self.wait("vector", ev_x)
                self.wait("vector", ev_x)
                self.fence("vector", "ve")
                ev_sq = self.tt("vector", self.tsq[:], self.xsb[par][:],
                                self.xsb[par][:], ALU.mult, sig=True, rt=True)
                self.red("vector", st[:, 0:1], self.xsb[par][:], ALU.add)
                self.wait("vector", ev_sq)
                self.red("vector", st[:, 1:2], self.tsq[:], ALU.add)
                self.ts("vector", st[:, 2:3], st[:, 0:1], 1.0 / D, rt=True)
                self.ts("vector", st[:, 3:4], st[:, 1:2], 1.0 / D, rt=True)
                self.tt("vector", st[:, 4:5], st[:, 2:3], st[:, 2:3], ALU.mult, rt=True)
                ev_var = self.tt("vector", st[:, 3:4], st[:, 3:4], st[:, 4:5],
                                 ALU.subtract, sig=True, rt=True)
                self.wait("scalar", ev_var)
                self.act(st[:, 5:6], st[:, 3:4], AF.Ln, bias=self.epsb[:, 0:1], rt=True)
                ev_rstd = self.act(st[:, 6:7], st[:, 5:6], AF.Exp, scale=-0.5, sig=True)
                self.wait("vector", ev_rstd, ev_xnst[par])
                ev_xn = self.ts("vector", xnb[par], self.xsb[par][:], st[:, 2:3],
                                st[:, 6:7], ALU.subtract, ALU.mult, sig=True)
                self.swait("vector", ev_xn)
                gate_xs[par] = (ev_xn, ev_sq)
                self.wait("sync", ev_xn)
                ev_xnst[par] = self.dma("sync", self.XN[128 * tt:128 * tt + 128, :],
                                        xnb[par], f"xn{par}")
                dst3 = xng[gpar].rearrange("p (c t) -> p c t", t=512)
                self.fence("vector", "pe")
                xng_ev = self._tm2fm(xnb[par], par, dst3, ttl)

            self.wait("tensor", xng_ev, *ev_w)
            slots = [self.sp[0][:, 0:512], self.sp[0][:, 512:1024],
                     self.sp[1][:, 0:512], self.sp[1][:, 512:1024]]
            ev_ctkv = [None] * 4
            for fc in range(4):
                self.wait("tensor", pb[fc])
                for dc in range(8):
                    ev_ctkv[fc] = self.mm(
                        slots[fc],
                        self.wdkv_sb[:, 512 * dc + 128 * fc:512 * dc + 128 * fc + 128],
                        xng[gpar][:, 512 * dc:512 * dc + 512],
                        start=(dc == 0), stop=(dc == 7), sig=(dc == 7))
            ctkv_b = ctkv
            for fc in range(4):
                self.wait("scalar", ev_ctkv[fc])
                ev_ctkv_e[fc] = self.act(ctkv_b[:, 512 * fc:512 * fc + 512], slots[fc],
                                         AF.Identity, bias=self.bdkvc[:, fc:fc + 1], sig=True)
                pb[fc] = ev_ctkv_e[fc]
            for ttl in range(4):
                tt = 4 * g + ttl
                par = tt % 2
                self.wait("sync", gate_tab[par])
                ev_ct = self.dma("sync", ctab[par],
                                 mlacos_d[128 * tt:128 * tt + 128, :], f"tca{par}")
                ev_st2 = self.dma("sync", stab[par],
                                  mlasin_d[128 * tt:128 * tt + 128, :], f"tcb{par}")
                self.wait("tensor", ev_ctkv_e[3], pbo[0], pbo[1])
                ev_ktc = None
                for fh, bank in ((0, self.o0), (1, self.o1)):
                    for kc in range(4):
                        ev_ktc = self.mm(
                            bank[:, :],
                            ctkv_b[:, 512 * kc + 128 * ttl:512 * kc + 128 * ttl + 128],
                            self.wuk_sb[:, 1024 * kc + 512 * fh:1024 * kc + 512 * fh + 512],
                            start=(kc == 0), stop=(kc == 3), sig=(fh == 1 and kc == 3))
                self.wait("tensor", pb[0], pb[1])
                ev_ktr = None
                for fh in range(2):
                    for dc in range(8):
                        ev_ktr = self.mm(
                            self.sp[0][:, 512 * fh:512 * fh + 512],
                            xng[gpar][:, 512 * dc + 128 * ttl:512 * dc + 128 * ttl + 128],
                            self.wkr_sb[:, 1024 * dc + 512 * fh:1024 * dc + 512 * fh + 512],
                            start=(dc == 0), stop=(dc == 7), sig=(fh == 1 and dc == 7))
                self.wait("tensor", pb[2], pb[3])
                ev_v = None
                for fh in range(2):
                    for kc in range(4):
                        ev_v = self.mm(
                            self.sp[1][:, 512 * fh:512 * fh + 512],
                            ctkv_b[:, 512 * kc + 128 * ttl:512 * kc + 128 * ttl + 128],
                            self.wuv_sb[:, 1024 * kc + 512 * fh:1024 * kc + 512 * fh + 512],
                            start=(kc == 0), stop=(kc == 3), sig=(fh == 1 and kc == 3))
                flush_k()
                self.wait("vector", ev_ktc)
                self.tt("vector", Ktm[:, 0:512], self.o0[:, :], self.bukm[:, 0:512], ALU.add)
                evko = self.tt("vector", Ktm[:, 512:1024], self.o1[:, :],
                               self.bukm[:, 512:1024], ALU.add, sig=True)
                pbo[0] = pbo[1] = evko
                self.wait("vector", ev_ktr)
                self.tt("vector", Rtm[:, 0:512], self.sp[0][:, 0:512], self.bkrm[:, 0:512], ALU.add)
                ev_rtm = self.tt("vector", Rtm[:, 512:1024], self.sp[0][:, 512:1024],
                                 self.bkrm[:, 512:1024], ALU.add, sig=True)
                pb[0] = pb[1] = ev_rtm
                self.wait("vector", ev_v, ev_vst[par], ev_vones)
                v3 = Vstage[par].rearrange("p (h c) -> p h c", c=66)
                self.tt("vector", v3[:, 0:8, 0:64], self.sp[1][:, 0:512],
                        self.buvm[:, 0:512], ALU.add)
                ev_vs = self.tt("vector", v3[:, 8:16, 0:64], self.sp[1][:, 512:1024],
                                self.buvm[:, 512:1024], ALU.add, sig=True)
                pb[2] = pb[3] = ev_vs
                Re = Rtm.rearrange("p (n two) -> p n two", two=2)[:, :, 0:1]
                Ro = Rtm.rearrange("p (n two) -> p n two", two=2)[:, :, 1:2]
                Ke = Ktm.rearrange("p (n two) -> p n two", two=2)[:, :, 0:1]
                Ko = Ktm.rearrange("p (n two) -> p n two", two=2)[:, :, 1:2]
                self.wait("vector", ev_ct, ev_st2, ev_rtm)
                self.fence("vector", "ve")
                self.tt("vector", t1, Re, ctab[par], ALU.mult)
                ev_g1 = self.tt("vector", t2, Ro, stab[par], ALU.mult, sig=True)
                ev_g2 = self.tt("vector", t3, Re, stab[par], ALU.mult, sig=True)
                self.wait("vector", ev_g1)
                selft1 = self.tt("vector", t1, t1, t2, ALU.subtract, sig=True)
                ev_ke = self.tt("vector", Ke, Ke, t1, ALU.add, sig=True)
                self.wait("vector", selft1)
                ev_g3 = self.tt("vector", t2, Ro, ctab[par], ALU.mult, sig=True)
                self.wait("vector", ev_g2, ev_g3)
                gate_tab[par] = ev_g3
                self.tt("vector", t2, t2, t3, ALU.add)
                ev_ko = self.tt("vector", Ko, Ko, t2, ALU.add, sig=True)
                self.wait("vector", ev_ke, ev_ko)
                ev_ksq = self.tt("vector", ksq, Ktm, Ktm, ALU.mult, sig=True)
                self.wait("vector", ev_ksq)
                self.red("vector", st[:, 24:40],
                         ksq.rearrange("p (h d) -> p h d", d=64), ALU.add)
                self.tt("vector", st[:, 8:24], st[:, 8:24], st[:, 24:40], ALU.max, rt=True)
                ev_ktb = self.cp("vector", Ktb2[ttl % 2], Ktm, sig=True)
                pend_k[0] = (ttl, ttl % 2, ev_ktb)
                self.wait("sync", ev_vs)
                vv3 = Vstage[par].rearrange("p (h c) -> p h c", c=66)
                ev_vst[par] = self.dma("sync", self.VH[:, tt, :],
                                       vv3[:, :, 0:65], f"vs{par}")
            flush_k()
            self.wait("sync", self._kstev)
            ev_kdma = self.dma("sync", self.KH[:, :, 512 * g:512 * g + 512],
                               Kstage.rearrange("p (c u t) -> p c u t", c=8, u=4),
                               "kgrp")
            evkd[0] = ev_kdma
        ev_r0 = self.red("vector", st[:, 49:50], st[:, 8:24], ALU.max)
        self.wait("tensor", ev_r0, pbo[0], self._tbevs[1])
        ev_t = self.tr(self.u0[0:1, 0:128], st[:, 49:50], self.identf[:], sig=True)
        self.wait("vector", ev_t)
        self.red("vector", st[0:1, 48:49], self.u0[0:1, 0:128], ALU.max)

        self.barrier()

        ev_w = []
        for wi, (wsb, wd) in enumerate(((self.wdq_sb, wdq), (self.wuq_sb, wuq),
                                        (self.wqr_sb, wqr))):
            wsrc = wd.rearrange("(c p) f -> p c f", p=128)
            ev_w.append(self.dma("gpsimd", wsb, wsrc, f"w{wi}"))
        ctq_b = ctkv
        Qtm, qsq = Ktm, ksq
        gate_xs = [None, None]
        gate_tab = [None, None]
        ev_qdma = None
        pend_k[0] = None
        evkd[0] = None
        self._kstev = None
        pb = [None] * 4
        pbo = [None, None]
        for g in range(NG):
            gpar = g % 2
            self.memset("vector", st[:, 50:66], 0.0)
            for ttl in range(4):
                tt = 4 * g + ttl
                par = tt % 2
                self.wait("sync", gate_xs[par])
                ev_x = self.dma("sync", self.xsb[par][:],
                                self.XN[128 * tt:128 * tt + 128, :], f"xs{par}")
                self.wait("vector", ev_x)
                self.fence("vector", "pe")
                dst3 = xng[gpar].rearrange("p (c t) -> p c t", t=512)
                gate_xs[par] = self._tm2fm(self.xsb[par][:], par, dst3, ttl)
            self.wait("tensor", gate_xs[0], gate_xs[1], *ev_w)
            ev_ctq = [None] * 4
            slots = [self.sp[0][:, 0:512], self.sp[0][:, 512:1024],
                     self.sp[1][:, 0:512], self.sp[1][:, 512:1024]]
            for fc in range(4):
                self.wait("tensor", pb[fc])
                for dc in range(8):
                    ev_ctq[fc] = self.mm(
                        slots[fc],
                        self.wdq_sb[:, 512 * dc + 128 * fc:512 * dc + 128 * fc + 128],
                        xng[gpar][:, 512 * dc:512 * dc + 512],
                        start=(dc == 0), stop=(dc == 7), sig=(dc == 7))
            for fc in range(4):
                self.wait("scalar", ev_ctq[fc])
                pb[fc] = self.act(ctq_b[:, 512 * fc:512 * fc + 512], slots[fc],
                                  AF.Identity, bias=self.bdqc[:, fc:fc + 1], sig=True)
            for ttl in range(4):
                tt = 4 * g + ttl
                par = tt % 2
                self.wait("sync", gate_tab[par])
                ev_ct = self.dma("sync", ctab[par],
                                 mlacos_d[128 * tt:128 * tt + 128, :], f"tca{par}")
                ev_st2 = self.dma("sync", stab[par],
                                  mlasin_d[128 * tt:128 * tt + 128, :], f"tcb{par}")
                self.wait("tensor", pb[3], pbo[0], pbo[1])
                ev_qtc = None
                for fh, bank in ((0, self.o0), (1, self.o1)):
                    for kc in range(4):
                        ev_qtc = self.mm(
                            bank[:, :],
                            ctq_b[:, 512 * kc + 128 * ttl:512 * kc + 128 * ttl + 128],
                            self.wuq_sb[:, 1024 * kc + 512 * fh:1024 * kc + 512 * fh + 512],
                            start=(kc == 0), stop=(kc == 3), sig=(fh == 1 and kc == 3))
                self.wait("tensor", pb[0], pb[1])
                ev_qr = None
                for fh in range(2):
                    for kc in range(4):
                        ev_qr = self.mm(
                            self.sp[0][:, 512 * fh:512 * fh + 512],
                            ctq_b[:, 512 * kc + 128 * ttl:512 * kc + 128 * ttl + 128],
                            self.wqr_sb[:, 1024 * kc + 512 * fh:1024 * kc + 512 * fh + 512],
                            start=(kc == 0), stop=(kc == 3), sig=(fh == 1 and kc == 3))
                flush_k()
                self.wait("vector", ev_qtc)
                self.tt("vector", Qtm[:, 0:512], self.o0[:, :], self.buqm[:, 0:512], ALU.add)
                evqo = self.tt("vector", Qtm[:, 512:1024], self.o1[:, :],
                               self.buqm[:, 512:1024], ALU.add, sig=True)
                pbo[0] = pbo[1] = evqo
                self.wait("vector", ev_qr)
                self.tt("vector", Rtm[:, 0:512], self.sp[0][:, 0:512], self.bqrm[:, 0:512], ALU.add)
                ev_rtm = self.tt("vector", Rtm[:, 512:1024], self.sp[0][:, 512:1024],
                                 self.bqrm[:, 512:1024], ALU.add, sig=True)
                pb[0] = pb[1] = ev_rtm
                Re = Rtm.rearrange("p (n two) -> p n two", two=2)[:, :, 0:1]
                Ro = Rtm.rearrange("p (n two) -> p n two", two=2)[:, :, 1:2]
                Qe = Qtm.rearrange("p (n two) -> p n two", two=2)[:, :, 0:1]
                Qo = Qtm.rearrange("p (n two) -> p n two", two=2)[:, :, 1:2]
                self.wait("vector", ev_ct, ev_st2, ev_rtm)
                self.fence("vector", "ve")
                self.tt("vector", t1, Re, ctab[par], ALU.mult)
                ev_g1 = self.tt("vector", t2, Ro, stab[par], ALU.mult, sig=True)
                ev_g2 = self.tt("vector", t3, Re, stab[par], ALU.mult, sig=True)
                self.wait("vector", ev_g1)
                selft1 = self.tt("vector", t1, t1, t2, ALU.subtract, sig=True)
                self.tt("vector", Qe, Qe, t1, ALU.add)
                self.wait("vector", selft1)
                ev_g3 = self.tt("vector", t2, Ro, ctab[par], ALU.mult, sig=True)
                self.wait("vector", ev_g2, ev_g3)
                gate_tab[par] = ev_g3
                self.tt("vector", t2, t2, t3, ALU.add)
                ev_qo = self.tt("vector", Qo, Qo, t2, ALU.add, sig=True)
                self.wait("vector", ev_qo)
                ev_qsq = self.tt("vector", qsq, Qtm, Qtm, ALU.mult, sig=True)
                self.wait("vector", ev_qsq)
                self.red("vector", st[:, 24:40],
                         qsq.rearrange("p (h d) -> p h d", d=64), ALU.add)
                self.tt("vector", st[:, 50:66], st[:, 50:66], st[:, 24:40], ALU.max, rt=True)
                ev_qtb = self.cp("vector", Ktb2[ttl % 2], Qtm, sig=True)
                pend_k[0] = (ttl, ttl % 2, ev_qtb)
            flush_k()
            self.wait("sync", self._kstev)
            ev_qdma = self.dma("sync", self.QH[:, :, 512 * g:512 * g + 512],
                               Kstage.rearrange("p (c u t) -> p c u t", c=8, u=4),
                               "kgrp")
            evkd[0] = ev_qdma
            ev_r = self.red("vector", st[:, 49:50], st[:, 50:66], ALU.max, sig=True)
            self.wait("tensor", ev_r, self._tbevs[1])
            ev_t = self.tr(self.u0[0:1, 0:128], st[:, 49:50], self.identf[:], sig=True)
            self.wait("vector", ev_t)
            self.red("vector", st[0:1, 67:68], self.u0[0:1, 0:128], ALU.max)
            ev_p = self.tt("vector", st[0:1, 68:69], st[0:1, 67:68], st[0:1, 48:49],
                           ALU.mult, sig=True, rt=True)
            self.wait("scalar", ev_p)
            self.act(st[0:1, 69:70], st[0:1, 68:69], AF.Ln, rt=True)
            ev_sq = self.act(st[0:1, 70:71], st[0:1, 69:70], AF.Exp, scale=0.5, sig=True)
            self.wait("vector", ev_sq)
            ev_m = self.ts("vector", st[0:1, 71:72], st[0:1, 70:71], -ALPHA, sig=True)
            self.wait("tensor", ev_m)
            ev_b = self.mm(self.u0[:, 0:1], self.ones_f[0:1, 0:128],
                           st[0:1, 71:72], sig=True)
            self.wait("vector", ev_b)
            self._u0ev = self.cp("vector", st[:, 40 + g:41 + g], self.u0[:, 0:1],
                                 sig=True)
            self._tbevs[1] = self._u0ev
        self.barrier()

    def _p2(self, wo):
        abf, af32 = self.abf, self.af32
        st = self.stat
        kc_sb = [abf(0, 4096), abf(4096, 4096)]
        qc_sb = [abf(8192, 4096), abf(12288, 4096)]
        vp_sb = [abf(16384, 4160), abf(20544, 4160)]
        expS = [abf(24704, 1024), abf(25728, 1024)]
        O_fm = abf(26752, 32768)
        asb = [af32(59520, 1024), af32(61568, 1024)]
        wo_sb = abf(63616, 8192)
        wobm = af32(71808, 1024)
        rb = af32(73856, 512)

        ev_wo = self.dma("gpsimd", wo_sb, wo.rearrange("(c p) f -> p c f", p=128), "w0")
        ev_wobm = self.dma("sync", wobm, self.wobm_d[:, :], "cw")

        def load_hp(hp):
            par = hp % 2
            evk = self.dma("sync", kc_sb[par], self.KH[:, hp, :], f"kc{par}")
            evq = self.dma("sync", qc_sb[par], self.QH[:, hp, :], f"qc{par}")
            evv = self.dma("sync", vp_sb[par],
                           self.VH[:, :, 130 * hp:130 * hp + 130], f"vc{par}")
            return (evk, evq, evv)

        ev_oh = [None, None]
        evE_prev = [None, None]
        ofin = [None, None]
        pend = load_hp(0)
        for hp in range(8):
            par = hp % 2
            evk, evq, evv = pend
            if hp < 7:
                self.fence("sync", "pe")
                pend = load_hp(hp + 1)
            self.wait("tensor", evk, evq, evv)
            kc, qc, vp = kc_sb[par], qc_sb[par], vp_sb[par]
            for g in range(NG):
                self.wait("tensor", evE_prev[0], evE_prev[1], ofin[0], ofin[1])
                evS = [None] * 32
                evE = [None] * 32
                evPV = [None] * 32

                def pv(kt):
                    self.wait("tensor", evE[kt])
                    self.mm(self.o0[0:65, :], vp[:, 130 * kt:130 * kt + 65],
                            expS[kt % 2][:, 0:512],
                            start=(kt == 0), stop=(kt == 31))
                    evPV[kt] = self.mm(self.o1[0:65, :], vp[:, 130 * kt + 65:130 * kt + 130],
                                       expS[kt % 2][:, 512:1024],
                                       start=(kt == 0), stop=(kt == 31), sig=True)

                for kt in range(32):
                    if kt >= 2:
                        self.wait("tensor", evE[kt - 2])
                    self.mm(self.sp[kt % 2][:, 0:512],
                            kc[0:64, 128 * kt:128 * kt + 128],
                            qc[0:64, 512 * g:512 * g + 512], tp=(0, 0))
                    evS[kt] = self.mm(self.sp[kt % 2][:, 512:1024],
                                      kc[64:128, 128 * kt:128 * kt + 128],
                                      qc[64:128, 512 * g:512 * g + 512],
                                      tp=(64, 0), sig=True)
                    if kt >= 1:
                        pv(kt - 1)
                    self.wait("scalar", evS[kt])
                    if kt >= 2:
                        self.wait("scalar", evPV[kt - 2])
                    evE[kt] = self.act(expS[kt % 2][:], self.sp[kt % 2][:, 0:1024],
                                       AF.Exp, bias=st[:, 40 + g:41 + g],
                                       scale=ALPHA, sig=True)
                pv(31)
                evE_prev[0], evE_prev[1] = evE[30], evE[31]
                for hodd, bank in ((0, self.o0), (1, self.o1)):
                    self.wait("vector", evPV[31])
                    evr = self.recip(self.scr64[64:65, :], bank[64:65, :], sig=True)
                    self.wait("tensor", evr)
                    evb = self.mm(self.u0[0:64, :], self.ones_f[64:65, 0:64],
                                  self.scr64[64:65, :], tp=(64, 0), sig=True)
                    self.wait("vector", evb)
                    evrb = self.cp("vector", rb[0:64, :], self.u0[0:64, :], sig=True)
                    cols = slice(4096 * hp + 512 * g, 4096 * hp + 512 * g + 512)
                    if hodd == 0:
                        ofin[0] = self.tt("vector", O_fm[0:64, cols], bank[0:64, :],
                                          rb[0:64, :], ALU.mult, sig=True)
                    else:
                        hpar = g % 2
                        self.wait("vector", ev_oh[hpar])
                        evh = self.tt("vector",
                                      self.tmph1[0:64, 512 * hpar:512 * hpar + 512],
                                      bank[0:64, :], rb[0:64, :], ALU.mult, sig=True)
                        self.wait("gpsimd", evh)
                        ofin[1] = evh
                        ev_oh[hpar] = self.dma(
                            "gpsimd", O_fm[64:128, cols],
                            self.tmph1[0:64, 512 * hpar:512 * hpar + 512], f"oh{hpar}")
        self.wait("tensor", ev_oh[0], ev_oh[1])
        self.fence("tensor", "ve")
        self.fence("tensor", "se")
        self.wait("tensor", ev_wo)
        ev_ast = [None, None]
        ev_e = [None, None]
        for tt in range(NT):
            par = tt % 2
            psum = self.sp[par]
            if ev_e[par] is not None:
                self.wait("tensor", ev_e[par])
            ev = None
            for fh in range(2):
                for c in range(8):
                    ev = self.mm(psum[:, 512 * fh:512 * fh + 512],
                                 O_fm[:, 4096 * c + 128 * tt:4096 * c + 128 * tt + 128],
                                 wo_sb[:, 1024 * c + 512 * fh:1024 * c + 512 * fh + 512],
                                 start=(c == 0), stop=(c == 7),
                                 sig=(fh == 1 and c == 7))
            self.wait("vector", ev, ev_ast[par], ev_wobm)
            self.tt("vector", asb[par][:, 0:512], psum[:, 0:512],
                    wobm[:, 0:512], ALU.add)
            ev_e[par] = self.tt("vector", asb[par][:, 512:1024], psum[:, 512:1024],
                                wobm[:, 512:1024], ALU.add, sig=True)
            self.wait("sync", ev_e[par])
            ev_ast[par] = self.dma("sync", self.AOUT[128 * tt:128 * tt + 128, :],
                                   asb[par], f"ao{par}")
        self.barrier()

    def _ffn(self, l, wqkv2, wo2, w1p, w2p, b1c_d, b2m_d, lng_d, lnb_d,
             qsm_d, ksm_d, xpcq_d, xpsq_d, xpck_d, xpsk_d):
        abf, af32 = self.abf, self.af32
        st = self.stat
        XIN = self.XN if l == 0 else self.XRES
        self.dma("sync", self.b1c[:], b1c_d[l], "lc0")
        self.dma("sync", self.b2m[:], b2m_d[l], "lc1")
        self.dma("sync", self.lng[:], lng_d[l], "lc2")
        self.dma("sync", self.lnb[:], lnb_d[l], "lc3")
        ev_qs = self.dma("sync", self.qsm[:], qsm_d[l], "lc4")
        ev_ks = self.dma("sync", self.ksm[:], ksm_d[l], "lc5")

        x_fm = abf(0, 32832)
        aa = abf(0, 22528)
        w1blk = [abf(22528, 4096), abf(26624, 4096)]
        gelu_t = abf(30720, 512)
        xr2 = [af32(22528, 1024), af32(26624, 1024)]
        wqkv_sb = abf(32832, 12288)
        k_ring = abf(45120, 4096)
        v_ring = abf(49216, 4096)
        q_fm = abf(53312, 2048)
        O_fmw = abf(55360, 2048)
        wo2_sb = abf(57408, 4096)
        cqb = af32(61504, 512)
        sqb = af32(62528, 512)
        ckb = af32(63552, 512)
        skb = af32(64576, 512)
        ln_fm = abf(32832, 32768)
        qtm2 = [af32(65600, 512), af32(80448, 512)]
        qsq2 = af32(66624, 512)
        qrot2 = [af32(67648, 512), af32(81472, 512)]
        qtb2 = [abf(68672, 512), abf(82496, 512)]
        Sm = [[af32(69184, 640), af32(70464, 640)],
              [af32(83008, 640), af32(84288, 640)]]
        pbf = [[abf(71744, 640), abf(72384, 640)],
               [abf(85568, 640), abf(86208, 640)]]
        pT = [abf(73024, 640), abf(73664, 640)]
        t1 = af32(74304, 512)
        t2 = af32(75328, 512)
        xr = [af32(76352, 1024), af32(78400, 1024)]
        hsq8 = abf(65600, 4096)
        srow = af32(69696, 1024)
        rstdbc = af32(71744, 512)
        omegabc = af32(72768, 512)
        t1c = af32(73792, 512)
        t2c = af32(74816, 512)
        w2sb = abf(65600, 22528)

        def build_xfm(src_dram):
            self._ds("fA")
            z3 = x_fm.rearrange("p (c t) -> p c t", t=4104)
            self.fence("gpsimd", "pe")
            ev_z = self.memset("gpsimd", z3[:, :, 0:1], 0.0, sig=True)
            self.wait("vector", ev_z)
            gate_xs = [None, None]
            for tt in range(NT):
                par = tt % 2
                self.wait("sync", gate_xs[par])
                ev_x = self.dma("sync", self.xsb[par][:],
                                src_dram[128 * tt:128 * tt + 128, :], f"xs{par}")
                self.wait("vector", ev_x)
                self.fence("vector", "pe")
                ev_bb = self.cp("vector", self.xnbb[par][:], self.xsb[par][:],
                                sig=True)
                gate_xs[par] = ev_bb
                TBk, kk = self.tb_get()
                self.wait("tensor", ev_bb, self._tbevs[kk])
                ev_tr = None
                for c in range(8):
                    ev_tr = self.tr(TBk[:, 128 * c:128 * c + 128],
                                    self.xnbb[par][:, 128 * c:128 * c + 128],
                                    self.identb[:], sig=(c == 7))
                self.wait("vector", ev_tr)
                self._tbevs[kk] = self.cp(
                    "vector", z3[:, :, 1 + 128 * tt:1 + 128 * tt + 128],
                    TBk.rearrange("p (c t) -> p c t", t=128), sig=True)
            self._ds(None)

        build_xfm(XIN)
        ev_wqkv = self.dma("gpsimd", wqkv_sb,
                           wqkv2[l].rearrange("(c p) f -> p c f", p=128), "w1")
        if WO2_SYNC_CAST:
            wo2_f32 = self.af32(80448, 4096)
            evt = self.dma("sync", wo2_f32,
                           wo2[l].rearrange("(c p) f -> p c f", p=128), "w2f")
            self.wait("vector", evt)
            ev_wo2 = self.cp("vector", wo2_sb, wo2_f32, sig=True)
        else:
            ev_wo2 = self.dma("gpsimd", wo2_sb,
                              wo2[l].rearrange("(c p) f -> p c f", p=128), "w2")
        self.barrier()

        self._gate_xt = None
        ev_xstw = [None, None]
        bankev = [None, None, None, None]
        oev = [None, None]
        pbf_war = [[None, None], [None, None]]
        slots4 = [self.sp[0][:, 0:512], self.sp[0][:, 512:1024],
                  self.sp[1][:, 0:512], self.sp[1][:, 512:1024]]
        for w in range(NG):
            ks = w % 2
            self.wait("tensor", ev_wqkv)
            self._ds("fqkv")
            pend_tr = [None]

            def flush_tr():
                if pend_tr[0] is None:
                    return
                ttl_p, fg_p, pp_p, evb_p = pend_tr[0]
                pend_tr[0] = None
                TBk, kk = self.tb_get()
                self.wait("tensor", evb_p, self._tbevs[kk])
                ev_tr = None
                for c in range(4):
                    ev_tr = self.tr(TBk[:, 128 * c:128 * c + 128],
                                    qtb2[pp_p][:, 128 * c:128 * c + 128],
                                    self.identb[:], sig=(c == 3))
                self.wait("vector", ev_tr)
                if fg_p == 0:
                    d3 = q_fm.rearrange("p (c t) -> p c t", t=512)
                    self._tbevs[kk] = self.cp(
                        "vector", d3[:, :, 128 * ttl_p:128 * ttl_p + 128],
                        TBk[:, 0:512].rearrange("p (c t) -> p c t", t=128),
                        sig=True)
                else:
                    d3 = k_ring.rearrange("p (s c t) -> p s c t", s=2, t=512)
                    self._tbevs[kk] = self.cp(
                        "vector", d3[:, ksp[0], :, 128 * ttl_p:128 * ttl_p + 128],
                        TBk[:, 0:512].rearrange("p (c t) -> p c t", t=128),
                        sig=True)

            ksp = [ks]
            for ttl in range(4):
                gt = 4 * w + ttl
                self.wait("sync", self._gate_xt)
                evtabs = [self.dma("sync", cqb, xpcq_d[128 * gt:128 * gt + 128, :], "tq0"),
                          self.dma("sync", sqb, xpsq_d[128 * gt:128 * gt + 128, :], "tq1"),
                          self.dma("sync", ckb, xpck_d[128 * gt:128 * gt + 128, :], "tq2"),
                          self.dma("sync", skb, xpsk_d[128 * gt:128 * gt + 128, :], "tq3")]
                def qk_pre(fg, bank, sidx, ev_mm):
                    pp = fg % 2
                    sb_ = 24 + 16 * pp
                    qtm = qtm2[pp]
                    self.wait("vector", ev_mm)
                    evq = self.cp("vector", qtm, bank, sig=True)
                    bankev[sidx] = evq
                    self.tt("vector", qsq2, qtm, qtm, ALU.mult)
                    self.red("vector", st[:, sb_:sb_ + 8],
                             qsq2.rearrange("p (h d) -> p h d", d=64), ALU.add)
                    evmx = self.op("vector", lambda e, sb_=sb_: e.tensor_scalar_max(
                        st[:, sb_:sb_ + 8], st[:, sb_:sb_ + 8], 1e-24), sig=True)
                    self.swait("vector", evmx)
                    self.wait("scalar", evmx)
                    self.act(st[:, sb_ + 8:sb_ + 16], st[:, sb_:sb_ + 8],
                             AF.Ln, rt=True)
                    evrn = self.act(st[:, sb_:sb_ + 8], st[:, sb_ + 8:sb_ + 16],
                                    AF.Exp, scale=-0.5, sig=True)
                    return (fg, pp, sb_, evrn)

                def qk_post(ctx):
                    fg, pp, sb_, evrn = ctx
                    qtm, qrot = qtm2[pp], qrot2[pp]
                    tabs = (cqb, sqb) if fg == 0 else (ckb, skb)
                    smap = self.qsm if fg == 0 else self.ksm
                    ev_sm = ev_qs if fg == 0 else ev_ks
                    self.wait("vector", evrn, ev_sm, *evtabs)
                    q3 = qtm.rearrange("p (h z) -> p h z", z=64)
                    rnb = bass.AP(self.stat, sb_, [[96, 128], [1, 8], [0, 64]])
                    self.tt("vector", q3, q3, rnb, ALU.mult)
                    self.tt("vector", qtm, qtm, smap[:], ALU.mult)
                    r3 = qrot.rearrange("p (h z) -> p h z", z=64)
                    c3 = tabs[0].rearrange("p (h z) -> p h z", z=64)
                    s3 = tabs[1].rearrange("p (h z) -> p h z", z=64)
                    e_, o_ = q3[:, :, 0:32], q3[:, :, 32:64]
                    ce, se_ = c3[:, :, 0:32], s3[:, :, 0:32]
                    t13 = t1[:, 0:256].rearrange("p (h z) -> p h z", z=32)
                    t23 = t2[:, 0:256].rearrange("p (h z) -> p h z", z=32)
                    self.tt("vector", t13, e_, ce, ALU.mult)
                    self.tt("vector", t23, o_, se_, ALU.mult)
                    self.tt("vector", r3[:, :, 0:32], t13, t23, ALU.subtract)
                    self.tt("vector", t13, o_, ce, ALU.mult)
                    ev5 = self.tt("vector", t23, e_, se_, ALU.mult, sig=True)
                    self._gate_xt = ev5
                    self.tt("vector", r3[:, :, 32:64], t13, t23, ALU.add, rt=True)
                    evb = self.cp("vector", qtb2[pp], qrot, sig=True)
                    pend_tr[0] = (ttl, fg, pp, evb)

                pend_qk = []
                for fg in range(3):
                    sidx = (3 * ttl + fg) % 4
                    bank = slots4[sidx]
                    self.wait("tensor", bankev[sidx])
                    ev_mm = None
                    for dc in range(8):
                        off = 4104 * dc + (0 if dc >= 4 else 1) + 128 * gt
                        ev_mm = self.mm(bank, x_fm[:, off:off + 128],
                                        wqkv_sb[:, 1536 * dc + 512 * fg:1536 * dc + 512 * fg + 512],
                                        start=(dc == 0), stop=(dc == 7), sig=(dc == 7))
                    flush_tr()
                    if fg == 2:
                        self.wait("vector", ev_mm)
                        bankev[sidx] = self.cp(
                            "vector",
                            v_ring[:, 2048 * ks + 512 * ttl:2048 * ks + 512 * ttl + 512],
                            bank, sig=True)
                        continue
                    ctx = qk_pre(fg, bank, sidx, ev_mm)
                    if pend_qk:
                        qk_post(pend_qk.pop())
                    pend_qk.append(ctx)
                if pend_qk:
                    qk_post(pend_qk.pop())
            flush_tr()
            self._ds("fattn")
            self.wait("tensor", self._tbevs[0], self._tbevs[1])
            for hp in range(4):
                ev_pv_last = [None, None]

                def p1a(a):
                    apar = a % 2
                    sb_ = 24 + 8 * apar
                    qb = 512 * w + 128 * a
                    kw0 = qb - 512
                    c0 = max(0, -kw0)
                    kstart = max(0, kw0)
                    width = qb + 128 - kstart
                    nb = width // 128
                    segs = []
                    j = kstart
                    pc = 0
                    while j < qb + 128:
                        wslot = (j // 512) % 2
                        tloc = j % 512
                        n = min(512 - tloc, qb + 128 - j)
                        n = min(n, (512 if pc < 512 else 1024) - pc)
                        segs.append((pc, n, wslot, tloc))
                        j += n
                        pc += n
                    evnb2 = [None, None]
                    for hh in range(2):
                        bank = self.sp[hh]
                        self.wait("tensor", bankev[2 * hh], bankev[2 * hh + 1])
                        qrow = slice(0, 64) if hh == 0 else slice(64, 128)
                        tp = (0, 0) if hh == 0 else (64, 0)
                        ev_s = None
                        for si, (pc, n, wslot, tloc) in enumerate(segs):
                            ev_s = self.mm(
                                bank[:, pc:pc + n],
                                q_fm[qrow, 512 * hp + 128 * a:512 * hp + 128 * a + 128],
                                k_ring[qrow, 2048 * wslot + 512 * hp + tloc:
                                       2048 * wslot + 512 * hp + tloc + n],
                                tp=tp, sig=(si == len(segs) - 1))
                        self.wait("vector", ev_s)
                        evm = self.tt("vector", Sm[apar][hh][:, 0:width],
                                      bank[:, 0:width],
                                      self.mask[:, c0:c0 + width], ALU.add, sig=True)
                        bankev[2 * hh] = bankev[2 * hh + 1] = evm
                        self.red("vector", st[:, sb_ + hh:sb_ + 1 + hh],
                                 Sm[apar][hh][:, 0:width], ALU.max, sig=True)
                        evnb2[hh] = self.ts("vector", st[:, sb_ + 2 + hh:sb_ + 3 + hh],
                                            st[:, sb_ + hh:sb_ + 1 + hh], -QSC,
                                            sig=True)
                    return [a, apar, width, nb, kstart, evnb2, None, None]

                def p1b(ctx):
                    a, apar, width, nb, kstart, evnb2, _, _ = ctx
                    sb_ = 24 + 8 * apar
                    evex2 = [None, None]
                    for hh in range(2):
                        self.wait("scalar", evnb2[hh], pbf_war[apar][hh])
                        evex2[hh] = self.act(pbf[apar][hh][:, 0:width],
                                             Sm[apar][hh][:, 0:width],
                                             AF.Exp, bias=st[:, sb_ + 2 + hh:sb_ + 3 + hh],
                                             scale=QSC,
                                             accum=st[:, sb_ + 4 + hh:sb_ + 5 + hh],
                                             sig=True)
                    ctx[6] = evex2

                def p2a(ctx):
                    a, apar, width, nb, kstart, _, evex2, _ = ctx
                    sb_ = 24 + 8 * apar
                    evnm2 = [None, None]
                    for hh in range(2):
                        self.wait("vector", evex2[hh])
                        evrc = self.recip(st[:, sb_ + 6 + hh:sb_ + 7 + hh],
                                          st[:, sb_ + 4 + hh:sb_ + 5 + hh], sig=True)
                        self.wait("scalar", evrc)
                        evnm2[hh] = self.act(pbf[apar][hh][:, 0:width],
                                             pbf[apar][hh][:, 0:width],
                                             AF.Copy,
                                             scale=st[:, sb_ + 6 + hh:sb_ + 7 + hh],
                                             sig=True)
                    ctx[7] = evnm2

                def p2b(ctx):
                    a, apar, width, nb, kstart, _, _, evnm2 = ctx
                    for hh in range(2):
                        TBk, kk = self.tb_get()
                        self.wait("tensor", evnm2[hh], self._tbevs[kk])
                        ev_tr = None
                        for kb in range(nb):
                            ev_tr = self.tr(TBk[:, 128 * kb:128 * kb + 128],
                                            pbf[apar][hh][:, 128 * kb:128 * kb + 128],
                                            self.identb[:], sig=(kb == nb - 1))
                        pbf_war[apar][hh] = ev_tr
                        self.wait("scalar", ev_tr)
                        evpt = self.act(pT[hh][:, 0:width], TBk[:, 0:width],
                                        AF.Copy, sig=True)
                        self._tbevs[kk] = evpt
                        self.wait("tensor", evpt)
                        if a == 0:
                            self.wait("tensor", oev[hh])
                        obank = self.o0 if hh == 0 else self.o1
                        orow = slice(0, 64) if hh == 0 else slice(64, 128)
                        otp = (0, 0) if hh == 0 else (0, 64)
                        h = 2 * hp + hh
                        ev_pv = None
                        for kb in range(nb):
                            j = kstart + 128 * kb
                            wslot = (j // 512) % 2
                            tl = (j % 512) // 128
                            vcol = 2048 * wslot + 512 * tl + 64 * h
                            ev_pv = self.mm(obank[orow, 128 * a:128 * a + 128],
                                            v_ring[:, vcol:vcol + 64],
                                            pT[hh][:, 128 * kb:128 * kb + 128],
                                            start=(kb == 0), stop=(kb == nb - 1),
                                            tp=otp, sig=(kb == nb - 1))
                        if a == 3:
                            ev_pv_last[hh] = ev_pv

                prev = None
                for a in range(5):
                    if prev is not None:
                        p2a(prev)
                    cur = None
                    if a < 4:
                        cur = p1a(a)
                    if prev is not None:
                        p2b(prev)
                    if cur is not None:
                        p1b(cur)
                    prev = cur
                self.wait("vector", ev_pv_last[0], ev_pv_last[1])
                oev[0] = self.cp("vector", O_fmw[0:64, 512 * hp:512 * hp + 512],
                                 self.o0[0:64, :], sig=True)
                oev[1] = self.cp("vector", O_fmw[64:128, 512 * hp:512 * hp + 512],
                                 self.o1[64:128, :], sig=True)
            if w == 0 and "FDBG" in self.dbg and l == 0:
                fd = self.nc.dram_tensor("FDBG", [128, 16384], BF16,
                                         kind="ExternalOutput").ap()
                evm1 = self.milestone("vector")
                self.wait("sync", evm1)
                self.dma("sync", fd[:, 0:4104], x_fm[:, 0:4104], "fdbg0")
                self.dma("sync", fd[:, 4104:6152], q_fm, "fdbg1")
                self.dma("sync", fd[:, 6152:8200], k_ring[:, 0:2048], "fdbg2")
                self.dma("sync", fd[:, 8200:10248], v_ring[:, 0:2048], "fdbg3")
                self.dma("sync", fd[:, 10248:12296], O_fmw, "fdbg4")
                self.dma("sync", fd[:, 12296:12936], pT[0], "fdbg5")
                self.dma("sync", fd[:, 13000:13080],
                         self.stat[:, 0:40].bitcast(BF16), "fdbg6")
            self._ds("fwo")
            self.wait("tensor", ev_wo2)
            for ttl in range(4):
                tt = 4 * w + ttl
                par = ttl % 2
                self.wait("sync", ev_xstw[par])
                ev_x = self.dma("sync", xr[par], XIN[128 * tt:128 * tt + 128, :],
                                f"xr{par}")
                self.wait("tensor", bankev[2 * par], bankev[2 * par + 1],
                          oev[0], oev[1])
                ev = None
                for fh in range(2):
                    for oc in range(4):
                        ev = self.mm(self.sp[par][:, 512 * fh:512 * fh + 512],
                                     O_fmw[:, 512 * oc + 128 * ttl:512 * oc + 128 * ttl + 128],
                                     wo2_sb[:, 1024 * oc + 512 * fh:1024 * oc + 512 * fh + 512],
                                     start=(oc == 0), stop=(oc == 3),
                                     sig=(fh == 1 and oc == 3))
                self.wait("vector", ev, ev_x, ev_xstw[par])
                evu = self.tt("vector", xr[par], xr[par], self.sp[par][:, 0:1024],
                              ALU.add, sig=True)
                bankev[2 * par] = bankev[2 * par + 1] = evu
                self.wait("sync", evu)
                if w == 0 and ttl == 0 and l == 0 and "FDBG2" in self.dbg:
                    fd2 = self.nc.dram_tensor("FDBG2", [128, 1024], F32,
                                              kind="ExternalOutput").ap()
                    fd3 = self.nc.dram_tensor("FDBG3", [128, 6144], BF16,
                                              kind="ExternalOutput").ap()
                    self.dma("sync", fd2[:, :], xr[par], "fdbg7")
                    self.dma("sync", fd3[:, 0:4096], wo2_sb, "fdbg8")
                    self.dma("sync", fd3[:, 4096:6144], O_fmw, "fdbg9")
                ev_xstw[par] = self.dma("sync", self.XRES[128 * tt:128 * tt + 128, :],
                                        xr[par], f"xo{par}")
            self._ds(None)
        self.barrier()

        build_xfm(self.XRES)
        self.barrier()

        self._ds("fC1")
        for gq in range(NG):
            evh = None
            for dc in range(8):
                off = 4104 * dc + (0 if dc >= 4 else 1) + 512 * gq
                evh = self.tt("vector", hsq8[:, 512 * dc:512 * dc + 512],
                              x_fm[:, off:off + 512], x_fm[:, off:off + 512],
                              ALU.mult, sig=(dc == 7))
            self.fence("tensor", "ve")
            self.fence("tensor", "se")
            evsum = None
            for dc in range(8):
                off = 4104 * dc + (0 if dc >= 4 else 1) + 512 * gq
                evsum = self.mm(self.u0[0:1, :], self.ones_b[:, 0:1],
                                x_fm[:, off:off + 512],
                                start=(dc == 0), stop=(dc == 7), sig=(dc == 7))
            self.wait("tensor", evh)
            evsq = None
            for dc in range(8):
                evsq = self.mm(self.o0[0:1, :], self.ones_b[:, 0:1],
                               hsq8[:, 512 * dc:512 * dc + 512],
                               start=(dc == 0), stop=(dc == 7), sig=(dc == 7))
            mean_r = srow[0:1, 0:512]
            var_r = srow[0:1, 512:1024]
            self.wait("vector", evsum, evsq)
            self.ts("vector", mean_r, self.u0[0:1, :], 1.0 / D, rt=True)
            self.ts("vector", var_r, self.o0[0:1, :], 1.0 / D, rt=True)
            self.tt("vector", self.scr64[0:1, 0:512], mean_r, mean_r, ALU.mult, rt=True)
            evv = self.tt("vector", var_r, var_r, self.scr64[0:1, 0:512],
                          ALU.subtract, sig=True, rt=True)
            self.wait("scalar", evv)
            self.act(self.scr64[0:1, 0:512], var_r, AF.Ln, bias=self.epsb[0:1, 0:1], rt=True)
            evrs = self.act(var_r, self.scr64[0:1, 0:512], AF.Exp, scale=-0.5, sig=True)
            self.wait("vector", evrs)
            evom = self.tt("vector", mean_r, mean_r, var_r, ALU.mult, sig=True)
            self.wait("tensor", evom)
            evb1 = self.mm(self.o1[:, :], self.ones_f[0:1, 0:128], var_r, sig=True)
            evb2 = self.mm(self.u0[:, :], self.ones_f[0:1, 0:128], mean_r, sig=True)
            self.fence("vector", "ge")
            self.wait("vector", evb1, evb2)
            self.cp("vector", rstdbc, self.o1[:, :])
            evbc = self.cp("vector", omegabc, self.u0[:, :], sig=True)
            self.wait("vector", evbc)
            for dc in range(8):
                off = 4104 * dc + (0 if dc >= 4 else 1) + 512 * gq
                self.fence("vector", "ve")
                evp1 = self.tt("vector", t1c, x_fm[:, off:off + 512], rstdbc,
                               ALU.mult, sig=True)
                self.wait("vector", evp1)
                self.tt("vector", t2c, t1c, omegabc, ALU.subtract)
                self.ts("vector",
                        ln_fm[:, 4096 * dc + 512 * gq:4096 * dc + 512 * gq + 512],
                        t2c, self.lng[:, dc:dc + 1], self.lnb[:, dc:dc + 1],
                        ALU.mult, ALU.add)
        self._ds(None)
        self.barrier()

        self._ds("fC2")
        for th in range(4):
            tok0 = 1024 * th
            slots = [self.sp[0][:, 0:512], self.sp[0][:, 512:1024],
                     self.sp[1][:, 0:512], self.sp[1][:, 512:1024],
                     self.o0[:, :], self.o1[:, :]]
            ev_ev = [None] * 6
            ev_wb = [None, None]
            for hb in range(11):
                wpar = hb % 2
                self.fence("gpsimd", "pe")
                ev_wb[wpar] = self.dma(
                    "gpsimd", w1blk[wpar],
                    w1p[l].rearrange("(c p) f -> p c f", p=128)[:, :, 512 * hb:512 * hb + 512],
                    f"w1b{wpar}")
                self.wait("tensor", ev_wb[wpar])
                for hcl in range(4):
                    hc = 4 * hb + hcl
                    for g2 in range(2):
                        sidx = (8 * hb + hcl * 2 + g2) % 6
                        if ev_ev[sidx] is not None:
                            self.wait("tensor", ev_ev[sidx])
                        ev = None
                        for dc in range(8):
                            ev = self.mm(
                                slots[sidx],
                                w1blk[wpar][:, 512 * dc + 128 * hcl:512 * dc + 128 * hcl + 128],
                                ln_fm[:, 4096 * dc + tok0 + 512 * g2:
                                      4096 * dc + tok0 + 512 * g2 + 512],
                                start=(dc == 0), stop=(dc == 7), sig=(dc == 7))
                        self.wait("scalar", ev)
                        if hc < NHC:
                            acol = 1024 * hc + 512 * g2
                            ev_ev[sidx] = self.act(aa[:, acol:acol + 512], slots[sidx],
                                                   AF.Identity,
                                                   bias=self.b1c[:, hc:hc + 1], sig=True)
                        else:
                            acol = 1024 * (hc - NHC) + 512 * g2
                            self.fence("scalar", "ve")
                            evg = self.act(gelu_t[:], slots[sidx], AF.Gelu,
                                           bias=self.b1c[:, hc:hc + 1], sig=True)
                            self.wait("vector", evg)
                            ev_ev[sidx] = self.tt("vector", aa[:, acol:acol + 512],
                                                  aa[:, acol:acol + 512], gelu_t[:],
                                                  ALU.mult, sig=True)
            self.barrier()
            self.fence("gpsimd", "pe")
            ev_w2 = self.dma("gpsimd", w2sb,
                             w2p[l].rearrange("(c p) f -> p c f", p=128), "w2s")
            self.wait("tensor", ev_w2)
            ev_xst = [None, None]
            ev_ee = [None, None]
            for ttl in range(8):
                tt = 8 * th + ttl
                par = ttl % 2
                self.wait("sync", ev_xst[par])
                ev_x = self.dma("sync", xr2[par],
                                self.XRES[128 * tt:128 * tt + 128, :], f"xq{par}")
                if ev_ee[par] is not None:
                    self.wait("tensor", ev_ee[par])
                ev = None
                for fh in range(2):
                    for hc in range(NHC):
                        ev = self.mm(
                            self.sp[par][:, 512 * fh:512 * fh + 512],
                            aa[:, 1024 * hc + 128 * ttl:1024 * hc + 128 * ttl + 128],
                            w2sb[:, 1024 * hc + 512 * fh:1024 * hc + 512 * fh + 512],
                            start=(hc == 0), stop=(hc == NHC - 1),
                            sig=(fh == 1 and hc == NHC - 1))
                self.wait("vector", ev, ev_x, ev_xst[par])
                self.tt("vector", xr2[par], xr2[par], self.b2m[:], ALU.add)
                ev_ee[par] = self.tt("vector", xr2[par], xr2[par],
                                     self.sp[par][:, 0:1024], ALU.add, sig=True)
                if l == 1:
                    self.wait("gpsimd", ev_ee[par])
                    e1 = self.dacc(xr2[par], self.Xd[128 * tt:128 * tt + 128, :],
                                   f"qa{par}")
                    self.wait("gpsimd", e1)
                    e2 = self.dacc(xr2[par], self.AOUT[128 * tt:128 * tt + 128, :],
                                   f"qb{par}")
                    self.wait("sync", e2)
                    ev_xst[par] = self.dma("sync",
                                           self.OUT[128 * tt:128 * tt + 128, :],
                                           xr2[par], f"xqs{par}")
                else:
                    self.wait("sync", ev_ee[par])
                    ev_xst[par] = self.dma("sync",
                                           self.XRES[128 * tt:128 * tt + 128, :],
                                           xr2[par], f"xqs{par}")
            self.barrier()
        self._ds(None)

    def _p5(self):
        af32 = self.af32
        b1 = [af32(0, 1024), af32(2048, 1024)]
        b2 = [af32(4096, 1024), af32(6144, 1024)]
        b3 = [af32(8192, 1024), af32(10240, 1024)]
        ob = [af32(12288, 1024), af32(14336, 1024)]
        ev_st = [None, None]
        gate5 = [None, None]
        for tt in range(NT):
            par = tt % 2
            sl = slice(128 * tt, 128 * tt + 128)
            self.wait("sync", gate5[par])
            e1 = self.dma("sync", b1[par], self.Xd[sl, :], f"p5a{par}")
            e2 = self.dma("sync", b2[par], self.AOUT[sl, :], f"p5b{par}")
            e3 = self.dma("sync", b3[par], self.XRES[sl, :], f"p5c{par}")
            self.wait("vector", e1, e2, e3, ev_st[par])
            self.tt("vector", ob[par], b1[par], b2[par], ALU.add)
            ev = self.tt("vector", ob[par], ob[par], b3[par], ALU.add, sig=True)
            gate5[par] = ev
            self.wait("sync", ev)
            ev_st[par] = self.dma("sync", self.OUT[sl, :], ob[par], f"os{par}")


_BUILD_CACHE = {}


def _get_nc(dbg=(), phases=("p0", "p1", "p2", "f0", "f1")):
    key = (tuple(sorted(dbg)), tuple(phases))
    if key not in _BUILD_CACHE:
        _BUILD_CACHE[key] = MK(dbg).build(phases)
    return _BUILD_CACHE[key]


def kernel(**inputs):
    nc = _get_nc()
    shared = _prep_shared(inputs)
    x = np.asarray(inputs["L0_attn_x"], np.float32)
    in_maps = []
    for c in range(8):
        m = dict(shared)
        m["x"] = np.ascontiguousarray(x[c])
        in_maps.append(m)
    res = run_bass_kernel_spmd(nc, in_maps, core_ids=list(range(8)))
    out = np.stack([r["out"] for r in res.results], 0)
    return out.astype(np.float32)

